# version 07
# baseline (speedup 1.0000x reference)
import numpy as np
import ml_dtypes
from contextlib import ExitStack

import concourse.bass as bass
import concourse.mybir as mybir
from concourse.bass_utils import run_bass_kernel_spmd

F32 = mybir.dt.float32
BF16 = mybir.dt.bfloat16
AX = mybir.AxisListType
ALU = mybir.AluOpType
ACT = mybir.ActivationFunctionType

D = 1024
T = 2048
HC = D // 128
NH = 16
DH = 64
SD = 128
DFF = 4096
NE = 16
IE = 512
EPS = 1e-5
NEG = -1.0e30
NT = 4

ENGINES = ("sync", "gpsimd", "vector", "scalar", "tensor")

TAPS = set()


class Sched:
    def __init__(self, nc, ctx):
        self.nc = nc
        self.q = {e: [] for e in ENGINES}
        self.sem = {}
        self.cnt = {}
        for name in ("dW", "dX", "dO", "tS", "vS", "sS", "gS", "B"):
            self.sem[name] = ctx.enter_context(nc.semaphore(name))
            self.cnt[name] = 0
        self.bank_last = {}

    def emit(self, eng, fn, inc=None, n=1):
        if inc is not None:
            sem = self.sem[inc]
            self.cnt[inc] += n
            self.q[eng].append(lambda e, fn=fn, sem=sem, n=n: fn(e).then_inc(sem, n))
            return self.cnt[inc]
        self.q[eng].append(fn)
        return None

    def wait(self, eng, name, val=None):
        v = self.cnt[name] if val is None else val
        s = self.sem[name]
        self.q[eng].append(lambda e, s=s, v=v: e.wait_ge(s, v))

    def bank_acquire(self, eng, bank):
        last = self.bank_last.get(bank)
        if last is not None:
            sem_name, v = last
            self.wait(eng, sem_name, v)
            self.bank_last[bank] = None

    def bank_release(self, bank, sem_name, v):
        self.bank_last[bank] = (sem_name, v)

    def vsync(self):
        v = self.emit("vector", lambda e: e.tensor_copy(self.dumV[:], self.dumV[:]),
                      inc="vS")
        self.wait("vector", "vS", v)

    def barrier(self):
        semB = self.sem["B"]
        self.wait("sync", "dX")
        self.wait("sync", "dO")
        self.wait("gpsimd", "dW")
        self.q["vector"].append(
            lambda e: e.tensor_copy(self.dumV[:], self.dumV[:]).then_inc(semB, 1))
        self.q["scalar"].append(
            lambda e: e.copy(self.dumS[:], self.dumS[:]).then_inc(semB, 1))
        self.q["gpsimd"].append(
            lambda e: e.tensor_copy(self.dumG[:], self.dumG[:]).then_inc(semB, 1))
        self.q["tensor"].append(lambda e: e.nop().then_inc(semB, 1))
        self.q["sync"].append(lambda e: e.nop().then_inc(semB, 1))
        self.cnt["B"] += 5
        tgt = self.cnt["B"]
        for eng in ENGINES:
            self.wait(eng, "B", tgt)
        self.bank_last = {}


def make_masks():
    i = np.arange(128)[:, None]
    j3 = np.arange(384)[None, :]
    m_int = np.where((j3 >= i + 1) & (j3 <= i + 256), 0.0, NEG)
    j2 = np.arange(256)[None, :]
    m_q1 = np.where(j2 <= i + 128, 0.0, NEG)
    j1 = np.arange(128)[None, :]
    m_q0 = np.where(j1 <= i, 0.0, NEG)
    bf = lambda a: np.ascontiguousarray(a.astype(np.float32)).astype(ml_dtypes.bfloat16)
    return bf(m_int), bf(m_q1), bf(m_q0)


def build_nc():
    nc = bass.Bass("TRN2", target_bir_lowering=False)
    ctx = ExitStack()

    def dram_in(name, shape, dt=F32):
        return nc.dram_tensor(name, list(shape), dt, kind="ExternalInput").ap()

    X = dram_in("X", [D, T])
    Wqkv = dram_in("Wqkv", [D, 3 * D])
    WO = dram_in("WO", [D, D])
    Wug = dram_in("Wug", [D, SD + D])
    Bw = dram_in("Bw", [D, SD])
    Cw = dram_in("Cw", [SD, D])
    W1 = dram_in("W1", [D, DFF])
    W2 = dram_in("W2", [DFF, D])
    gateT = dram_in("gateT", [D, NE])
    WgT = dram_in("WgT", [NE, D, IE])
    WuT = dram_in("WuT", [NE, D, IE])
    WdT = dram_in("WdT", [NE, IE, D])
    WsgT = dram_in("WsgT", [D, 2 * IE])
    WsuT = dram_in("WsuT", [D, 2 * IE])
    WsdT = dram_in("WsdT", [2 * IE, D])
    g1c = dram_in("g1c", [128, HC])
    b1c = dram_in("b1c", [128, HC])
    g2c = dram_in("g2c", [128, HC])
    b2c = dram_in("b2c", [128, HC])
    bOc = dram_in("bOc", [128, HC])
    b2mc = dram_in("b2mc", [128, HC])
    bugc = dram_in("bugc", [128, HC + 1])
    b1mc = dram_in("b1mc", [128, DFF // 128])
    gbias = dram_in("gbias", [NE, 1])
    Avec_d = dram_in("Avec", [SD, 1])
    id16_d = dram_in("id16", [128, 128], BF16)
    id32_d = dram_in("id32", [128, 128])
    mint_d = dram_in("mint", [128, 384], BF16)
    mq1_d = dram_in("mq1", [128, 256], BF16)
    mq0_d = dram_in("mq0", [128, 128], BF16)

    O = nc.dram_tensor("O", [D, T], F32, kind="ExternalOutput").ap()

    xn16_d = nc.dram_tensor("xn16_d", [D, T], BF16, kind="Internal").ap()
    xn1_d = nc.dram_tensor("xn1_d", [D, T], BF16, kind="Internal").ap()
    gt_d = nc.dram_tensor("gt_d", [D, T], F32, kind="Internal").ap()
    x2_d = nc.dram_tensor("x2_d", [D, T], F32, kind="Internal").ap()
    s_d = nc.dram_tensor("s_d", [D, T], F32, kind="Internal").ap()
    cw_d = nc.dram_tensor("cw_d", [NE, T], F32, kind="Internal").ap()

    taps = {}
    for tname, shape, dt in (
        ("tap_xn16", [D, T], BF16), ("tap_xn1", [D, T], BF16),
        ("tap_gt", [D, T], F32), ("tap_drive", [SD, T], F32),
        ("tap_states", [SD, T], F32), ("tap_x2", [D, T], F32),
        ("tap_s", [D, T], F32), ("tap_cw", [NE, T], F32),
        ("tap_y", [D, T], F32), ("tap_attn", [D, T], F32),
        ("tap_yssm", [D, T], F32), ("tap_scfm", [NE, T], F32),
        ("tap_sctm", [128, (T // 128) * NE], F32),
        ("tap_sfctm", [128, (T // 128) * NE], F32),
        ("tap_gsc", [128, (T // 128) * 4], F32),
        ("tap_sel", [128, (T // 128) * NE], F32),
        ("tap_cwtm", [128, (T // 128) * NE], F32),
    ):
        if tname in TAPS:
            taps[tname] = nc.dram_tensor(tname, shape, dt, kind="ExternalOutput").ap()

    with ctx:
        e = ctx.enter_context
        sch = Sched(nc, ctx)

        id16 = e(nc.sbuf_tensor("id16s", [128, 128], BF16))
        id32 = e(nc.sbuf_tensor("id32s", [128, 128], F32))
        mint = e(nc.sbuf_tensor("mints", [128, 384], BF16))
        mq1 = e(nc.sbuf_tensor("mq1s", [128, 256], BF16))
        mq0 = e(nc.sbuf_tensor("mq0s", [128, 128], BF16))
        ones_c = e(nc.sbuf_tensor("ones_c", [128, 2], F32))
        ones_r = e(nc.sbuf_tensor("ones_r", [1, 128], F32))
        zrow16 = e(nc.sbuf_tensor("zrow16", [1, 512], BF16))
        vg1 = e(nc.sbuf_tensor("vg1", [128, HC], F32))
        vb1 = e(nc.sbuf_tensor("vb1", [128, HC], F32))
        vg2 = e(nc.sbuf_tensor("vg2", [128, HC], F32))
        vb2 = e(nc.sbuf_tensor("vb2", [128, HC], F32))
        vbO = e(nc.sbuf_tensor("vbO", [128, HC], F32))
        vb2m = e(nc.sbuf_tensor("vb2m", [128, HC], F32))
        vbug = e(nc.sbuf_tensor("vbug", [128, HC + 1], F32))
        vb1m = e(nc.sbuf_tensor("vb1m", [128, DFF // 128], F32))
        vgb = e(nc.sbuf_tensor("vgb", [NE, 1], F32))
        vA = e(nc.sbuf_tensor("vA", [SD, 1], F32))

        rows = {}
        for rn in ("rstd0", "c0", "mu2", "rstd2", "c2"):
            rows[rn] = e(nc.sbuf_tensor("row_" + rn, [1, T], F32))

        pb = [e(nc.psum_tensor(f"pb{i}", [128, 512], F32)) for i in range(8)]
        sch.dumV = e(nc.sbuf_tensor("dumV", [128, 64], F32))
        sch.dumS = e(nc.sbuf_tensor("dumS", [128, 64], F32))
        sch.dumG = e(nc.sbuf_tensor("dumG", [128, 64], F32))

        def ld(dst, src):
            return sch.emit("sync",
                            lambda s, d=dst, a=src: s.dma_start(out=d, in_=a),
                            inc="dX", n=16)

        def wld(dst_ap, src_ap):
            return sch.emit("gpsimd",
                            lambda g, d=dst_ap, s=src_ap: g.dma_start(out=d, in_=s),
                            inc="dW", n=16)

        def chunked(dst_sb, src_dram, cdim):
            return (dst_sb.ap().rearrange("p (c j) -> p c j", c=cdim),
                    src_dram.rearrange("(c p) j -> p c j", p=128))

        def ld_cols(dst_sb, src_dram, tksl, cdim=HC):
            return ld(dst_sb.ap().rearrange("p (c t) -> p c t", c=cdim),
                      src_dram[:, tksl].rearrange("(c p) t -> p c t", p=128))

        def st_cols(src_sb, dst_dram, tksl, cdim=HC):
            return sch.emit("sync", lambda s, a=src_sb, d=dst_dram, tk=tksl, c=cdim:
                            s.dma_start(
                                out=d[:, tk].rearrange("(c p) t -> p c t", p=128),
                                in_=a.ap().rearrange("p (c t) -> p c t", c=c)),
                            inc="dO", n=16)

        ld(id16[:], id16_d[:, :])
        ld(id32[:], id32_d[:, :])
        ld(mint[:], mint_d[:, :])
        ld(mq1[:], mq1_d[:, :])
        ld(mq0[:], mq0_d[:, :])
        for dst, src in ((vg1, g1c), (vb1, b1c), (vg2, g2c), (vb2, b2c),
                         (vbO, bOc), (vb2m, b2mc), (vbug, bugc), (vb1m, b1mc),
                         (vgb, gbias), (vA, Avec_d)):
            ld(dst[:], src[:, :])
        cst_loaded = sch.cnt["dX"]

        sch.wait("vector", "dX", cst_loaded)
        sch.emit("vector", lambda v: v.memset(ones_c[:], 1.0))
        sch.emit("vector", lambda v: v.memset(ones_r[:], 1.0))
        v_consts = sch.emit("vector", lambda v: v.memset(zrow16[:], 0.0), inc="vS")
        sch.wait("tensor", "vS", v_consts)

        def bcast2(row_a, row_b, dst_a, dst_b, tksl, banks=(6, 7), n=256):
            ba, bb = banks
            sch.bank_acquire("tensor", f"pb{ba}")
            sch.emit("tensor", lambda t, r=row_a, tk=tksl, n=n: t.matmul(
                pb[ba][:, 0:n], ones_r[0:1, :], r[0:1, tk], start=True, stop=True))
            sch.bank_acquire("tensor", f"pb{bb}")
            vt = sch.emit("tensor", lambda t, r=row_b, tk=tksl, n=n: t.matmul(
                pb[bb][:, 0:n], ones_r[0:1, :], r[0:1, tk], start=True, stop=True),
                inc="tS")
            sch.wait("vector", "tS", vt)
            sch.emit("vector", lambda v, d=dst_a, n=n: v.tensor_copy(
                d[:, 0:n], pb[ba][:, 0:n]))
            vv = sch.emit("vector", lambda v, d=dst_b, n=n: v.tensor_copy(
                d[:, 0:n], pb[bb][:, 0:n]), inc="vS")
            sch.bank_release(f"pb{ba}", "vS", vv)
            sch.bank_release(f"pb{bb}", "vS", vv)
            return vv

        spanA = ExitStack()
        Wug16 = spanA.enter_context(
            nc.sbuf_tensor("Wug16", [128, HC * (SD + D)], BF16, side="right"))
        Bw16 = spanA.enter_context(
            nc.sbuf_tensor("Bw16", [128, HC * SD], BF16, side="right"))
        wld(*chunked(Wug16, Wug, HC))
        w_p1a = wld(*chunked(Bw16, Bw, HC))

        spanB = ExitStack()
        eB = spanB.enter_context
        drive = eB(nc.sbuf_tensor("drive", [SD, T], F32))
        states = eB(nc.sbuf_tensor("states", [SD, T], F32))
        st16 = eB(nc.sbuf_tensor("st16", [SD, T], BF16))
        pw64 = eB(nc.sbuf_tensor("pw64", [SD, 64], F32))
        apows = eB(nc.sbuf_tensor("apows", [SD, 16], F32))

        with ExitStack() as p0:
            e0 = p0.enter_context
            xbuf = [e0(nc.sbuf_tensor(f"xbuf{i}", [128, T], F32)) for i in range(2)]
            xsq = e0(nc.sbuf_tensor("xsq", [128, T], F32))
            bA = e0(nc.sbuf_tensor("bA", [128, T], F32))
            bB = e0(nc.sbuf_tensor("bB", [128, T], F32))
            stg = [e0(nc.sbuf_tensor(f"stg{i}", [128, T], BF16)) for i in range(2)]
            prow = {}
            for rn in ("mu0", "mu1", "rstd1", "tmpA", "tmpB"):
                prow[rn] = e0(nc.sbuf_tensor("prow_" + rn, [1, T], F32))

            def rsqrt_row(dst_row, src_row):
                sch.emit("scalar", lambda s: s.activation(
                    prow["tmpB"][0:1, :], src_row, ACT.Ln))
                return sch.emit("scalar", lambda s: s.activation(
                    dst_row, prow["tmpB"][0:1, :], ACT.Exp, scale=-0.5), inc="sS")

            def stats_mms(src_fn, c):
                if c == 0:
                    for b_ in range(2 * NT):
                        sch.bank_acquire("tensor", f"pb{b_}")
                for nt in range(NT):
                    sch.emit("tensor", lambda t, nt=nt, c=c, src=src_fn: t.matmul(
                        pb[nt][0:1, 0:512], ones_c[:, 0:1], src(nt),
                        start=(c == 0), stop=(c == HC - 1)))

            def stats_mms_sq(c):
                vt = None
                for nt in range(NT):
                    vt = sch.emit("tensor", lambda t, nt=nt, c=c: t.matmul(
                        pb[4 + nt][0:1, 0:512], ones_c[:, 0:1],
                        xsq[:, nt * 512:(nt + 1) * 512],
                        start=(c == 0), stop=(c == HC - 1)),
                        inc="tS" if nt == NT - 1 else None)
                return vt

            def row_stats(mu_row, rstd_row):
                for nt in range(NT):
                    sch.emit("vector", lambda v, nt=nt, mu=mu_row: v.tensor_scalar(
                        mu[0:1, nt * 512:(nt + 1) * 512], pb[nt][0:1, 0:512],
                        1.0 / D, None, ALU.mult))
                    sch.emit("vector", lambda v, nt=nt: v.tensor_scalar(
                        prow["tmpA"][0:1, nt * 512:(nt + 1) * 512],
                        pb[4 + nt][0:1, 0:512], 1.0 / D, None, ALU.mult))
                vv = sch.emit("vector", lambda v, mu=mu_row: v.tensor_tensor(
                    prow["tmpB"][0:1, :], mu[0:1, :], mu[0:1, :], ALU.mult),
                    inc="vS")
                for b_ in range(2 * NT):
                    sch.bank_release(f"pb{b_}", "vS", vv)
                sch.emit("vector", lambda v: v.tensor_tensor(
                    prow["tmpA"][0:1, :], prow["tmpA"][0:1, :], prow["tmpB"][0:1, :],
                    ALU.subtract))
                vv = sch.emit("vector", lambda v: v.tensor_scalar(
                    prow["tmpA"][0:1, :], prow["tmpA"][0:1, :], EPS, None, ALU.add),
                    inc="vS")
                sch.wait("scalar", "vS", vv)
                return rsqrt_row(rstd_row, prow["tmpA"][0:1, :])

            ld(xbuf[0][:], X[0:128, :])
            ld(xbuf[1][:], X[128:256, :])
            vt = None
            for c in range(HC):
                x_c = xbuf[c % 2]
                sch.wait("scalar", "dX")
                if vt is not None:
                    sch.wait("scalar", "tS", vt)
                vs = sch.emit("scalar", lambda s, xc=x_c: s.activation(
                    xsq[:], xc[:], ACT.Square), inc="sS")
                sch.wait("tensor", "dX")
                stats_mms(lambda nt, xc=x_c: xc[:, nt * 512:(nt + 1) * 512], c)
                sch.wait("tensor", "sS", vs)
                vt = stats_mms_sq(c)
                if c + 2 < HC:
                    sch.wait("sync", "tS", vt)
                    sch.wait("sync", "sS", vs)
                    ld(xbuf[c % 2][:], X[(c + 2) * 128:(c + 3) * 128, :])
            sch.wait("vector", "tS")
            vs_r0 = row_stats(prow["mu0"][0:1, :], rows["rstd0"][0:1, :])
            sch.wait("vector", "sS", vs_r0)
            vv_c0 = sch.emit("vector", lambda v: v.scalar_tensor_tensor(
                rows["c0"][0:1, :], prow["mu0"][0:1, :], -1.0, rows["rstd0"][0:1, :],
                ALU.mult, ALU.mult), inc="vS")
            sch.wait("tensor", "vS", vv_c0)
            for nt in range(NT):
                bcast2(rows["rstd0"], rows["c0"],
                       bA.ap()[:, nt * 512:nt * 512 + 512],
                       bB.ap()[:, nt * 512:nt * 512 + 512],
                       slice(nt * 512, (nt + 1) * 512), banks=(nt % 2, 2 + nt % 2),
                       n=512)

            sch.wait("sync", "tS")
            sch.wait("sync", "sS")
            sch.wait("sync", "vS")
            ld(xbuf[0][:], X[0:128, :])
            ld(xbuf[1][:], X[128:256, :])
            vt = None
            for c in range(HC):
                x_c = xbuf[c % 2]
                sch.wait("vector", "dX")
                sch.emit("vector", lambda v, xc=x_c: v.tensor_tensor(
                    xc[:], xc[:], bA[:], ALU.mult))
                vv = sch.emit("vector", lambda v, xc=x_c: v.tensor_tensor(
                    xc[:], xc[:], bB[:], ALU.add), inc="vS")
                st = stg[c % 2]
                sch.wait("vector", "dO")
                vv2 = sch.emit("vector", lambda v, xc=x_c, st=st: v.tensor_copy(
                    st[:], xc[:]), inc="vS")
                sch.wait("sync", "vS", vv2)
                sch.emit("sync", lambda s, st=st, c=c: s.dma_start(
                    out=xn16_d[c * 128:(c + 1) * 128, :], in_=st[:]), inc="dO", n=16)
                sch.wait("scalar", "vS", vv)
                if vt is not None:
                    sch.wait("scalar", "tS", vt)
                vs = sch.emit("scalar", lambda s, xc=x_c: s.activation(
                    xsq[:], xc[:], ACT.Square), inc="sS")
                sch.wait("tensor", "vS", vv)
                stats_mms(lambda nt, xc=x_c: xc[:, nt * 512:(nt + 1) * 512], c)
                sch.wait("tensor", "sS", vs)
                vt = stats_mms_sq(c)
                if c + 2 < HC:
                    sch.wait("sync", "tS", vt)
                    sch.wait("sync", "vS", vv2)
                    ld(xbuf[c % 2][:], X[(c + 2) * 128:(c + 3) * 128, :])
            sch.wait("vector", "tS")
            vs_r1 = row_stats(prow["mu1"][0:1, :], prow["rstd1"][0:1, :])
            sch.wait("vector", "sS", vs_r1)
            sch.emit("vector", lambda v: v.tensor_tensor(
                prow["tmpA"][0:1, :], rows["rstd0"][0:1, :], prow["rstd1"][0:1, :],
                ALU.mult))
            sch.emit("vector", lambda v: v.tensor_tensor(
                prow["tmpB"][0:1, :], rows["c0"][0:1, :], prow["mu1"][0:1, :],
                ALU.subtract))
            vv_cc = sch.emit("vector", lambda v: v.tensor_tensor(
                prow["tmpB"][0:1, :], prow["tmpB"][0:1, :], prow["rstd1"][0:1, :],
                ALU.mult), inc="vS")
            sch.wait("tensor", "vS", vv_cc)
            for nt in range(NT):
                bcast2(prow["tmpA"], prow["tmpB"],
                       bA.ap()[:, nt * 512:nt * 512 + 512],
                       bB.ap()[:, nt * 512:nt * 512 + 512],
                       slice(nt * 512, (nt + 1) * 512), banks=(nt % 2, 2 + nt % 2),
                       n=512)

            sch.wait("sync", "tS")
            sch.wait("sync", "sS")
            sch.wait("sync", "vS")
            ld(xbuf[0][:], X[0:128, :])
            ld(xbuf[1][:], X[128:256, :])
            for c in range(HC):
                x_c = xbuf[c % 2]
                sch.wait("vector", "dX")
                sch.emit("vector", lambda v, xc=x_c: v.tensor_tensor(
                    xc[:], xc[:], bA[:], ALU.mult))
                vv = sch.emit("vector", lambda v, xc=x_c: v.tensor_tensor(
                    xc[:], xc[:], bB[:], ALU.add), inc="vS")
                st = stg[c % 2]
                sch.wait("scalar", "vS", vv)
                sch.wait("scalar", "dO")
                vs = sch.emit("scalar", lambda s, xc=x_c, st=st, c=c: s.activation(
                    st[:], xc[:], ACT.Identity,
                    bias=vb1[:, c:c + 1], scale=vg1[:, c:c + 1]), inc="sS")
                sch.wait("sync", "sS", vs)
                sch.emit("sync", lambda s, st=st, c=c: s.dma_start(
                    out=xn1_d[c * 128:(c + 1) * 128, :], in_=st[:]), inc="dO", n=16)
                if c + 2 < HC:
                    sch.wait("sync", "vS", vv)
                    sch.wait("sync", "sS", vs)
                    ld(xbuf[c % 2][:], X[(c + 2) * 128:(c + 3) * 128, :])

            if "tap_xn1" in taps:
                sch.wait("sync", "dO")
                sch.emit("sync", lambda s: s.dma_start(
                    out=taps["tap_xn1"][:, :], in_=xn1_d[:, :]), inc="dO", n=16)
            if "tap_xn16" in taps:
                sch.wait("sync", "dO")
                sch.emit("sync", lambda s: s.dma_start(
                    out=taps["tap_xn16"][:, :], in_=xn16_d[:, :]), inc="dO", n=16)

            sch.barrier()

        with ExitStack() as p1a:
            e1 = p1a.enter_context
            gtst = [e1(nc.sbuf_tensor(f"gtst{i}", [128, 512], F32)) for i in range(2)]
            xn1t = [e1(nc.sbuf_tensor(f"xn1tA{i}", [128, HC * 512], BF16))
                    for i in range(2)]

            sch.wait("tensor", "dW", w_p1a)
            ld_nt = {}
            ld_nt[0] = ld_cols(xn1t[0], xn1_d, slice(0, 512))
            ld_nt[1] = ld_cols(xn1t[1], xn1_d, slice(512, 1024))
            for nt in range(NT):
                t0 = nt * 512
                xb = xn1t[nt % 2]
                sch.wait("tensor", "dX", ld_nt[nt])
                sch.bank_acquire("tensor", "pb0")
                for h in range(HC):
                    sch.emit("tensor", lambda t, h=h, xb=xb: t.matmul(
                        pb[0][:, 0:512], Bw16.ap()[:, h * SD:h * SD + 128],
                        xb.ap()[:, h * 512:(h + 1) * 512],
                        start=(h == 0), stop=False))
                for h in range(HC):
                    vt = sch.emit("tensor", lambda t, h=h, xb=xb: t.matmul(
                        pb[0][:, 0:512],
                        Wug16.ap()[:, h * (SD + D):h * (SD + D) + 128],
                        xb.ap()[:, h * 512:(h + 1) * 512],
                        start=False, stop=(h == HC - 1)),
                        inc="tS" if h == HC - 1 else None)
                sch.wait("scalar", "tS", vt)
                vs = sch.emit("scalar", lambda s, t0=t0: s.activation(
                    drive.ap()[:, t0:t0 + 512], pb[0][:, 0:512], ACT.Identity,
                    bias=vbug[:, 0:1]), inc="sS")
                sch.bank_release("pb0", "sS", vs)
                for dc in range(1, HC + 1):
                    bank = 1 + (dc % 2)
                    sch.bank_acquire("tensor", f"pb{bank}")
                    for h in range(HC):
                        vt = sch.emit("tensor", lambda t, h=h, dc=dc, b=bank, xb=xb:
                                      t.matmul(
                                          pb[b][:, 0:512],
                                          Wug16.ap()[:, h * (SD + D) + dc * 128:
                                                     h * (SD + D) + dc * 128 + 128],
                                          xb.ap()[:, h * 512:(h + 1) * 512],
                                          start=(h == 0), stop=(h == HC - 1)),
                                      inc="tS" if h == HC - 1 else None)
                    stb = gtst[dc % 2]
                    sch.wait("scalar", "tS", vt)
                    sch.wait("scalar", "dO")
                    vs = sch.emit("scalar", lambda s, b=bank, stb=stb, dc=dc:
                                  s.activation(stb[:], pb[b][:, 0:512], ACT.Sigmoid,
                                               bias=vbug[:, dc:dc + 1]), inc="sS")
                    sch.bank_release(f"pb{bank}", "sS", vs)
                    sch.wait("sync", "sS", vs)
                    sch.emit("sync", lambda s, dc=dc, t0=t0, stb=stb: s.dma_start(
                        out=gt_d[(dc - 1) * 128:dc * 128, t0:t0 + 512], in_=stb[:]),
                        inc="dO", n=16)
                if nt + 2 < NT:
                    sch.wait("sync", "tS")
                    ld_nt[nt + 2] = ld_cols(xn1t[nt % 2], xn1_d,
                                            slice((nt + 2) * 512, (nt + 3) * 512))

            sch.emit("vector", lambda v: v.memset(pw64[:], 1.0))
            sch.vsync()
            sch.emit("vector", lambda v: v.tensor_scalar(
                pw64[:], pw64[:], vA[:, 0:1], None, ALU.mult))
            sch.vsync()
            for l in range(11):
                sch.emit("vector", lambda v, l=l: v.tensor_copy(
                    apows[:, l:l + 1], pw64[:, 0:1]))
                sch.vsync()
                sch.emit("vector", lambda v: v.tensor_tensor(
                    pw64[:], pw64[:], pw64[:], ALU.mult))
                sch.vsync()
            sch.wait("vector", "sS")
            bufs = [drive, states]
            for l in range(11):
                dd = 1 << l
                srcb = bufs[l % 2]
                dstb = bufs[1 - l % 2]
                sch.emit("vector", lambda v, dd=dd, s=srcb, d=dstb: v.tensor_copy(
                    d.ap()[:, 0:dd], s.ap()[:, 0:dd]))
                sch.emit("vector", lambda v, dd=dd, s=srcb, d=dstb, l=l:
                         v.scalar_tensor_tensor(
                             d.ap()[:, dd:T], s.ap()[:, 0:T - dd],
                             apows[:, l:l + 1], s.ap()[:, dd:T],
                             ALU.mult, ALU.add))
                sch.vsync()
            sch.emit("vector", lambda v: v.tensor_copy(st16[:], states[:]))

            if "tap_drive" in taps:
                sch.wait("sync", "vS")
                sch.emit("sync", lambda s: s.dma_start(
                    out=taps["tap_drive"][:, :], in_=drive[:]), inc="dO", n=16)
            if "tap_states" in taps:
                sch.wait("sync", "vS")
                sch.emit("sync", lambda s: s.dma_start(
                    out=taps["tap_states"][:, :], in_=states[:]), inc="dO", n=16)
            if "tap_gt" in taps:
                sch.wait("sync", "dO")
                sch.emit("sync", lambda s: s.dma_start(
                    out=taps["tap_gt"][:, :], in_=gt_d[:, :]), inc="dO", n=16)

            sch.barrier()

        spanA.close()

        with ExitStack() as p1b:
            e1 = p1b.enter_context
            Wqkv16 = e1(nc.sbuf_tensor("Wqkv16", [128, HC * 3 * D], BF16))
            WO16 = e1(nc.sbuf_tensor("WO16", [128, HC * D], BF16))
            Cw16 = e1(nc.sbuf_tensor("Cw16", [SD, D], BF16))
            xn1t = [e1(nc.sbuf_tensor(f"xn1tB{i}", [128, HC * 256], BF16))
                    for i in range(2)]
            xt = e1(nc.sbuf_tensor("xt", [128, HC * 256], F32))
            gtt = e1(nc.sbuf_tensor("gtt", [128, HC * 256], F32))
            q16 = e1(nc.sbuf_tensor("q16", [128, HC * 256], BF16))
            kring = [e1(nc.sbuf_tensor(f"kring{i}", [128, HC * 256], BF16))
                     for i in range(2)]
            vring = [[e1(nc.sbuf_tensor(f"vring{i}{j}", [128, D], BF16))
                      for j in range(2)] for i in range(2)]
            o16 = e1(nc.sbuf_tensor("o16", [128, HC * 256], BF16))
            E16 = e1(nc.sbuf_tensor("E16", [128, 5 * 384], BF16))
            P16 = e1(nc.sbuf_tensor("P16", [128, 5 * 384], BF16))
            PT16 = e1(nc.sbuf_tensor("PT16", [128, 5 * 384], BF16))
            nmax = e1(nc.sbuf_tensor("nmax", [128, 5], F32))
            denom = e1(nc.sbuf_tensor("denom", [128, 10], F32))
            a_c = e1(nc.sbuf_tensor("a_c", [128, 256], F32))
            y_c = e1(nc.sbuf_tensor("y_c", [128, 256], F32))
            x2t = e1(nc.sbuf_tensor("x2t", [128, HC * 256], F32))
            xsqt = e1(nc.sbuf_tensor("xsqt", [128, 256], F32))
            br0t = e1(nc.sbuf_tensor("br0t", [128, 256], F32))
            bc0t = e1(nc.sbuf_tensor("bc0t", [128, 256], F32))
            tmpArow = xt

            wld(*chunked(Wqkv16, Wqkv, HC))
            wld(*chunked(WO16, WO, HC))
            w_p1b = wld(Cw16.ap(), Cw[:, :])
            sch.wait("tensor", "dW", w_p1b)

            n_tiles = T // 256
            ld_t = {0: ld_cols(xn1t[0], xn1_d, slice(0, 256))}
            for i in range(n_tiles):
                tok0 = i * 256
                tksl = slice(tok0, tok0 + 256)
                xb = xn1t[i % 2]
                sch.wait("sync", "vS")
                sch.wait("sync", "gS")
                ld_cols(xt, X, tksl)
                vXg = ld_cols(gtt, gt_d, tksl)
                if i + 1 < n_tiles:
                    ld_t[i + 1] = ld_cols(xn1t[(i + 1) % 2], xn1_d,
                                          slice(tok0 + 256, tok0 + 512))

                bcast2(rows["rstd0"], rows["c0"], br0t, bc0t, tksl)

                sch.wait("vector", "dX", vXg)
                sch.wait("vector", "dO")
                sch.emit("vector", lambda v: v.tensor_tensor(
                    xt.ap().rearrange("p (c t) -> p c t", c=HC),
                    xt.ap().rearrange("p (c t) -> p c t", c=HC),
                    br0t.ap().unsqueeze(1).broadcast_to([128, HC, 256]), ALU.mult))
                vv_xn = sch.emit("vector", lambda v: v.tensor_tensor(
                    xt.ap().rearrange("p (c t) -> p c t", c=HC),
                    xt.ap().rearrange("p (c t) -> p c t", c=HC),
                    bc0t.ap().unsqueeze(1).broadcast_to([128, HC, 256]), ALU.add),
                    inc="vS")

                sch.wait("tensor", "dX", ld_t[i])
                ksl = kring[i % 2]
                for d in range(16):
                    bank = d % 2
                    sch.bank_acquire("tensor", f"pb{bank}")
                    for h in range(HC):
                        vt = sch.emit("tensor", lambda t, d=d, h=h, b=bank, xb=xb:
                                      t.matmul(
                                          pb[b][:, 0:256],
                                          Wqkv16.ap()[:, h * 3 * D + d * 128:
                                                      h * 3 * D + d * 128 + 128],
                                          xb.ap()[:, h * 256:(h + 1) * 256],
                                          start=(h == 0), stop=(h == HC - 1)),
                                      inc="tS" if h == HC - 1 else None)
                    dst = q16 if d < HC else ksl
                    dc = d % HC
                    sch.wait("vector", "tS", vt)
                    if d < HC:
                        vv = sch.emit("vector", lambda v, dc=dc, b=bank:
                                      v.tensor_scalar(
                                          q16.ap()[:, dc * 256:(dc + 1) * 256],
                                          pb[b][:, 0:256], 0.125, None, ALU.mult),
                                      inc="vS")
                    else:
                        vv = sch.emit("vector", lambda v, dst=dst, dc=dc, b=bank:
                                      v.tensor_copy(
                                          dst.ap()[:, dc * 256:(dc + 1) * 256],
                                          pb[b][:, 0:256]), inc="vS")
                    sch.bank_release(f"pb{bank}", "vS", vv)
                for ts_ in range(2):
                    for half in range(2):
                        bank = 2 + half
                        sch.bank_acquire("tensor", f"pb{bank}")
                        for h in range(HC):
                            vt = sch.emit("tensor",
                                          lambda t, ts_=ts_, half=half, h=h, b=bank,
                                          xb=xb: t.matmul(
                                              pb[b][:, 0:512],
                                              xb.ap()[:, h * 256 + ts_ * 128:
                                                      h * 256 + ts_ * 128 + 128],
                                              Wqkv16.ap()[:, h * 3 * D + 2 * D +
                                                          half * 512:
                                                          h * 3 * D + 2 * D +
                                                          half * 512 + 512],
                                              start=(h == 0), stop=(h == HC - 1)),
                                          inc="tS" if h == HC - 1 else None)
                        vtm = vring[i % 2][ts_]
                        sch.wait("vector", "tS", vt)
                        vv = sch.emit("vector", lambda v, vtm=vtm, half=half, b=bank:
                                      v.tensor_copy(
                                          vtm.ap()[:, half * 512:(half + 1) * 512],
                                          pb[b][:, 0:512]), inc="vS")
                        sch.bank_release(f"pb{bank}", "vS", vv)

                NK_IT = NH * 2
                DEPTH = 4
                NSL = 5

                rec_done = {}
                max_done = {}
                exp_done = {}

                def it_meta(k):
                    h = k // 2
                    qc = k % 2
                    p0_ = 64 * (h % 2)
                    hd = h // 2
                    kstart = tok0 + qc * 128 - 256
                    kchunks = [kstart + j * 128 for j in range(3)
                               if kstart + j * 128 >= 0]
                    w = 128 * len(kchunks)
                    mask = {384: mint, 256: mq1, 128: mq0}[w]
                    return h, qc, p0_, hd, kchunks, w, mask

                def attn_front(k):
                    h, qc, p0_, hd, kchunks, w, mask = it_meta(k)
                    s = k % NSL
                    sbk = (0, 1, 6)[k % 3]
                    sch.bank_acquire("tensor", f"pb{sbk}")
                    for ji, abs0 in enumerate(kchunks):
                        slot = (abs0 // 256) % 2
                        off = abs0 % 256
                        sch.emit("tensor", lambda t, ji=ji, slot=slot, off=off,
                                 sbk=sbk, hd=hd, p0_=p0_, qc=qc: t.matmul(
                                     pb[sbk][:, ji * 128:(ji + 1) * 128],
                                     q16.ap()[p0_:p0_ + 64,
                                              hd * 256 + qc * 128:
                                              hd * 256 + qc * 128 + 128],
                                     kring[slot].ap()[p0_:p0_ + 64,
                                                      hd * 256 + off:
                                                      hd * 256 + off + 128],
                                     start=(ji == 0), stop=False))
                    vt = sch.emit("tensor", lambda t, sbk=sbk, w=w, mask=mask:
                                  t.matmul(pb[sbk][:, 0:w], id16[:],
                                           mask.ap()[:, 0:w],
                                           start=False, stop=True), inc="tS")
                    sch.wait("vector", "tS", vt)
                    max_done[k] = sch.emit("vector", lambda v, sbk=sbk, w=w, s=s:
                                           v.reduce_max(nmax[:, s:s + 1],
                                                        pb[sbk][:, 0:w],
                                                        axis=AX.X, negate=True),
                                           inc="vS")

                def attn_exp(k):
                    h, qc, p0_, hd, kchunks, w, mask = it_meta(k)
                    s = k % NSL
                    sbk = (0, 1, 6)[k % 3]
                    sch.wait("scalar", "vS", max_done[k])
                    vs = sch.emit("scalar", lambda sc, sbk=sbk, w=w, s=s:
                                  sc.activation(E16.ap()[:, s * 384:s * 384 + w],
                                                pb[sbk][:, 0:w], ACT.Exp,
                                                bias=nmax[:, s:s + 1],
                                                accum_out=denom[:, 2 * s:2 * s + 1]),
                                  inc="sS")
                    sch.bank_release(f"pb{sbk}", "sS", vs)
                    exp_done[k] = vs

                def attn_rec(k):
                    s = k % NSL
                    sch.wait("vector", "sS", exp_done[k])
                    rec_done[k] = sch.emit("vector", lambda v, s=s: v.reciprocal(
                        denom[:, 2 * s + 1:2 * s + 2], denom[:, 2 * s:2 * s + 1]),
                        inc="vS")

                def attn_back(k):
                    h, qc, p0_, hd, kchunks, w, mask = it_meta(k)
                    s = k % NSL
                    nk = len(kchunks)
                    sch.wait("scalar", "vS", rec_done[k])
                    vv = sch.emit("scalar", lambda sc, w=w, s=s: sc.mul(
                        P16.ap()[:, s * 384:s * 384 + w],
                        E16.ap()[:, s * 384:s * 384 + w],
                        denom[:, 2 * s + 1:2 * s + 2]), inc="sS")
                    ptb = 2 + (k % 2)
                    sch.bank_acquire("tensor", f"pb{ptb}")
                    sch.wait("tensor", "sS", vv)
                    for j in range(nk):
                        vt = sch.emit("tensor", lambda t, j=j, ptb=ptb, s=s:
                                      t.matmul(pb[ptb][:, j * 128:(j + 1) * 128],
                                               P16.ap()[:, s * 384 + j * 128:
                                                        s * 384 + (j + 1) * 128],
                                               id16[:], start=True, stop=True),
                                      inc="tS" if j == nk - 1 else None)
                    sch.wait("scalar", "tS", vt)
                    vv = sch.emit("scalar", lambda sc, ptb=ptb, w=w, s=s:
                                  sc.copy(PT16.ap()[:, s * 384:s * 384 + w],
                                          pb[ptb][:, 0:w]), inc="sS")
                    sch.bank_release(f"pb{ptb}", "sS", vv)
                    pvb = 4 + (hd % 2)
                    if k % 4 == 0:
                        sch.bank_acquire("tensor", f"pb{pvb}")
                    sch.wait("tensor", "sS", vv)
                    for j, abs0 in enumerate(kchunks):
                        slot = (abs0 // 256) % 2
                        ts_ = (abs0 % 256) // 128
                        last_of_pair = (j == nk - 1 and k % 4 == 3)
                        vt = sch.emit("tensor", lambda t, j=j, slot=slot, ts_=ts_,
                                      h=h, qc=qc, pvb=pvb, p0_=p0_, nk=nk, s=s:
                                      t.matmul(
                                          pb[pvb][p0_:p0_ + 64,
                                                  qc * 128:qc * 128 + 128],
                                          vring[slot][ts_].ap()[:, 64 * h:
                                                                64 * h + 64],
                                          PT16.ap()[:, s * 384 + j * 128:
                                                    s * 384 + (j + 1) * 128],
                                          start=(j == 0), stop=(j == nk - 1),
                                          tile_position=(0, p0_)),
                                      inc="tS" if last_of_pair else None)
                    if k % 4 == 3:
                        sch.wait("vector", "tS", vt)
                        vv = sch.emit("vector", lambda v, hd=hd, pvb=pvb:
                                      v.tensor_copy(
                                          o16.ap()[:, hd * 256:(hd + 1) * 256],
                                          pb[pvb][:, 0:256]), inc="vS")
                        sch.bank_release(f"pb{pvb}", "vS", vv)

                for step in range(NK_IT + DEPTH):
                    if step >= DEPTH:
                        attn_back(step - DEPTH)
                    if 1 <= step < NK_IT + 1:
                        attn_exp(step - 1)
                    if 2 <= step < NK_IT + 2:
                        attn_rec(step - 2)
                    if step < NK_IT:
                        attn_front(step)
                vv_o16 = sch.cnt["vS"]

                sch.wait("tensor", "vS", vv_o16)
                sch.wait("gpsimd", "dX", vXg)
                sch.wait("gpsimd", "vS", vv_xn)
                vv_prev_comb = None
                for d in range(HC):
                    bko = 6 + (d % 2)
                    sch.bank_acquire("tensor", f"pb{bko}")
                    for h in range(HC):
                        sch.emit("tensor", lambda t, d=d, h=h, b=bko: t.matmul(
                            pb[b][:, 0:256],
                            WO16.ap()[:, h * D + d * 128:h * D + d * 128 + 128],
                            o16.ap()[:, h * 256:(h + 1) * 256],
                            start=(h == 0), stop=(h == HC - 1)))
                    bky = 4 + (d % 2)
                    sch.bank_acquire("tensor", f"pb{bky}")
                    vt = sch.emit("tensor", lambda t, d=d, b=bky, tok0=tok0:
                                  t.matmul(pb[b][:, 0:256],
                                           Cw16.ap()[:, d * 128:(d + 1) * 128],
                                           st16.ap()[:, tok0:tok0 + 256],
                                           start=True, stop=True), inc="tS")
                    sch.wait("scalar", "tS", vt)
                    sch.wait("scalar", "gS")
                    if vv_prev_comb is not None:
                        sch.wait("scalar", "vS", vv_prev_comb)
                    vs = sch.emit("scalar", lambda s, d=d, b=bko: s.activation(
                        a_c[:], pb[b][:, 0:256], ACT.Identity, bias=vbO[:, d:d + 1]),
                        inc="sS")
                    sch.bank_release(f"pb{bko}", "sS", vs)
                    vs = sch.emit("scalar", lambda s, b=bky: s.copy(
                        y_c[:], pb[b][:, 0:256]), inc="sS")
                    sch.bank_release(f"pb{bky}", "sS", vs)
                    if "tap_attn" in taps:
                        sch.wait("sync", "sS", vs)
                        sch.emit("sync", lambda s, d=d, tksl=tksl: s.dma_start(
                            out=taps["tap_attn"][d * 128:(d + 1) * 128, tksl],
                            in_=a_c[:]), inc="dO", n=16)
                        sch.emit("sync", lambda s, d=d, tksl=tksl: s.dma_start(
                            out=taps["tap_yssm"][d * 128:(d + 1) * 128, tksl],
                            in_=y_c[:]), inc="dO", n=16)
                        sch.wait("gpsimd", "dO")
                    sch.wait("gpsimd", "sS", vs)
                    sch.emit("gpsimd", lambda g: g.tensor_tensor(
                        a_c[:], a_c[:], y_c[:], ALU.subtract))
                    vg = sch.emit("gpsimd", lambda g, d=d: g.tensor_tensor(
                        y_c[:], xt.ap()[:, d * 256:(d + 1) * 256], y_c[:], ALU.add),
                        inc="gS")
                    sch.wait("vector", "gS", vg)
                    sch.emit("vector", lambda v, d=d: v.tensor_tensor(
                        a_c[:], gtt.ap()[:, d * 256:(d + 1) * 256], a_c[:],
                        ALU.mult))
                    vv = sch.emit("vector", lambda v, d=d: v.tensor_tensor(
                        x2t.ap()[:, d * 256:(d + 1) * 256], y_c[:], a_c[:],
                        ALU.add), inc="vS")
                    vv_prev_comb = vv
                vv_x2 = sch.cnt["vS"]

                sch.wait("sync", "vS", vv_x2)
                st_cols(x2t, x2_d, tksl)
                sch.bank_acquire("tensor", "pb6")
                sch.bank_acquire("tensor", "pb7")
                vt = None
                for c in range(HC):
                    sch.wait("scalar", "vS", vv_x2)
                    if vt is not None:
                        sch.wait("scalar", "tS", vt)
                    vs = sch.emit("scalar", lambda s, c=c: s.activation(
                        xsqt[:], x2t.ap()[:, c * 256:(c + 1) * 256], ACT.Square),
                        inc="sS")
                    sch.wait("tensor", "vS", vv_x2)
                    sch.emit("tensor", lambda t, c=c: t.matmul(
                        pb[6][0:1, 0:256], ones_c[:, 0:1],
                        x2t.ap()[:, c * 256:(c + 1) * 256],
                        start=(c == 0), stop=(c == HC - 1)))
                    sch.wait("tensor", "sS", vs)
                    vt = sch.emit("tensor", lambda t, c=c: t.matmul(
                        pb[7][0:1, 0:256], ones_c[:, 0:1], xsqt[:],
                        start=(c == 0), stop=(c == HC - 1)), inc="tS")
                sch.wait("vector", "tS", vt)
                sch.emit("vector", lambda v, i=i: v.tensor_scalar(
                    rows["mu2"][0:1, i * 256:(i + 1) * 256], pb[6][0:1, 0:256],
                    1.0 / D, None, ALU.mult))
                vv = sch.emit("vector", lambda v, i=i: v.tensor_scalar(
                    rows["c2"][0:1, i * 256:(i + 1) * 256], pb[7][0:1, 0:256],
                    1.0 / D, None, ALU.mult), inc="vS")
                sch.bank_release("pb6", "vS", vv)
                sch.bank_release("pb7", "vS", vv)

            sch.emit("vector", lambda v: v.tensor_tensor(
                tmpArow[0:1, :], rows["mu2"][0:1, :], rows["mu2"][0:1, :], ALU.mult))
            sch.emit("vector", lambda v: v.tensor_tensor(
                tmpArow[0:1, :], rows["c2"][0:1, :], tmpArow[0:1, :], ALU.subtract))
            vv = sch.emit("vector", lambda v: v.tensor_scalar(
                tmpArow[0:1, :], tmpArow[0:1, :], EPS, None, ALU.add), inc="vS")
            sch.wait("scalar", "vS", vv)
            sch.emit("scalar", lambda s: s.activation(
                rows["rstd2"][0:1, :], tmpArow[0:1, :], ACT.Ln))
            vs = sch.emit("scalar", lambda s: s.activation(
                tmpArow[0:1, :], rows["rstd2"][0:1, :], ACT.Exp, scale=-0.5),
                inc="sS")
            sch.wait("vector", "sS", vs)
            sch.emit("vector", lambda v: v.tensor_copy(
                rows["rstd2"][0:1, :], tmpArow[0:1, :]))
            sch.emit("vector", lambda v: v.scalar_tensor_tensor(
                rows["c2"][0:1, :], rows["mu2"][0:1, :], -1.0, rows["rstd2"][0:1, :],
                ALU.mult, ALU.mult))

            if "tap_x2" in taps:
                sch.wait("sync", "dO")
                sch.emit("sync", lambda s: s.dma_start(
                    out=taps["tap_x2"][:, :], in_=x2_d[:, :]), inc="dO", n=16)

            sch.barrier()

        spanB.close()

        with ExitStack() as p2:
            e2 = p2.enter_context
            W116 = e2(nc.sbuf_tensor("W116", [128, HC * (DFF // 2)], BF16))
            W216 = e2(nc.sbuf_tensor("W216", [128, (DFF // 256) * D], BF16))
            x2b = [e2(nc.sbuf_tensor(f"x2b{j}", [128, HC * 256], F32))
                   for j in range(3)]
            tmp2 = e2(nc.sbuf_tensor("tmp2", [128, HC * 256], F32))
            h16 = [e2(nc.sbuf_tensor(f"h16{j}", [128, HC * 256], BF16))
                   for j in range(2)]
            hid16 = [e2(nc.sbuf_tensor(f"hid16{j}", [128, (DFF // 256) * 256], BF16))
                     for j in range(2)]
            s_t = e2(nc.sbuf_tensor("s_t", [128, HC * 256], F32))
            sprev = [e2(nc.sbuf_tensor(f"sprev{j}", [128, HC * 256], F32))
                     for j in range(3)]
            br2 = [e2(nc.sbuf_tensor(f"br2{j}", [128, 256], F32)) for j in range(2)]
            bc2 = [e2(nc.sbuf_tensor(f"bc2{j}", [128, 256], F32)) for j in range(2)]

            DH2 = DFF // 2
            NDC = DH2 // 128
            NTt = T // 256
            for half in range(2):
                sch.wait("gpsimd", "tS")
                wld(W116.ap().rearrange("p (c j) -> p c j", c=HC),
                    W1[:, half * DH2:(half + 1) * DH2].rearrange(
                        "(c p) j -> p c j", p=128))
                w_p2 = wld(W216.ap().rearrange("p (c j) -> p c j", c=NDC),
                           W2[half * DH2:(half + 1) * DH2, :].rearrange(
                               "(c p) j -> p c j", p=128))

                end_V = {}
                end_H = {}
                end_W1 = {}
                end_G = {}
                end_S = {}
                end_ST = {}

                def LD(i):
                    j = i % 2
                    j3 = i % 3
                    tksl = slice(i * 256, (i + 1) * 256)
                    if i - 3 in end_S:
                        sch.wait("sync", "vS", end_S[i - 3])
                    if i - 2 in end_H:
                        sch.wait("sync", "sS", end_H[i - 2])
                    ld_cols(x2b[j3], x2_d, tksl)
                    vX = sch.cnt["dX"]
                    if half == 1:
                        vX = ld_cols(sprev[j3], s_d, tksl)
                    bcast2(rows["rstd2"], rows["c2"], br2[j], bc2[j], tksl)
                    return vX

                def V(i, vX):
                    j = i % 2
                    j3 = i % 3
                    sch.wait("vector", "dX", vX)
                    if i - 1 in end_H:
                        sch.wait("vector", "sS", end_H[i - 1])
                    sch.emit("vector", lambda v, j=j, j3=j3: v.tensor_tensor(
                        tmp2.ap().rearrange("p (c t) -> p c t", c=HC),
                        x2b[j3].ap().rearrange("p (c t) -> p c t", c=HC),
                        br2[j].ap().unsqueeze(1).broadcast_to([128, HC, 256]),
                        ALU.mult))
                    vv = sch.emit("vector", lambda v, j=j: v.tensor_tensor(
                        tmp2.ap().rearrange("p (c t) -> p c t", c=HC),
                        tmp2.ap().rearrange("p (c t) -> p c t", c=HC),
                        bc2[j].ap().unsqueeze(1).broadcast_to([128, HC, 256]),
                        ALU.add), inc="vS")
                    end_V[i] = vv

                def H(i):
                    j = i % 2
                    sch.wait("scalar", "vS", end_V[i])
                    if i - 2 in end_W1:
                        sch.wait("scalar", "tS", end_W1[i - 2])
                    for c in range(HC):
                        sch.emit("scalar", lambda s, c=c, j=j: s.activation(
                            h16[j].ap()[:, c * 256:(c + 1) * 256],
                            tmp2.ap()[:, c * 256:(c + 1) * 256], ACT.Identity,
                            bias=vb2[:, c:c + 1], scale=vg2[:, c:c + 1]), inc="sS")
                    end_H[i] = sch.cnt["sS"]

                def W1G(i):
                    j = i % 2
                    sch.wait("tensor", "sS", end_H[i])
                    if i == 0:
                        sch.wait("tensor", "dW", w_p2)
                    for d in range(NDC):
                        bank = d % 2
                        sch.bank_acquire("tensor", f"pb{bank}")
                        for h in range(HC):
                            vt = sch.emit("tensor", lambda t, d=d, h=h, b=bank, j=j:
                                          t.matmul(
                                              pb[b][:, 0:256],
                                              W116.ap()[:, h * DH2 + d * 128:
                                                        h * DH2 + d * 128 + 128],
                                              h16[j].ap()[:, h * 256:(h + 1) * 256],
                                              start=(h == 0), stop=(h == HC - 1)),
                                          inc="tS" if h == HC - 1 else None)
                        sch.wait("scalar", "tS", vt)
                        vs = sch.emit("scalar", lambda s, d=d, b=bank, half=half,
                                      j=j: s.activation(
                                          hid16[j].ap()[:, d * 256:(d + 1) * 256],
                                          pb[b][:, 0:256], ACT.Gelu,
                                          bias=vb1m[:, half * NDC + d:
                                                    half * NDC + d + 1]), inc="sS")
                        sch.bank_release(f"pb{bank}", "sS", vs)
                    end_W1[i] = sch.cnt["tS"]
                    end_G[i] = sch.cnt["sS"]

                def W2S(i):
                    j = i % 2
                    j3 = i % 3
                    sch.wait("tensor", "sS", end_G[i])
                    if i - 1 in end_ST:
                        sch.wait("vector", "dO", end_ST[i - 1])
                    for d in range(HC):
                        bank = 2 + (d % 2)
                        sch.bank_acquire("tensor", f"pb{bank}")
                        for h in range(NDC):
                            vt = sch.emit("tensor", lambda t, d=d, h=h, b=bank, j=j:
                                          t.matmul(
                                              pb[b][:, 0:256],
                                              W216.ap()[:, h * D + d * 128:
                                                        h * D + d * 128 + 128],
                                              hid16[j].ap()[:, h * 256:
                                                            (h + 1) * 256],
                                              start=(h == 0), stop=(h == NDC - 1)),
                                          inc="tS" if h == NDC - 1 else None)
                        sch.wait("vector", "tS", vt)
                        if half == 0:
                            vv = sch.emit("vector", lambda v, d=d, b=bank, j3=j3:
                                          v.tensor_tensor(
                                              s_t.ap()[:, d * 256:(d + 1) * 256],
                                              pb[b][:, 0:256],
                                              x2b[j3].ap()[:, d * 256:
                                                           (d + 1) * 256],
                                              ALU.add), inc="vS")
                        else:
                            vv = sch.emit("vector", lambda v, d=d, b=bank, j3=j3:
                                          v.scalar_tensor_tensor(
                                              s_t.ap()[:, d * 256:(d + 1) * 256],
                                              pb[b][:, 0:256], vb2m[:, d:d + 1],
                                              sprev[j3].ap()[:, d * 256:
                                                             (d + 1) * 256],
                                              ALU.add, ALU.add), inc="vS")
                        sch.bank_release(f"pb{bank}", "vS", vv)
                    end_S[i] = sch.cnt["vS"]

                def ST(i):
                    tksl = slice(i * 256, (i + 1) * 256)
                    sch.wait("sync", "vS", end_S[i])
                    st_cols(s_t, s_d, tksl)
                    end_ST[i] = sch.cnt["dO"]

                vX0 = LD(0)
                V(0, vX0)
                H(0)
                for i in range(NTt):
                    W1G(i)
                    if i + 1 < NTt:
                        vX = LD(i + 1)
                        V(i + 1, vX)
                        H(i + 1)
                    if i >= 1:
                        W2S(i - 1)
                        ST(i - 1)
                W2S(NTt - 1)
                ST(NTt - 1)

            if "tap_s" in taps:
                sch.wait("sync", "dO")
                sch.emit("sync", lambda s: s.dma_start(
                    out=taps["tap_s"][:, :], in_=s_d[:, :]), inc="dO", n=16)

            sch.barrier()

        with ExitStack() as p3:
            e3 = p3.enter_context
            TH = T // 2
            gateT16 = e3(nc.sbuf_tensor("gateT16", [128, HC * NE], BF16))
            cw_fm = e3(nc.sbuf_tensor("cw_fm", [NE, T], F32))

            w_gate = wld(*chunked(gateT16, gateT, HC))

            with ExitStack() as pr:
                er = pr.enter_context
                NTL = T // 128
                sc_fm = er(nc.sbuf_tensor("sc_fm", [NE, T], F32))
                sfc_fm = er(nc.sbuf_tensor("sfc_fm", [NE, T], F32))
                sc_tm = er(nc.sbuf_tensor("sc_tm", [128, NTL * NE], F32))
                sfc_tm = er(nc.sbuf_tensor("sfc_tm", [128, NTL * NE], F32))
                gsc = er(nc.sbuf_tensor("gsc", [128, NTL * 4], F32))
                gth = er(nc.sbuf_tensor("gth", [128, NTL], F32))
                emask = er(nc.sbuf_tensor("emask", [128, NTL * NE],
                                          mybir.dt.uint8))
                neg30 = er(nc.sbuf_tensor("neg30", [128, NTL * NE], F32))
                maskd = er(nc.sbuf_tensor("maskd", [128, NTL * NE], F32))
                sel = er(nc.sbuf_tensor("sel", [128, NTL * NE], F32))
                eqm = er(nc.sbuf_tensor("eqm", [128, NTL * NE], F32))
                mrow = er(nc.sbuf_tensor("mrow", [128, NTL], F32))
                sumw = er(nc.sbuf_tensor("sumw", [128, 2 * NTL], F32))
                cw_tm = er(nc.sbuf_tensor("cw_tm", [128, NTL * NE], F32))

                xng = er(nc.sbuf_tensor("xng", [128, HC * 512], BF16))
                sch.wait("tensor", "dW", w_gate)
                for gh in range(NT):
                    sch.wait("sync", "tS")
                    v_xn = ld_cols(xng, xn16_d, slice(gh * 512, (gh + 1) * 512))
                    sch.wait("tensor", "dX", v_xn)
                    for nt in range(1):
                        bank = gh % 2
                        sch.bank_acquire("tensor", f"pb{bank}")
                        for h in range(HC):
                            vt = sch.emit("tensor",
                                          lambda t, h=h, b=bank: t.matmul(
                                              pb[b][0:NE, 0:512],
                                              gateT16.ap()[:, h * NE:(h + 1) * NE],
                                              xng.ap()[:, h * 512:(h + 1) * 512],
                                              start=(h == 0), stop=(h == HC - 1)),
                                          inc="tS" if h == HC - 1 else None)
                        sch.wait("scalar", "tS", vt)
                        vs = sch.emit("scalar", lambda s, b=bank, gh=gh:
                                      s.activation(
                                          sc_fm.ap()[:, gh * 512:(gh + 1) * 512],
                                          pb[b][0:NE, 0:512], ACT.Sigmoid),
                                      inc="sS")
                        sch.bank_release(f"pb{bank}", "sS", vs)
                v_sc = sch.cnt["sS"]
                sch.wait("vector", "sS", v_sc)
                vv = sch.emit("vector", lambda v: v.tensor_scalar(
                    sfc_fm[:], sc_fm[:], vgb[:, 0:1], None, ALU.add), inc="vS")

                sch.wait("tensor", "vS", vv)
                for it in range(NTL):
                    bank = it % 2
                    sch.bank_acquire("tensor", f"pb{bank}")
                    sch.emit("tensor", lambda t, it=it, b=bank: t.matmul(
                        pb[b][:, 0:NE], sc_fm.ap()[:, it * 128:(it + 1) * 128],
                        id32[0:NE, 0:NE], start=True, stop=True))
                    vt = sch.emit("tensor", lambda t, it=it, b=bank: t.matmul(
                        pb[b][:, NE:2 * NE], sfc_fm.ap()[:, it * 128:(it + 1) * 128],
                        id32[0:NE, 0:NE], start=True, stop=True), inc="tS")
                    sch.wait("vector", "tS", vt)
                    sch.emit("vector", lambda v, it=it, b=bank: v.tensor_copy(
                        sc_tm.ap()[:, it * NE:(it + 1) * NE], pb[b][:, 0:NE]))
                    vv = sch.emit("vector", lambda v, it=it, b=bank: v.tensor_copy(
                        sfc_tm.ap()[:, it * NE:(it + 1) * NE], pb[b][:, NE:2 * NE]),
                        inc="vS")
                    sch.bank_release(f"pb{bank}", "vS", vv)

                Sf = sfc_tm.ap()
                G64 = gsc.ap()
                qtmp = eqm.ap()[:, 128:192]
                tmin = eqm.ap()[:, 0:128]
                gm64 = neg30.ap()[:, 0:64]
                M3 = maskd.ap().rearrange("p (tl e) -> p tl e", tl=NTL)

                firstpair = True
                for aa in range(4):
                    for bb in range(aa + 1, 4):
                        sch.emit("vector", lambda v, aa=aa, bb=bb: v.tensor_tensor(
                            qtmp, Sf[:, aa::4], Sf[:, bb::4], ALU.add))
                        if firstpair:
                            sch.emit("vector", lambda v: v.tensor_copy(G64, qtmp))
                            firstpair = False
                        else:
                            sch.emit("vector", lambda v: v.tensor_tensor(
                                G64, G64, qtmp, ALU.max))
                sch.vsync()
                sch.emit("vector", lambda v: v.memset(tmin, NEG))
                sch.vsync()
                kk = 0
                for aa in range(4):
                    for bb in range(aa + 1, 4):
                        sch.emit("vector", lambda v, aa=aa, bb=bb, kk=kk:
                                 v.tensor_tensor(tmin[:, kk::8], G64[:, aa::4],
                                                 G64[:, bb::4], ALU.min))
                        kk += 1
                sch.vsync()
                sch.emit("vector", lambda v: v.tensor_reduce(
                    gth.ap().unsqueeze(2)[:, :, 0:1],
                    tmin.rearrange("p (tl k) -> p tl k", tl=NTL), AX.X, ALU.max))
                sch.vsync()
                for g in range(4):
                    sch.emit("vector", lambda v, g=g: v.tensor_tensor(
                        gm64[:, g::4], G64[:, g::4], gth.ap()[:, :], ALU.is_ge))
                sch.vsync()
                for ee in range(NE):
                    sch.emit("vector", lambda v, ee=ee: v.tensor_copy(
                        emask.ap()[:, ee::NE], gm64[:, (ee // 4)::4]))
                sch.vsync()
                sch.emit("vector", lambda v: v.memset(neg30[:], NEG))
                sch.vsync()
                sch.emit("vector", lambda v: v.select(
                    maskd[:], emask[:], sfc_tm[:], neg30[:]))
                sch.emit("vector", lambda v: v.memset(sel[:], 0.0))
                for k in range(4):
                    sch.emit("vector", lambda v: v.tensor_reduce(
                        mrow.ap().unsqueeze(2)[:, :, 0:1], M3, AX.X, ALU.max))
                    sch.vsync()
                    for ee in range(NE):
                        sch.emit("vector", lambda v, ee=ee: v.tensor_tensor(
                            eqm.ap()[:, ee::NE], maskd.ap()[:, ee::NE],
                            mrow.ap()[:, :], ALU.is_equal))
                    sch.vsync()
                    sch.emit("vector", lambda v: v.tensor_tensor(
                        sel[:], sel[:], eqm[:], ALU.add))
                    if k < 3:
                        sch.emit("vector", lambda v: v.scalar_tensor_tensor(
                            maskd[:], eqm[:], -2.0e30, maskd[:], ALU.mult, ALU.add))
                sch.emit("vector", lambda v: v.tensor_tensor(
                    cw_tm[:], sel[:], sc_tm[:], ALU.mult))
                CW3 = cw_tm.ap().rearrange("p (tl e) -> p tl e", tl=NTL)
                sch.emit("vector", lambda v: v.tensor_reduce(
                    sumw.ap()[:, 0:NTL].unsqueeze(2)[:, :, 0:1], CW3, AX.X, ALU.add))
                sch.vsync()
                sch.emit("vector", lambda v: v.tensor_scalar(
                    sumw.ap()[:, 0:NTL], sumw.ap()[:, 0:NTL], 1e-20, None, ALU.add))
                sch.vsync()
                sch.emit("vector", lambda v: v.reciprocal(
                    sumw.ap()[:, NTL:2 * NTL], sumw.ap()[:, 0:NTL]))
                sch.vsync()
                sch.emit("vector", lambda v: v.tensor_scalar(
                    sumw.ap()[:, NTL:2 * NTL], sumw.ap()[:, NTL:2 * NTL], 2.5, None,
                    ALU.mult))
                sch.vsync()
                vv = None
                for ee in range(NE):
                    vv = sch.emit("vector", lambda v, ee=ee: v.tensor_tensor(
                        cw_tm.ap()[:, ee::NE], cw_tm.ap()[:, ee::NE],
                        sumw.ap()[:, NTL:2 * NTL], ALU.mult),
                        inc="vS" if ee == NE - 1 else None)
                sch.wait("vector", "vS", vv)

                sch.wait("tensor", "vS", vv)
                for it in range(NTL):
                    bank = it % 2
                    sch.bank_acquire("tensor", f"pb{bank}")
                    vt = sch.emit("tensor", lambda t, it=it, b=bank: t.matmul(
                        pb[b][0:NE, 0:128], cw_tm.ap()[:, it * NE:(it + 1) * NE],
                        id32[:, 0:128], start=True, stop=True), inc="tS")
                    sch.wait("vector", "tS", vt)
                    vv = sch.emit("vector", lambda v, it=it, b=bank: v.tensor_copy(
                        cw_fm.ap()[:, it * 128:(it + 1) * 128], pb[b][0:NE, 0:128]),
                        inc="vS")
                    sch.bank_release(f"pb{bank}", "vS", vv)

                if "tap_sel" in taps:
                    sch.wait("sync", "vS")
                    for nm, buf_ in (("tap_scfm", sc_fm), ("tap_sctm", sc_tm),
                                     ("tap_sfctm", sfc_tm), ("tap_gsc", gsc),
                                     ("tap_sel", sel), ("tap_cwtm", cw_tm)):
                        if nm in taps:
                            sch.emit("sync", lambda s, nm=nm, b=buf_: s.dma_start(
                                out=taps[nm][:, :], in_=b[:]), inc="dO", n=16)
                sch.wait("sync", "vS")
                sch.emit("sync", lambda s: s.dma_start(
                    out=cw_d[:, :], in_=cw_fm[:]), inc="dO", n=16)
                if "tap_cw" in taps:
                    sch.wait("sync", "dO")
                    sch.emit("sync", lambda s: s.dma_start(
                        out=taps["tap_cw"][:, :], in_=cw_d[:, :]), inc="dO", n=16)

            xn16s = e3(nc.sbuf_tensor("xn16s", [128, HC * TH], BF16))
            ybuf = e3(nc.sbuf_tensor("ybuf", [128, HC * TH], F32))
            Wg16 = [e3(nc.sbuf_tensor(f"Wg16_{i}", [128, HC * IE], BF16))
                    for i in range(2)]
            Wu16 = [e3(nc.sbuf_tensor(f"Wu16_{i}", [128, HC * IE], BF16))
                    for i in range(2)]
            Wd16 = [e3(nc.sbuf_tensor(f"Wd16_{i}", [128, (IE // 128) * D], BF16))
                    for i in range(2)]
            silu16 = e3(nc.sbuf_tensor("silu16", [128, (IE // 128) * 256], BF16))
            hu16 = e3(nc.sbuf_tensor("hu16", [128, (IE // 128) * 256], BF16))
            act16 = [e3(nc.sbuf_tensor(f"act16_{i}", [128, (IE // 128) * 256],
                                       BF16)) for i in range(2)]
            cwb16 = e3(nc.sbuf_tensor("cwb16", [128, 256], BF16))
            cwrow = [e3(nc.sbuf_tensor(f"cwrow{i}", [1, TH], F32)) for i in range(2)]
            Xf = e3(nc.sbuf_tensor("Xf", [128, HC * 256], F32))
            sf = e3(nc.sbuf_tensor("sf", [128, HC * 256], F32))
            gf = e3(nc.sbuf_tensor("gf", [128, HC * 256], F32))
            brf = e3(nc.sbuf_tensor("brf", [128, TH], F32))
            bcf = e3(nc.sbuf_tensor("bcf", [128, TH], F32))
            EXL = NE + 2
            GROUP = 1
            NTC = TH // 256

            def eld(buf_i, eidx):
                if eidx < NE:
                    wld(*chunked(Wg16[buf_i], WgT[eidx], HC))
                    wld(*chunked(Wu16[buf_i], WuT[eidx], HC))
                    return wld(*chunked(Wd16[buf_i], WdT[eidx], IE // 128))
                hh = eidx - NE
                wld(*chunked(Wg16[buf_i], WsgT[:, hh * IE:(hh + 1) * IE], HC))
                wld(*chunked(Wu16[buf_i], WsuT[:, hh * IE:(hh + 1) * IE], HC))
                return wld(*chunked(Wd16[buf_i], WsdT[hh * IE:(hh + 1) * IE, :],
                                    IE // 128))

            def expert_front(half0, eidx, tc, slot, buf, v_cw, prev_act):
                tok0 = tc * 256
                if eidx < NE:
                    sch.bank_acquire("tensor", "pb6")
                    if tc == 0:
                        sch.wait("tensor", "dX", v_cw)
                    vt = sch.emit("tensor", lambda t, buf=buf, tok0=tok0:
                                  t.matmul(pb[6][:, 0:256], ones_r[0:1, :],
                                           cwrow[buf].ap()[0:1, tok0:tok0 + 256],
                                           start=True, stop=True), inc="tS")
                    sch.wait("vector", "tS", vt)
                    vv = sch.emit("vector", lambda v: v.tensor_copy(
                        cwb16[:], pb[6][:, 0:256]), inc="vS")
                    sch.bank_release("pb6", "vS", vv)
                if prev_act is not None:
                    sch.wait("scalar", "vS", prev_act)
                for ic in range(IE // 128):
                    bank = 4 + (ic % 2)
                    sch.bank_acquire("tensor", f"pb{bank}")
                    for h in range(HC):
                        vt = sch.emit("tensor", lambda t, ic=ic, h=h, b=bank,
                                      buf=buf, tok0=tok0: t.matmul(
                                          pb[b][:, 0:256],
                                          Wg16[buf].ap()[:, h * IE + ic * 128:
                                                         h * IE + ic * 128 + 128],
                                          xn16s.ap()[:, h * TH + tok0:
                                                     h * TH + tok0 + 256],
                                          start=(h == 0), stop=(h == HC - 1)),
                                      inc="tS" if h == HC - 1 else None)
                    sch.wait("scalar", "tS", vt)
                    vs = sch.emit("scalar", lambda s, ic=ic, b=bank: s.activation(
                        silu16.ap()[:, ic * 256:(ic + 1) * 256], pb[b][:, 0:256],
                        ACT.Silu), inc="sS")
                    sch.bank_release(f"pb{bank}", "sS", vs)
                for ic in range(IE // 128):
                    bank = 6 + (ic % 2)
                    sch.bank_acquire("tensor", f"pb{bank}")
                    for h in range(HC):
                        vt = sch.emit("tensor", lambda t, ic=ic, h=h, b=bank,
                                      buf=buf, tok0=tok0: t.matmul(
                                          pb[b][:, 0:256],
                                          Wu16[buf].ap()[:, h * IE + ic * 128:
                                                         h * IE + ic * 128 + 128],
                                          xn16s.ap()[:, h * TH + tok0:
                                                     h * TH + tok0 + 256],
                                          start=(h == 0), stop=(h == HC - 1)),
                                      inc="tS" if h == HC - 1 else None)
                    sch.wait("vector", "tS", vt)
                    if eidx < NE:
                        vv = sch.emit("vector", lambda v, ic=ic, b=bank:
                                      v.scalar_tensor_tensor(
                                          hu16.ap()[:, ic * 256:(ic + 1) * 256],
                                          pb[b][:, 0:256], 1.0, cwb16[:],
                                          ALU.mult, ALU.mult), inc="vS")
                    else:
                        vv = sch.emit("vector", lambda v, ic=ic, b=bank:
                                      v.tensor_copy(
                                          hu16.ap()[:, ic * 256:(ic + 1) * 256],
                                          pb[b][:, 0:256]), inc="vS")
                    sch.bank_release(f"pb{bank}", "vS", vv)
                sch.wait("vector", "sS")
                return sch.emit("vector", lambda v, slot=slot: v.tensor_tensor(
                    act16[slot][:], silu16[:], hu16[:], ALU.mult), inc="vS")

            def expert_back(eidx, tc, slot, buf, vv_act):
                tok0 = tc * 256
                sch.wait("tensor", "vS", vv_act)
                for d in range(HC):
                    bank = d // 2
                    hfb = d % 2
                    if hfb == 0:
                        sch.bank_acquire("tensor", f"pb{bank}")
                    for icc in range(IE // 128):
                        last = (icc == IE // 128 - 1 and d == HC - 1)
                        vt = sch.emit("tensor", lambda t, d=d, icc=icc, b=bank,
                                      hfb=hfb, buf=buf, slot=slot: t.matmul(
                                          pb[b][:, hfb * 256:hfb * 256 + 256],
                                          Wd16[buf].ap()[:, icc * D + d * 128:
                                                         icc * D + d * 128 + 128],
                                          act16[slot].ap()[:, icc * 256:
                                                           (icc + 1) * 256],
                                          start=(icc == 0),
                                          stop=(icc == IE // 128 - 1),
                                          skip_group_check=True),
                                      inc="tS" if last else None)
                sch.wait("vector", "tS", vt)
                for d in range(HC):
                    bank = d // 2
                    hfb = d % 2
                    if eidx == 0:
                        vv = sch.emit("vector", lambda v, d=d, b=bank, hfb=hfb,
                                      tok0=tok0: v.tensor_copy(
                                          ybuf.ap()[:, d * TH + tok0:
                                                    d * TH + tok0 + 256],
                                          pb[b][:, hfb * 256:hfb * 256 + 256]),
                                      inc="vS")
                    else:
                        vv = sch.emit("vector", lambda v, d=d, b=bank, hfb=hfb,
                                      tok0=tok0: v.tensor_tensor(
                                          ybuf.ap()[:, d * TH + tok0:
                                                    d * TH + tok0 + 256],
                                          ybuf.ap()[:, d * TH + tok0:
                                                    d * TH + tok0 + 256],
                                          pb[b][:, hfb * 256:hfb * 256 + 256],
                                          ALU.add), inc="vS")
                    if hfb == 1:
                        sch.bank_release(f"pb{bank}", "vS", vv)

            def expert_loop(half0, v_xn, w_e, v_cw0):
                sch.wait("tensor", "dX", v_xn)
                pipe = []
                v_cw = {0: v_cw0}
                gstep = 0
                prev_act = None
                for eidx in range(EXL):
                    buf = eidx % 2
                    if eidx + 1 < NE:
                        sch.wait("sync", "tS")
                        v_cw[eidx + 1] = ld(cwrow[(eidx + 1) % 2][0:1, :],
                                            cw_d[eidx + 1:eidx + 2,
                                                 half0:half0 + TH])
                    sch.wait("tensor", "dW", w_e[eidx])
                    for tc in range(NTC):
                        slot = gstep % 2
                        vv_act = expert_front(half0, eidx, tc, slot, buf,
                                              v_cw.get(eidx), prev_act)
                        prev_act = vv_act
                        pipe.append((eidx, tc, slot, buf, vv_act))
                        if len(pipe) > 1:
                            expert_back(*pipe.pop(0))
                        gstep += 1
                        if tc == 1 and 1 <= eidx < EXL - 1:
                            sch.wait("gpsimd", "tS", sch.cnt["tS"])
                            w_e[eidx + 1] = eld((eidx + 1) % 2, eidx + 1)
                expert_back(*pipe.pop(0))

            def finale_bcast(half0):
                for tc in range(NTC):
                    bcast2(rows["rstd0"], rows["c0"],
                           brf.ap()[:, tc * 256:(tc + 1) * 256],
                           bcf.ap()[:, tc * 256:(tc + 1) * 256],
                           slice(half0 + tc * 256, half0 + (tc + 1) * 256))

            def finale(half0):
                for tc in range(NTC):
                    tok0 = tc * 256
                    tksl = slice(half0 + tok0, half0 + tok0 + 256)
                    sch.wait("sync", "gS")
                    sch.wait("sync", "vS")
                    ld_cols(Xf, X, tksl)
                    vX = ld_cols(sf, s_d, tksl)
                    sch.wait("vector", "dX", vX)
                    sch.emit("vector", lambda v, tok0=tok0: v.tensor_tensor(
                        gf.ap().rearrange("p (c t) -> p c t", c=HC),
                        Xf.ap().rearrange("p (c t) -> p c t", c=HC),
                        brf.ap()[:, tok0:tok0 + 256].unsqueeze(1).broadcast_to(
                            [128, HC, 256]), ALU.mult))
                    sch.emit("vector", lambda v, tok0=tok0: v.tensor_tensor(
                        gf.ap().rearrange("p (c t) -> p c t", c=HC),
                        gf.ap().rearrange("p (c t) -> p c t", c=HC),
                        bcf.ap()[:, tok0:tok0 + 256].unsqueeze(1).broadcast_to(
                            [128, HC, 256]), ALU.add))
                    vv = sch.emit("vector", lambda v, tok0=tok0: v.tensor_tensor(
                        gf.ap().rearrange("p (c t) -> p c t", c=HC),
                        gf.ap().rearrange("p (c t) -> p c t", c=HC),
                        ybuf.ap().rearrange("p (c t2) -> p c t2", c=HC)[
                            :, :, tok0:tok0 + 256], ALU.add), inc="vS")
                    sch.wait("scalar", "vS", vv)
                    vs = sch.emit("scalar", lambda s: s.activation(
                        gf[:], gf[:], ACT.Silu), inc="sS")
                    sch.wait("gpsimd", "sS", vs)
                    sch.wait("gpsimd", "dX", vX)
                    sch.wait("gpsimd", "dO")
                    vg = sch.emit("gpsimd", lambda g: g.tensor_tensor(
                        sf[:], sf[:], gf[:], ALU.mult), inc="gS")
                    sch.wait("vector", "gS", vg)
                    vv = sch.emit("vector", lambda v: v.tensor_tensor(
                        Xf[:], Xf[:], sf[:], ALU.add), inc="vS")
                    sch.wait("sync", "vS", vv)
                    st_cols(Xf, O, tksl)

            sch.wait("sync", "tS")
            v_xn_cur = ld_cols(xn16s, xn16_d, slice(0, TH))
            sch.wait("gpsimd", "tS")
            w_e_cur = [None] * EXL
            w_e_cur[0] = eld(0, 0)
            w_e_cur[1] = eld(1, 1)
            sch.wait("sync", "tS")
            v_cw0_cur = ld(cwrow[0][0:1, :], cw_d[0:1, 0:TH])

            for half in range(2):
                half0 = half * TH
                expert_loop(half0, v_xn_cur, w_e_cur, v_cw0_cur)
                finale_bcast(half0)
                if half == 0:
                    sch.wait("sync", "tS")
                    v_xn_cur = ld_cols(xn16s, xn16_d, slice(TH, T))
                    sch.wait("gpsimd", "tS")
                    w_e_cur = [None] * EXL
                    w_e_cur[0] = eld(0, 0)
                    w_e_cur[1] = eld(1, 1)
                    sch.wait("sync", "tS")
                    v_cw0_cur = ld(cwrow[0][0:1, :], cw_d[0:1, TH:T])
                if "tap_y" in taps:
                    sch.wait("sync", "vS")
                    for c in range(HC):
                        sch.emit("sync", lambda s, c=c, half0=half0: s.dma_start(
                            out=taps["tap_y"][c * 128:(c + 1) * 128,
                                              half0:half0 + TH],
                            in_=ybuf.ap()[:, c * TH:(c + 1) * TH]), inc="dO", n=16)
                finale(half0)

            sch.barrier()

        block = ctx.enter_context(nc.Block())

        @block.sync
        def _(eng):
            for fn in sch.q["sync"]:
                fn(eng)

        @block.gpsimd
        def _(eng):
            for fn in sch.q["gpsimd"]:
                fn(eng)

        @block.vector
        def _(eng):
            for fn in sch.q["vector"]:
                fn(eng)

        @block.scalar
        def _(eng):
            for fn in sch.q["scalar"]:
                fn(eng)

        @block.tensor
        def _(eng):
            for fn in sch.q["tensor"]:
                fn(eng)

    return nc


def host_inputs(inputs):
    x = np.asarray(inputs["L0_attn_x"], dtype=np.float32)
    m_int, m_q1, m_q0 = make_masks()

    def t(a):
        return np.ascontiguousarray(np.asarray(a, dtype=np.float32).T)

    def cc(vec, n):
        return np.ascontiguousarray(
            np.asarray(vec, dtype=np.float32).reshape(n, 128).T)

    common = dict(
        Wqkv=np.ascontiguousarray(np.asarray(inputs["L0_attn_W_qkv"], np.float32)),
        WO=np.ascontiguousarray(np.asarray(inputs["L0_attn_W_O"], np.float32)),
        Wug=np.ascontiguousarray(np.asarray(inputs["L0_attn_W_ug"], np.float32)),
        Bw=np.ascontiguousarray(np.asarray(inputs["L0_attn_B_w"], np.float32)),
        Cw=np.ascontiguousarray(np.asarray(inputs["L0_attn_C_w"], np.float32)),
        W1=np.ascontiguousarray(np.asarray(inputs["L0_attn_mlp_W1"], np.float32)),
        W2=np.ascontiguousarray(np.asarray(inputs["L0_attn_mlp_W2"], np.float32)),
        gateT=t(inputs["L0_ffn_gate_w"]),
        WgT=np.ascontiguousarray(
            np.asarray(inputs["L0_ffn_Wg"], np.float32).transpose(0, 2, 1)),
        WuT=np.ascontiguousarray(
            np.asarray(inputs["L0_ffn_Wu"], np.float32).transpose(0, 2, 1)),
        WdT=np.ascontiguousarray(
            np.asarray(inputs["L0_ffn_Wd"], np.float32).transpose(0, 2, 1)),
        WsgT=t(inputs["L0_ffn_Ws_g"]),
        WsuT=t(inputs["L0_ffn_Ws_u"]),
        WsdT=t(inputs["L0_ffn_Ws_d"]),
        g1c=cc(inputs["L0_attn_ln1_g"], HC), b1c=cc(inputs["L0_attn_ln1_b"], HC),
        g2c=cc(inputs["L0_attn_ln2_g"], HC), b2c=cc(inputs["L0_attn_ln2_b"], HC),
        bOc=cc(inputs["L0_attn_b_O"], HC), b2mc=cc(inputs["L0_attn_mlp_b2"], HC),
        bugc=cc(inputs["L0_attn_b_ug"], HC + 1),
        b1mc=cc(inputs["L0_attn_mlp_b1"], DFF // 128),
        gbias=np.ascontiguousarray(
            np.asarray(inputs["L0_ffn_gate_bias"], np.float32).reshape(NE, 1)),
        Avec=np.ascontiguousarray(
            np.asarray(inputs["L0_attn_A"], np.float32).reshape(SD, 1)),
        id16=np.eye(128, dtype=np.float32).astype(ml_dtypes.bfloat16),
        id32=np.eye(128, dtype=np.float32),
        mint=m_int, mq1=m_q1, mq0=m_q0,
    )
    in_maps = []
    for b in range(8):
        m = dict(common)
        m["X"] = np.ascontiguousarray(x[b].T)
        in_maps.append(m)
    return in_maps


def kernel(**inputs):
    in_maps = host_inputs(inputs)
    nc = build_nc()
    res = run_bass_kernel_spmd(nc, in_maps, core_ids=list(range(8)))
    out = np.stack([np.asarray(r["O"]).T for r in res.results])
    return out.astype(np.float32)


if __name__ == "__main__":
    nc = build_nc()
    print("built", nc.n_instructions(), "instructions")


# version 11
# speedup vs baseline: 1.1067x; 1.1067x over previous
import numpy as np
import ml_dtypes
from contextlib import ExitStack

import concourse.bass as bass
import concourse.mybir as mybir
from concourse.bass_utils import run_bass_kernel_spmd

F32 = mybir.dt.float32
BF16 = mybir.dt.bfloat16
AX = mybir.AxisListType
ALU = mybir.AluOpType
ACT = mybir.ActivationFunctionType

D = 1024
T = 2048
HC = D // 128
NH = 16
DH = 64
SD = 128
DFF = 4096
NE = 16
IE = 512
EPS = 1e-5
NEG = -1.0e30
NT = 4

ENGINES = ("sync", "gpsimd", "vector", "scalar", "tensor")

TAPS = set()


class Sched:
    def __init__(self, nc, ctx):
        self.nc = nc
        self.q = {e: [] for e in ENGINES}
        self.sem = {}
        self.cnt = {}
        for name in ("dW", "dX", "dO", "tS", "vS", "sS", "gS", "B"):
            self.sem[name] = ctx.enter_context(nc.semaphore(name))
            self.cnt[name] = 0
        self.bank_last = {}

    def emit(self, eng, fn, inc=None, n=1):
        if inc is not None:
            sem = self.sem[inc]
            self.cnt[inc] += n
            self.q[eng].append(lambda e, fn=fn, sem=sem, n=n: fn(e).then_inc(sem, n))
            return self.cnt[inc]
        self.q[eng].append(fn)
        return None

    def wait(self, eng, name, val=None):
        v = self.cnt[name] if val is None else val
        s = self.sem[name]
        self.q[eng].append(lambda e, s=s, v=v: e.wait_ge(s, v))

    def bank_acquire(self, eng, bank):
        last = self.bank_last.get(bank)
        if last is not None:
            sem_name, v = last
            self.wait(eng, sem_name, v)
            self.bank_last[bank] = None

    def bank_release(self, bank, sem_name, v):
        self.bank_last[bank] = (sem_name, v)

    def vsync(self):
        v = self.emit("vector", lambda e: e.tensor_copy(self.dumV[:], self.dumV[:]),
                      inc="vS")
        self.wait("vector", "vS", v)

    def barrier(self):
        semB = self.sem["B"]
        self.wait("sync", "dX")
        self.wait("sync", "dO")
        self.wait("gpsimd", "dW")
        self.q["vector"].append(
            lambda e: e.tensor_copy(self.dumV[:], self.dumV[:]).then_inc(semB, 1))
        self.q["scalar"].append(
            lambda e: e.copy(self.dumS[:], self.dumS[:]).then_inc(semB, 1))
        self.q["gpsimd"].append(
            lambda e: e.tensor_copy(self.dumG[:], self.dumG[:]).then_inc(semB, 1))
        self.q["tensor"].append(lambda e: e.nop().then_inc(semB, 1))
        self.q["sync"].append(lambda e: e.nop().then_inc(semB, 1))
        self.cnt["B"] += 5
        tgt = self.cnt["B"]
        for eng in ENGINES:
            self.wait(eng, "B", tgt)
        self.bank_last = {}


def make_masks():
    i = np.arange(128)[:, None]
    j3 = np.arange(384)[None, :]
    m_int = np.where((j3 >= i + 1) & (j3 <= i + 256), 0.0, NEG)
    j2 = np.arange(256)[None, :]
    m_q1 = np.where(j2 <= i + 128, 0.0, NEG)
    j1 = np.arange(128)[None, :]
    m_q0 = np.where(j1 <= i, 0.0, NEG)
    bf = lambda a: np.ascontiguousarray(a.astype(np.float32)).astype(ml_dtypes.bfloat16)
    return bf(m_int), bf(m_q1), bf(m_q0)


def build_nc():
    nc = bass.Bass("TRN2", target_bir_lowering=False)
    ctx = ExitStack()

    def dram_in(name, shape, dt=F32):
        return nc.dram_tensor(name, list(shape), dt, kind="ExternalInput").ap()

    X = dram_in("X", [D, T])
    Wqkv = dram_in("Wqkv", [D, 3 * D])
    WO = dram_in("WO", [D, D])
    Wug = dram_in("Wug", [D, SD + D])
    Bw = dram_in("Bw", [D, SD])
    Cw = dram_in("Cw", [SD, D])
    W1 = dram_in("W1", [D, DFF])
    W2 = dram_in("W2", [DFF, D])
    gateT = dram_in("gateT", [D, NE])
    WgT = dram_in("WgT", [NE, D, IE])
    WuT = dram_in("WuT", [NE, D, IE])
    WdT = dram_in("WdT", [NE, IE, D])
    WsgT = dram_in("WsgT", [D, 2 * IE])
    WsuT = dram_in("WsuT", [D, 2 * IE])
    WsdT = dram_in("WsdT", [2 * IE, D])
    g1c = dram_in("g1c", [128, HC])
    b1c = dram_in("b1c", [128, HC])
    g2c = dram_in("g2c", [128, HC])
    b2c = dram_in("b2c", [128, HC])
    bOc = dram_in("bOc", [128, HC])
    b2mc = dram_in("b2mc", [128, HC])
    bugc = dram_in("bugc", [128, HC + 1])
    b1mc = dram_in("b1mc", [128, DFF // 128])
    gbias = dram_in("gbias", [NE, 1])
    Avec_d = dram_in("Avec", [SD, 1])
    id16_d = dram_in("id16", [128, 128], BF16)
    id32_d = dram_in("id32", [128, 128])
    mint_d = dram_in("mint", [128, 384], BF16)
    mq1_d = dram_in("mq1", [128, 256], BF16)
    mq0_d = dram_in("mq0", [128, 128], BF16)

    O = nc.dram_tensor("O", [D, T], F32, kind="ExternalOutput").ap()

    xn16_d = nc.dram_tensor("xn16_d", [D, T], BF16, kind="Internal").ap()
    xn1_d = nc.dram_tensor("xn1_d", [D, T], BF16, kind="Internal").ap()
    gt_d = nc.dram_tensor("gt_d", [D, T], F32, kind="Internal").ap()
    x2_d = nc.dram_tensor("x2_d", [D, T], F32, kind="Internal").ap()
    s_d = nc.dram_tensor("s_d", [D, T], F32, kind="Internal").ap()
    cw_d = nc.dram_tensor("cw_d", [NE, T], F32, kind="Internal").ap()
    y_d = nc.dram_tensor("y_d", [D, T], F32, kind="Internal").ap()

    taps = {}
    for tname, shape, dt in (
        ("tap_xn16", [D, T], BF16), ("tap_xn1", [D, T], BF16),
        ("tap_gt", [D, T], F32), ("tap_drive", [SD, T], F32),
        ("tap_states", [SD, T], F32), ("tap_x2", [D, T], F32),
        ("tap_s", [D, T], F32), ("tap_cw", [NE, T], F32),
        ("tap_y", [D, T], F32), ("tap_attn", [D, T], F32),
        ("tap_yssm", [D, T], F32), ("tap_scfm", [NE, T], F32),
        ("tap_sctm", [128, (T // 128) * NE], F32),
        ("tap_sfctm", [128, (T // 128) * NE], F32),
        ("tap_gsc", [128, (T // 128) * 4], F32),
        ("tap_sel", [128, (T // 128) * NE], F32),
        ("tap_cwtm", [128, (T // 128) * NE], F32),
    ):
        if tname in TAPS:
            taps[tname] = nc.dram_tensor(tname, shape, dt, kind="ExternalOutput").ap()

    with ctx:
        e = ctx.enter_context
        sch = Sched(nc, ctx)

        id16 = e(nc.sbuf_tensor("id16s", [128, 128], BF16))
        id32 = e(nc.sbuf_tensor("id32s", [128, 128], F32))
        mint = e(nc.sbuf_tensor("mints", [128, 384], BF16))
        mq1 = e(nc.sbuf_tensor("mq1s", [128, 256], BF16))
        mq0 = e(nc.sbuf_tensor("mq0s", [128, 128], BF16))
        ones_c = e(nc.sbuf_tensor("ones_c", [128, 2], F32))
        ones_r = e(nc.sbuf_tensor("ones_r", [1, 128], F32))
        zrow16 = e(nc.sbuf_tensor("zrow16", [1, 512], BF16))
        vg1 = e(nc.sbuf_tensor("vg1", [128, HC], F32))
        vb1 = e(nc.sbuf_tensor("vb1", [128, HC], F32))
        vg2 = e(nc.sbuf_tensor("vg2", [128, HC], F32))
        vb2 = e(nc.sbuf_tensor("vb2", [128, HC], F32))
        vbO = e(nc.sbuf_tensor("vbO", [128, HC], F32))
        vb2m = e(nc.sbuf_tensor("vb2m", [128, HC], F32))
        vbug = e(nc.sbuf_tensor("vbug", [128, HC + 1], F32))
        vb1m = e(nc.sbuf_tensor("vb1m", [128, DFF // 128], F32))
        vgb = e(nc.sbuf_tensor("vgb", [NE, 1], F32))
        vA = e(nc.sbuf_tensor("vA", [SD, 1], F32))

        rows = {}
        for rn in ("rstd0", "c0", "mu2", "rstd2", "c2"):
            rows[rn] = e(nc.sbuf_tensor("row_" + rn, [1, T], F32))

        pb = [e(nc.psum_tensor(f"pb{i}", [128, 512], F32)) for i in range(8)]
        sch.dumV = e(nc.sbuf_tensor("dumV", [128, 64], F32))
        sch.dumS = e(nc.sbuf_tensor("dumS", [128, 64], F32))
        sch.dumG = e(nc.sbuf_tensor("dumG", [128, 64], F32))

        def ld(dst, src):
            return sch.emit("sync",
                            lambda s, d=dst, a=src: s.dma_start(out=d, in_=a),
                            inc="dX", n=16)

        def wld(dst_ap, src_ap):
            return sch.emit("gpsimd",
                            lambda g, d=dst_ap, s=src_ap: g.dma_start(out=d, in_=s),
                            inc="dW", n=16)

        def chunked(dst_sb, src_dram, cdim):
            return (dst_sb.ap().rearrange("p (c j) -> p c j", c=cdim),
                    src_dram.rearrange("(c p) j -> p c j", p=128))

        def ld_cols(dst_sb, src_dram, tksl, cdim=HC):
            return ld(dst_sb.ap().rearrange("p (c t) -> p c t", c=cdim),
                      src_dram[:, tksl].rearrange("(c p) t -> p c t", p=128))

        def st_cols(src_sb, dst_dram, tksl, cdim=HC):
            return sch.emit("sync", lambda s, a=src_sb, d=dst_dram, tk=tksl, c=cdim:
                            s.dma_start(
                                out=d[:, tk].rearrange("(c p) t -> p c t", p=128),
                                in_=a.ap().rearrange("p (c t) -> p c t", c=c)),
                            inc="dO", n=16)

        ld(id16[:], id16_d[:, :])
        ld(id32[:], id32_d[:, :])
        ld(mint[:], mint_d[:, :])
        ld(mq1[:], mq1_d[:, :])
        ld(mq0[:], mq0_d[:, :])
        for dst, src in ((vg1, g1c), (vb1, b1c), (vg2, g2c), (vb2, b2c),
                         (vbO, bOc), (vb2m, b2mc), (vbug, bugc), (vb1m, b1mc),
                         (vgb, gbias), (vA, Avec_d)):
            ld(dst[:], src[:, :])
        cst_loaded = sch.cnt["dX"]

        sch.wait("vector", "dX", cst_loaded)
        sch.emit("vector", lambda v: v.memset(ones_c[:], 1.0))
        sch.emit("vector", lambda v: v.memset(ones_r[:], 1.0))
        v_consts = sch.emit("vector", lambda v: v.memset(zrow16[:], 0.0), inc="vS")
        sch.wait("tensor", "vS", v_consts)

        def bcast2(row_a, row_b, dst_a, dst_b, tksl, banks=(6, 7), n=256):
            ba, bb = banks
            sch.bank_acquire("tensor", f"pb{ba}")
            sch.emit("tensor", lambda t, r=row_a, tk=tksl, n=n: t.matmul(
                pb[ba][:, 0:n], ones_r[0:1, :], r[0:1, tk], start=True, stop=True))
            sch.bank_acquire("tensor", f"pb{bb}")
            vt = sch.emit("tensor", lambda t, r=row_b, tk=tksl, n=n: t.matmul(
                pb[bb][:, 0:n], ones_r[0:1, :], r[0:1, tk], start=True, stop=True),
                inc="tS")
            sch.wait("vector", "tS", vt)
            sch.emit("vector", lambda v, d=dst_a, n=n: v.tensor_copy(
                d[:, 0:n], pb[ba][:, 0:n]))
            vv = sch.emit("vector", lambda v, d=dst_b, n=n: v.tensor_copy(
                d[:, 0:n], pb[bb][:, 0:n]), inc="vS")
            sch.bank_release(f"pb{ba}", "vS", vv)
            sch.bank_release(f"pb{bb}", "vS", vv)
            return vv

        spanA = ExitStack()
        Wug16 = spanA.enter_context(
            nc.sbuf_tensor("Wug16", [128, HC * (SD + D)], BF16, side="right"))
        Bw16 = spanA.enter_context(
            nc.sbuf_tensor("Bw16", [128, HC * SD], BF16, side="right"))
        wld(*chunked(Wug16, Wug, HC))
        w_p1a = wld(*chunked(Bw16, Bw, HC))

        spanB = ExitStack()
        eB = spanB.enter_context
        drive = eB(nc.sbuf_tensor("drive", [SD, T], F32))
        states = eB(nc.sbuf_tensor("states", [SD, T], F32))
        st16 = eB(nc.sbuf_tensor("st16", [SD, T], BF16))
        pw64 = eB(nc.sbuf_tensor("pw64", [SD, 64], F32))
        apows = eB(nc.sbuf_tensor("apows", [SD, 16], F32))

        with ExitStack() as p0:
            e0 = p0.enter_context
            xbuf = [e0(nc.sbuf_tensor(f"xbuf{i}", [128, T], F32)) for i in range(2)]
            xsq2 = [e0(nc.sbuf_tensor(f"xsq{i}", [128, T], F32)) for i in range(2)]
            bA = e0(nc.sbuf_tensor("bA", [128, T], F32))
            bB = e0(nc.sbuf_tensor("bB", [128, T], F32))
            stg = [e0(nc.sbuf_tensor(f"stg{i}", [128, T], BF16)) for i in range(2)]
            prow = {}
            for rn in ("mu0", "mu1", "rstd1", "tmpA", "tmpB"):
                prow[rn] = e0(nc.sbuf_tensor("prow_" + rn, [1, T], F32))

            def rsqrt_row(dst_row, src_row):
                sch.emit("scalar", lambda s: s.activation(
                    prow["tmpB"][0:1, :], src_row, ACT.Ln))
                return sch.emit("scalar", lambda s: s.activation(
                    dst_row, prow["tmpB"][0:1, :], ACT.Exp, scale=-0.5), inc="sS")

            def stats_mms(src_fn, c):
                if c == 0:
                    for b_ in range(2 * NT):
                        sch.bank_acquire("tensor", f"pb{b_}")
                for nt in range(NT):
                    sch.emit("tensor", lambda t, nt=nt, c=c, src=src_fn: t.matmul(
                        pb[nt][0:1, 0:512], ones_c[:, 0:1], src(nt),
                        start=(c == 0), stop=(c == HC - 1)))

            def stats_mms_sq(c):
                vt = None
                xq = xsq2[c % 2]
                for nt in range(NT):
                    vt = sch.emit("tensor", lambda t, nt=nt, c=c, xq=xq: t.matmul(
                        pb[4 + nt][0:1, 0:512], ones_c[:, 0:1],
                        xq[:, nt * 512:(nt + 1) * 512],
                        start=(c == 0), stop=(c == HC - 1)),
                        inc="tS" if nt == NT - 1 else None)
                return vt

            def row_stats(mu_row, rstd_row):
                for nt in range(NT):
                    sch.emit("vector", lambda v, nt=nt, mu=mu_row: v.tensor_scalar(
                        mu[0:1, nt * 512:(nt + 1) * 512], pb[nt][0:1, 0:512],
                        1.0 / D, None, ALU.mult))
                    sch.emit("vector", lambda v, nt=nt: v.tensor_scalar(
                        prow["tmpA"][0:1, nt * 512:(nt + 1) * 512],
                        pb[4 + nt][0:1, 0:512], 1.0 / D, None, ALU.mult))
                vv = sch.emit("vector", lambda v, mu=mu_row: v.tensor_tensor(
                    prow["tmpB"][0:1, :], mu[0:1, :], mu[0:1, :], ALU.mult),
                    inc="vS")
                for b_ in range(2 * NT):
                    sch.bank_release(f"pb{b_}", "vS", vv)
                sch.emit("vector", lambda v: v.tensor_tensor(
                    prow["tmpA"][0:1, :], prow["tmpA"][0:1, :], prow["tmpB"][0:1, :],
                    ALU.subtract))
                vv = sch.emit("vector", lambda v: v.tensor_scalar(
                    prow["tmpA"][0:1, :], prow["tmpA"][0:1, :], EPS, None, ALU.add),
                    inc="vS")
                sch.wait("scalar", "vS", vv)
                return rsqrt_row(rstd_row, prow["tmpA"][0:1, :])

            ld(xbuf[0][:], X[0:128, :])
            ld(xbuf[1][:], X[128:256, :])
            vts = {}
            for c in range(HC):
                x_c = xbuf[c % 2]
                sch.wait("scalar", "dX")
                if c >= 2:
                    sch.wait("scalar", "tS", vts[c - 2])
                vs = sch.emit("scalar", lambda s, xc=x_c, c=c: s.activation(
                    xsq2[c % 2][:], xc[:], ACT.Square), inc="sS")
                sch.wait("tensor", "dX")
                stats_mms(lambda nt, xc=x_c: xc[:, nt * 512:(nt + 1) * 512], c)
                sch.wait("tensor", "sS", vs)
                vts[c] = stats_mms_sq(c)
                if c + 2 < HC:
                    sch.wait("sync", "tS", vts[c])
                    sch.wait("sync", "sS", vs)
                    ld(xbuf[c % 2][:], X[(c + 2) * 128:(c + 3) * 128, :])
            sch.wait("vector", "tS")
            vs_r0 = row_stats(prow["mu0"][0:1, :], rows["rstd0"][0:1, :])
            sch.wait("vector", "sS", vs_r0)
            vv_c0 = sch.emit("vector", lambda v: v.scalar_tensor_tensor(
                rows["c0"][0:1, :], prow["mu0"][0:1, :], -1.0, rows["rstd0"][0:1, :],
                ALU.mult, ALU.mult), inc="vS")
            sch.wait("tensor", "vS", vv_c0)
            for nt in range(NT):
                bcast2(rows["rstd0"], rows["c0"],
                       bA.ap()[:, nt * 512:nt * 512 + 512],
                       bB.ap()[:, nt * 512:nt * 512 + 512],
                       slice(nt * 512, (nt + 1) * 512), banks=(nt % 2, 2 + nt % 2),
                       n=512)

            sch.wait("sync", "tS")
            sch.wait("sync", "sS")
            sch.wait("sync", "vS")
            ld(xbuf[0][:], X[0:128, :])
            ld(xbuf[1][:], X[128:256, :])
            vts = {}
            for c in range(HC):
                x_c = xbuf[c % 2]
                sch.wait("vector", "dX")
                sch.emit("vector", lambda v, xc=x_c: v.tensor_tensor(
                    xc[:], xc[:], bA[:], ALU.mult))
                vv = sch.emit("vector", lambda v, xc=x_c: v.tensor_tensor(
                    xc[:], xc[:], bB[:], ALU.add), inc="vS")
                st = stg[c % 2]
                sch.wait("vector", "dO")
                vv2 = sch.emit("vector", lambda v, xc=x_c, st=st: v.tensor_copy(
                    st[:], xc[:]), inc="vS")
                sch.wait("sync", "vS", vv2)
                sch.emit("sync", lambda s, st=st, c=c: s.dma_start(
                    out=xn16_d[c * 128:(c + 1) * 128, :], in_=st[:]), inc="dO", n=16)
                sch.wait("scalar", "vS", vv)
                if c >= 2:
                    sch.wait("scalar", "tS", vts[c - 2])
                vs = sch.emit("scalar", lambda s, xc=x_c, c=c: s.activation(
                    xsq2[c % 2][:], xc[:], ACT.Square), inc="sS")
                sch.wait("tensor", "vS", vv)
                stats_mms(lambda nt, xc=x_c: xc[:, nt * 512:(nt + 1) * 512], c)
                sch.wait("tensor", "sS", vs)
                vts[c] = stats_mms_sq(c)
                if c + 2 < HC:
                    sch.wait("sync", "tS", vts[c])
                    sch.wait("sync", "vS", vv2)
                    ld(xbuf[c % 2][:], X[(c + 2) * 128:(c + 3) * 128, :])
            sch.wait("vector", "tS")
            vs_r1 = row_stats(prow["mu1"][0:1, :], prow["rstd1"][0:1, :])
            sch.wait("vector", "sS", vs_r1)
            sch.emit("vector", lambda v: v.tensor_tensor(
                prow["tmpA"][0:1, :], rows["rstd0"][0:1, :], prow["rstd1"][0:1, :],
                ALU.mult))
            sch.emit("vector", lambda v: v.tensor_tensor(
                prow["tmpB"][0:1, :], rows["c0"][0:1, :], prow["mu1"][0:1, :],
                ALU.subtract))
            vv_cc = sch.emit("vector", lambda v: v.tensor_tensor(
                prow["tmpB"][0:1, :], prow["tmpB"][0:1, :], prow["rstd1"][0:1, :],
                ALU.mult), inc="vS")
            sch.wait("tensor", "vS", vv_cc)
            for nt in range(NT):
                bcast2(prow["tmpA"], prow["tmpB"],
                       bA.ap()[:, nt * 512:nt * 512 + 512],
                       bB.ap()[:, nt * 512:nt * 512 + 512],
                       slice(nt * 512, (nt + 1) * 512), banks=(nt % 2, 2 + nt % 2),
                       n=512)

            sch.wait("sync", "tS")
            sch.wait("sync", "sS")
            sch.wait("sync", "vS")
            ld(xbuf[0][:], X[0:128, :])
            ld(xbuf[1][:], X[128:256, :])
            for c in range(HC):
                x_c = xbuf[c % 2]
                sch.wait("vector", "dX")
                sch.emit("vector", lambda v, xc=x_c: v.tensor_tensor(
                    xc[:], xc[:], bA[:], ALU.mult))
                vv = sch.emit("vector", lambda v, xc=x_c: v.tensor_tensor(
                    xc[:], xc[:], bB[:], ALU.add), inc="vS")
                st = stg[c % 2]
                sch.wait("scalar", "vS", vv)
                sch.wait("scalar", "dO")
                vs = sch.emit("scalar", lambda s, xc=x_c, st=st, c=c: s.activation(
                    st[:], xc[:], ACT.Identity,
                    bias=vb1[:, c:c + 1], scale=vg1[:, c:c + 1]), inc="sS")
                sch.wait("sync", "sS", vs)
                sch.emit("sync", lambda s, st=st, c=c: s.dma_start(
                    out=xn1_d[c * 128:(c + 1) * 128, :], in_=st[:]), inc="dO", n=16)
                if c + 2 < HC:
                    sch.wait("sync", "vS", vv)
                    sch.wait("sync", "sS", vs)
                    ld(xbuf[c % 2][:], X[(c + 2) * 128:(c + 3) * 128, :])

            if "tap_xn1" in taps:
                sch.wait("sync", "dO")
                sch.emit("sync", lambda s: s.dma_start(
                    out=taps["tap_xn1"][:, :], in_=xn1_d[:, :]), inc="dO", n=16)
            if "tap_xn16" in taps:
                sch.wait("sync", "dO")
                sch.emit("sync", lambda s: s.dma_start(
                    out=taps["tap_xn16"][:, :], in_=xn16_d[:, :]), inc="dO", n=16)

            sch.barrier()

        with ExitStack() as p1a:
            e1 = p1a.enter_context
            gtst = [e1(nc.sbuf_tensor(f"gtst{i}", [128, 512], F32)) for i in range(4)]
            xn1t = [e1(nc.sbuf_tensor(f"xn1tA{i}", [128, HC * 512], BF16))
                    for i in range(4)]

            sch.wait("tensor", "dW", w_p1a)
            ld_nt = {}
            for nt in range(NT):
                ld_nt[nt] = ld_cols(xn1t[nt], xn1_d,
                                    slice(nt * 512, (nt + 1) * 512))
            for nt in range(NT):
                t0 = nt * 512
                xb = xn1t[nt]
                sch.wait("tensor", "dX", ld_nt[nt])
                bank0 = nt % 2
                sch.bank_acquire("tensor", f"pb{bank0}")
                for h in range(HC):
                    sch.emit("tensor", lambda t, h=h, xb=xb, b=bank0: t.matmul(
                        pb[b][:, 0:512], Bw16.ap()[:, h * SD:h * SD + 128],
                        xb.ap()[:, h * 512:(h + 1) * 512],
                        start=(h == 0), stop=False))
                for h in range(HC):
                    vt = sch.emit("tensor", lambda t, h=h, xb=xb, b=bank0: t.matmul(
                        pb[b][:, 0:512],
                        Wug16.ap()[:, h * (SD + D):h * (SD + D) + 128],
                        xb.ap()[:, h * 512:(h + 1) * 512],
                        start=False, stop=(h == HC - 1)),
                        inc="tS" if h == HC - 1 else None)
                sch.wait("scalar", "tS", vt)
                vs = sch.emit("scalar", lambda s, t0=t0, b=bank0: s.activation(
                    drive.ap()[:, t0:t0 + 512], pb[b][:, 0:512], ACT.Identity,
                    bias=vbug[:, 0:1]), inc="sS")
                sch.bank_release(f"pb{bank0}", "sS", vs)

            def p1a_gates():
                st_done = {}
                gi = 0
                for nt in range(NT):
                    t0 = nt * 512
                    xb = xn1t[nt]
                    for dc in range(1, HC + 1):
                        bank = 2 + (gi % 2)
                        sch.bank_acquire("tensor", f"pb{bank}")
                        for h in range(HC):
                            vt = sch.emit("tensor",
                                          lambda t, h=h, dc=dc, b=bank, xb=xb:
                                          t.matmul(
                                              pb[b][:, 0:512],
                                              Wug16.ap()[:, h * (SD + D) + dc * 128:
                                                         h * (SD + D) + dc * 128
                                                         + 128],
                                              xb.ap()[:, h * 512:(h + 1) * 512],
                                              start=(h == 0), stop=(h == HC - 1)),
                                          inc="tS" if h == HC - 1 else None)
                        stb = gtst[gi % 4]
                        sch.wait("scalar", "tS", vt)
                        if gi >= 4:
                            sch.wait("scalar", "dO", st_done[gi - 4])
                        vs = sch.emit("scalar", lambda s, b=bank, stb=stb, dc=dc:
                                      s.activation(stb[:], pb[b][:, 0:512],
                                                   ACT.Sigmoid,
                                                   bias=vbug[:, dc:dc + 1]),
                                      inc="sS")
                        sch.bank_release(f"pb{bank}", "sS", vs)
                        sch.wait("sync", "sS", vs)
                        st_done[gi] = sch.emit("sync",
                                               lambda s, dc=dc, t0=t0, stb=stb:
                                               s.dma_start(
                                                   out=gt_d[(dc - 1) * 128:dc * 128,
                                                            t0:t0 + 512],
                                                   in_=stb[:]),
                                               inc="dO", n=16)
                        gi += 1

            sch.emit("vector", lambda v: v.memset(pw64[:], 1.0))
            sch.vsync()
            sch.emit("vector", lambda v: v.tensor_scalar(
                pw64[:], pw64[:], vA[:, 0:1], None, ALU.mult))
            sch.vsync()
            for l in range(11):
                sch.emit("vector", lambda v, l=l: v.tensor_copy(
                    apows[:, l:l + 1], pw64[:, 0:1]))
                sch.vsync()
                sch.emit("vector", lambda v: v.tensor_tensor(
                    pw64[:], pw64[:], pw64[:], ALU.mult))
                sch.vsync()
            sch.wait("vector", "sS")
            bufs = [drive, states]
            for l in range(11):
                dd = 1 << l
                srcb = bufs[l % 2]
                dstb = bufs[1 - l % 2]
                sch.emit("vector", lambda v, dd=dd, s=srcb, d=dstb: v.tensor_copy(
                    d.ap()[:, 0:dd], s.ap()[:, 0:dd]))
                sch.emit("vector", lambda v, dd=dd, s=srcb, d=dstb, l=l:
                         v.scalar_tensor_tensor(
                             d.ap()[:, dd:T], s.ap()[:, 0:T - dd],
                             apows[:, l:l + 1], s.ap()[:, dd:T],
                             ALU.mult, ALU.add))
                sch.vsync()
            v_scan = sch.emit("vector", lambda v: v.tensor_copy(
                st16[:], states[:]), inc="vS")

            p1a_gates()

            if "tap_drive" in taps:
                sch.wait("sync", "vS")
                sch.emit("sync", lambda s: s.dma_start(
                    out=taps["tap_drive"][:, :], in_=drive[:]), inc="dO", n=16)
            if "tap_states" in taps:
                sch.wait("sync", "vS")
                sch.emit("sync", lambda s: s.dma_start(
                    out=taps["tap_states"][:, :], in_=states[:]), inc="dO", n=16)
            if "tap_gt" in taps:
                sch.wait("sync", "dO")
                sch.emit("sync", lambda s: s.dma_start(
                    out=taps["tap_gt"][:, :], in_=gt_d[:, :]), inc="dO", n=16)

            p1a_end = {k: sch.cnt[k] for k in ("tS", "sS", "dO")}

        spanA.close()

        with ExitStack() as p1b:
            e1 = p1b.enter_context
            xn1t = [e1(nc.sbuf_tensor(f"xn1tB{i}", [128, HC * 256], BF16))
                    for i in range(2)]
            xt = e1(nc.sbuf_tensor("xt", [128, HC * 256], F32))
            gtt = e1(nc.sbuf_tensor("gtt", [128, HC * 256], F32))
            q16 = e1(nc.sbuf_tensor("q16", [128, HC * 256], BF16))
            kring = [e1(nc.sbuf_tensor(f"kring{i}", [128, HC * 256], BF16))
                     for i in range(2)]
            vring = [[e1(nc.sbuf_tensor(f"vring{i}{j}", [128, D], BF16))
                      for j in range(2)] for i in range(2)]
            Wqkv16 = e1(nc.sbuf_tensor("Wqkv16", [128, HC * 3 * D], BF16))
            WO16 = e1(nc.sbuf_tensor("WO16", [128, HC * D], BF16))
            Cw16 = e1(nc.sbuf_tensor("Cw16", [SD, D], BF16))
            o16 = e1(nc.sbuf_tensor("o16", [128, HC * 256], BF16))
            E16 = e1(nc.sbuf_tensor("E16", [128, 5 * 384], BF16))
            P16 = e1(nc.sbuf_tensor("P16", [128, 5 * 384], BF16))
            PT16 = e1(nc.sbuf_tensor("PT16", [128, 5 * 384], BF16))
            nmax = e1(nc.sbuf_tensor("nmax", [128, 5], F32))
            denom = e1(nc.sbuf_tensor("denom", [128, 10], F32))
            a_c = e1(nc.sbuf_tensor("a_c", [128, 256], F32))
            y_c = e1(nc.sbuf_tensor("y_c", [128, 256], F32))
            x2t = e1(nc.sbuf_tensor("x2t", [128, HC * 256], F32))
            xsqt = e1(nc.sbuf_tensor("xsqt", [128, 256], F32))
            br0t = e1(nc.sbuf_tensor("br0t", [128, 256], F32))
            bc0t = e1(nc.sbuf_tensor("bc0t", [128, 256], F32))
            tmpArow = xt

            for k_, v_ in p1a_end.items():
                sch.wait("sync", k_, v_)
            wld(*chunked(Wqkv16, Wqkv, HC))
            wld(*chunked(WO16, WO, HC))
            w_p1b = wld(Cw16.ap(), Cw[:, :])
            sch.wait("tensor", "dW", w_p1b)

            n_tiles = T // 256
            stats_pend = [None]
            stats_done = [None]

            def ln2_stats(si, vv_x2s):
                tksl_s = slice(si * 256, (si + 1) * 256)
                sch.wait("sync", "vS", vv_x2s)
                st_cols(x2t, x2_d, tksl_s)
                v_st = sch.cnt["dO"]
                sch.bank_acquire("tensor", "pb6")
                sch.bank_acquire("tensor", "pb7")
                vt = None
                vs = None
                for c in range(HC):
                    sch.wait("scalar", "vS", vv_x2s)
                    if vt is not None:
                        sch.wait("scalar", "tS", vt)
                    vs = sch.emit("scalar", lambda s, c=c: s.activation(
                        xsqt[:], x2t.ap()[:, c * 256:(c + 1) * 256], ACT.Square),
                        inc="sS")
                    sch.wait("tensor", "vS", vv_x2s)
                    sch.emit("tensor", lambda t, c=c: t.matmul(
                        pb[6][0:1, 0:256], ones_c[:, 0:1],
                        x2t.ap()[:, c * 256:(c + 1) * 256],
                        start=(c == 0), stop=(c == HC - 1)))
                    sch.wait("tensor", "sS", vs)
                    vt = sch.emit("tensor", lambda t, c=c: t.matmul(
                        pb[7][0:1, 0:256], ones_c[:, 0:1], xsqt[:],
                        start=(c == 0), stop=(c == HC - 1)), inc="tS")
                sch.wait("vector", "tS", vt)
                sch.emit("vector", lambda v, si=si: v.tensor_scalar(
                    rows["mu2"][0:1, si * 256:(si + 1) * 256], pb[6][0:1, 0:256],
                    1.0 / D, None, ALU.mult))
                vv = sch.emit("vector", lambda v, si=si: v.tensor_scalar(
                    rows["c2"][0:1, si * 256:(si + 1) * 256], pb[7][0:1, 0:256],
                    1.0 / D, None, ALU.mult), inc="vS")
                sch.bank_release("pb6", "vS", vv)
                sch.bank_release("pb7", "vS", vv)
                stats_done[0] = (vt, vs, v_st)

            ld_t = {0: ld_cols(xn1t[0], xn1_d, slice(0, 256))}
            for i in range(n_tiles):
                tok0 = i * 256
                tksl = slice(tok0, tok0 + 256)
                xb = xn1t[i % 2]
                sch.wait("sync", "vS")
                sch.wait("sync", "sS")
                sch.wait("sync", "gS")
                if i + 1 < n_tiles:
                    ld_t[i + 1] = ld_cols(xn1t[(i + 1) % 2], xn1_d,
                                          slice(tok0 + 256, tok0 + 512))
                ld_cols(xt, X, tksl)
                vXg = ld_cols(gtt, gt_d, tksl)

                bcast2(rows["rstd0"], rows["c0"], br0t, bc0t, tksl)

                sch.wait("vector", "dX", vXg)
                sch.wait("vector", "dO")
                sch.emit("vector", lambda v: v.tensor_tensor(
                    xt.ap().rearrange("p (c t) -> p c t", c=HC),
                    xt.ap().rearrange("p (c t) -> p c t", c=HC),
                    br0t.ap().unsqueeze(1).broadcast_to([128, HC, 256]), ALU.mult))
                vv_xn = sch.emit("vector", lambda v: v.tensor_tensor(
                    xt.ap().rearrange("p (c t) -> p c t", c=HC),
                    xt.ap().rearrange("p (c t) -> p c t", c=HC),
                    bc0t.ap().unsqueeze(1).broadcast_to([128, HC, 256]), ALU.add),
                    inc="vS")

                sch.wait("tensor", "dX", ld_t[i])
                ksl = kring[i % 2]
                for d in range(16):
                    bank = d % 2
                    sch.bank_acquire("tensor", f"pb{bank}")
                    for h in range(HC):
                        vt = sch.emit("tensor", lambda t, d=d, h=h, b=bank, xb=xb:
                                      t.matmul(
                                          pb[b][:, 0:256],
                                          Wqkv16.ap()[:, h * 3 * D + d * 128:
                                                      h * 3 * D + d * 128 + 128],
                                          xb.ap()[:, h * 256:(h + 1) * 256],
                                          start=(h == 0), stop=(h == HC - 1)),
                                      inc="tS" if h == HC - 1 else None)
                    dst = q16 if d < HC else ksl
                    dc = d % HC
                    sch.wait("scalar", "tS", vt)
                    if d < HC:
                        vv = sch.emit("scalar", lambda s, dc=dc, b=bank:
                                      s.activation(
                                          q16.ap()[:, dc * 256:(dc + 1) * 256],
                                          pb[b][:, 0:256], ACT.Identity,
                                          scale=0.125), inc="sS")
                    else:
                        vv = sch.emit("scalar", lambda s, dst=dst, dc=dc, b=bank:
                                      s.copy(
                                          dst.ap()[:, dc * 256:(dc + 1) * 256],
                                          pb[b][:, 0:256]), inc="sS")
                    sch.bank_release(f"pb{bank}", "sS", vv)
                for ts_ in range(2):
                    for half in range(2):
                        bank = 2 + half
                        sch.bank_acquire("tensor", f"pb{bank}")
                        for h in range(HC):
                            vt = sch.emit("tensor",
                                          lambda t, ts_=ts_, half=half, h=h, b=bank,
                                          xb=xb: t.matmul(
                                              pb[b][:, 0:512],
                                              xb.ap()[:, h * 256 + ts_ * 128:
                                                      h * 256 + ts_ * 128 + 128],
                                              Wqkv16.ap()[:, h * 3 * D + 2 * D +
                                                          half * 512:
                                                          h * 3 * D + 2 * D +
                                                          half * 512 + 512],
                                              start=(h == 0), stop=(h == HC - 1)),
                                          inc="tS" if h == HC - 1 else None)
                        vtm = vring[i % 2][ts_]
                        sch.wait("scalar", "tS", vt)
                        vv = sch.emit("scalar", lambda s, vtm=vtm, half=half, b=bank:
                                      s.copy(
                                          vtm.ap()[:, half * 512:(half + 1) * 512],
                                          pb[b][:, 0:512]), inc="sS")
                        sch.bank_release(f"pb{bank}", "sS", vv)

                if stats_pend[0] is not None:
                    ln2_stats(*stats_pend[0])
                    stats_pend[0] = None

                NK_IT = NH * 2
                DEPTH = 4
                NSL = 5

                rec_done = {}
                max_done = {}
                exp_done = {}

                def it_meta(k):
                    h = k // 2
                    qc = k % 2
                    p0_ = 64 * (h % 2)
                    hd = h // 2
                    kstart = tok0 + qc * 128 - 256
                    kchunks = [kstart + j * 128 for j in range(3)
                               if kstart + j * 128 >= 0]
                    w = 128 * len(kchunks)
                    mask = {384: mint, 256: mq1, 128: mq0}[w]
                    return h, qc, p0_, hd, kchunks, w, mask

                def attn_front(k):
                    h, qc, p0_, hd, kchunks, w, mask = it_meta(k)
                    s = k % NSL
                    sbk = (0, 1, 6)[k % 3]
                    sch.bank_acquire("tensor", f"pb{sbk}")
                    for ji, abs0 in enumerate(kchunks):
                        slot = (abs0 // 256) % 2
                        off = abs0 % 256
                        sch.emit("tensor", lambda t, ji=ji, slot=slot, off=off,
                                 sbk=sbk, hd=hd, p0_=p0_, qc=qc: t.matmul(
                                     pb[sbk][:, ji * 128:(ji + 1) * 128],
                                     q16.ap()[p0_:p0_ + 64,
                                              hd * 256 + qc * 128:
                                              hd * 256 + qc * 128 + 128],
                                     kring[slot].ap()[p0_:p0_ + 64,
                                                      hd * 256 + off:
                                                      hd * 256 + off + 128],
                                     start=(ji == 0), stop=False))
                    vt = sch.emit("tensor", lambda t, sbk=sbk, w=w, mask=mask:
                                  t.matmul(pb[sbk][:, 0:w], id16[:],
                                           mask.ap()[:, 0:w],
                                           start=False, stop=True), inc="tS")
                    sch.wait("vector", "tS", vt)
                    max_done[k] = sch.emit("vector", lambda v, sbk=sbk, w=w, s=s:
                                           v.reduce_max(nmax[:, s:s + 1],
                                                        pb[sbk][:, 0:w],
                                                        axis=AX.X, negate=True),
                                           inc="vS")

                def attn_exp(k):
                    h, qc, p0_, hd, kchunks, w, mask = it_meta(k)
                    s = k % NSL
                    sbk = (0, 1, 6)[k % 3]
                    sch.wait("scalar", "vS", max_done[k])
                    vs = sch.emit("scalar", lambda sc, sbk=sbk, w=w, s=s:
                                  sc.activation(E16.ap()[:, s * 384:s * 384 + w],
                                                pb[sbk][:, 0:w], ACT.Exp,
                                                bias=nmax[:, s:s + 1],
                                                accum_out=denom[:, 2 * s:2 * s + 1]),
                                  inc="sS")
                    sch.bank_release(f"pb{sbk}", "sS", vs)
                    exp_done[k] = vs

                def attn_rec(k):
                    s = k % NSL
                    sch.wait("vector", "sS", exp_done[k])
                    rec_done[k] = sch.emit("vector", lambda v, s=s: v.reciprocal(
                        denom[:, 2 * s + 1:2 * s + 2], denom[:, 2 * s:2 * s + 1]),
                        inc="vS")

                def attn_back(k):
                    h, qc, p0_, hd, kchunks, w, mask = it_meta(k)
                    s = k % NSL
                    nk = len(kchunks)
                    sch.wait("scalar", "vS", rec_done[k])
                    vv = sch.emit("scalar", lambda sc, w=w, s=s: sc.mul(
                        P16.ap()[:, s * 384:s * 384 + w],
                        E16.ap()[:, s * 384:s * 384 + w],
                        denom[:, 2 * s + 1:2 * s + 2]), inc="sS")
                    ptb = 2 + (k % 2)
                    sch.bank_acquire("tensor", f"pb{ptb}")
                    sch.wait("tensor", "sS", vv)
                    for j in range(nk):
                        vt = sch.emit("tensor", lambda t, j=j, ptb=ptb, s=s:
                                      t.matmul(pb[ptb][:, j * 128:(j + 1) * 128],
                                               P16.ap()[:, s * 384 + j * 128:
                                                        s * 384 + (j + 1) * 128],
                                               id16[:], start=True, stop=True),
                                      inc="tS" if j == nk - 1 else None)
                    sch.wait("scalar", "tS", vt)
                    vv = sch.emit("scalar", lambda sc, ptb=ptb, w=w, s=s:
                                  sc.copy(PT16.ap()[:, s * 384:s * 384 + w],
                                          pb[ptb][:, 0:w]), inc="sS")
                    sch.bank_release(f"pb{ptb}", "sS", vv)
                    pvb = 4 + (hd % 2)
                    if k % 4 == 0:
                        sch.bank_acquire("tensor", f"pb{pvb}")
                    sch.wait("tensor", "sS", vv)
                    for j, abs0 in enumerate(kchunks):
                        slot = (abs0 // 256) % 2
                        ts_ = (abs0 % 256) // 128
                        last_of_pair = (j == nk - 1 and k % 4 == 3)
                        vt = sch.emit("tensor", lambda t, j=j, slot=slot, ts_=ts_,
                                      h=h, qc=qc, pvb=pvb, p0_=p0_, nk=nk, s=s:
                                      t.matmul(
                                          pb[pvb][p0_:p0_ + 64,
                                                  qc * 128:qc * 128 + 128],
                                          vring[slot][ts_].ap()[:, 64 * h:
                                                                64 * h + 64],
                                          PT16.ap()[:, s * 384 + j * 128:
                                                    s * 384 + (j + 1) * 128],
                                          start=(j == 0), stop=(j == nk - 1),
                                          tile_position=(0, p0_)),
                                      inc="tS" if last_of_pair else None)
                    if k % 4 == 3:
                        sch.wait("vector", "tS", vt)
                        vv = sch.emit("vector", lambda v, hd=hd, pvb=pvb:
                                      v.tensor_copy(
                                          o16.ap()[:, hd * 256:(hd + 1) * 256],
                                          pb[pvb][:, 0:256]), inc="vS")
                        sch.bank_release(f"pb{pvb}", "vS", vv)

                for step in range(NK_IT + DEPTH):
                    if step >= DEPTH:
                        attn_back(step - DEPTH)
                    if 1 <= step < NK_IT + 1:
                        attn_exp(step - 1)
                    if 2 <= step < NK_IT + 2:
                        attn_rec(step - 2)
                    if step < NK_IT:
                        attn_front(step)
                vv_o16 = sch.cnt["vS"]

                if i == 0:
                    sch.wait("tensor", "vS", v_scan)
                sch.wait("tensor", "vS", vv_o16)
                sch.wait("gpsimd", "dX", vXg)
                sch.wait("gpsimd", "vS", vv_xn)
                if stats_done[0] is not None:
                    svt, svs, sst = stats_done[0]
                    sch.wait("vector", "tS", svt)
                    sch.wait("vector", "sS", svs)
                    sch.wait("vector", "dO", sst)
                    stats_done[0] = None
                vv_prev_comb = None
                for d in range(HC):
                    bko = 6 + (d % 2)
                    sch.bank_acquire("tensor", f"pb{bko}")
                    for h in range(HC):
                        sch.emit("tensor", lambda t, d=d, h=h, b=bko: t.matmul(
                            pb[b][:, 0:256],
                            WO16.ap()[:, h * D + d * 128:h * D + d * 128 + 128],
                            o16.ap()[:, h * 256:(h + 1) * 256],
                            start=(h == 0), stop=(h == HC - 1)))
                    bky = 4 + (d % 2)
                    sch.bank_acquire("tensor", f"pb{bky}")
                    vt = sch.emit("tensor", lambda t, d=d, b=bky, tok0=tok0:
                                  t.matmul(pb[b][:, 0:256],
                                           Cw16.ap()[:, d * 128:(d + 1) * 128],
                                           st16.ap()[:, tok0:tok0 + 256],
                                           start=True, stop=True), inc="tS")
                    sch.wait("scalar", "tS", vt)
                    sch.wait("scalar", "gS")
                    if vv_prev_comb is not None:
                        sch.wait("scalar", "vS", vv_prev_comb)
                    vs = sch.emit("scalar", lambda s, d=d, b=bko: s.activation(
                        a_c[:], pb[b][:, 0:256], ACT.Identity, bias=vbO[:, d:d + 1]),
                        inc="sS")
                    sch.bank_release(f"pb{bko}", "sS", vs)
                    vs = sch.emit("scalar", lambda s, b=bky: s.copy(
                        y_c[:], pb[b][:, 0:256]), inc="sS")
                    sch.bank_release(f"pb{bky}", "sS", vs)
                    if "tap_attn" in taps:
                        sch.wait("sync", "sS", vs)
                        sch.emit("sync", lambda s, d=d, tksl=tksl: s.dma_start(
                            out=taps["tap_attn"][d * 128:(d + 1) * 128, tksl],
                            in_=a_c[:]), inc="dO", n=16)
                        sch.emit("sync", lambda s, d=d, tksl=tksl: s.dma_start(
                            out=taps["tap_yssm"][d * 128:(d + 1) * 128, tksl],
                            in_=y_c[:]), inc="dO", n=16)
                        sch.wait("gpsimd", "dO")
                    sch.wait("gpsimd", "sS", vs)
                    sch.emit("gpsimd", lambda g: g.tensor_tensor(
                        a_c[:], a_c[:], y_c[:], ALU.subtract))
                    vg = sch.emit("gpsimd", lambda g, d=d: g.tensor_tensor(
                        y_c[:], xt.ap()[:, d * 256:(d + 1) * 256], y_c[:], ALU.add),
                        inc="gS")
                    sch.wait("vector", "gS", vg)
                    sch.emit("vector", lambda v, d=d: v.tensor_tensor(
                        a_c[:], gtt.ap()[:, d * 256:(d + 1) * 256], a_c[:],
                        ALU.mult))
                    vv = sch.emit("vector", lambda v, d=d: v.tensor_tensor(
                        x2t.ap()[:, d * 256:(d + 1) * 256], y_c[:], a_c[:],
                        ALU.add), inc="vS")
                    vv_prev_comb = vv
                vv_x2 = sch.cnt["vS"]
                stats_pend[0] = (i, vv_x2)

            ln2_stats(*stats_pend[0])
            stats_pend[0] = None

            sch.emit("vector", lambda v: v.tensor_tensor(
                tmpArow[0:1, :], rows["mu2"][0:1, :], rows["mu2"][0:1, :], ALU.mult))
            sch.emit("vector", lambda v: v.tensor_tensor(
                tmpArow[0:1, :], rows["c2"][0:1, :], tmpArow[0:1, :], ALU.subtract))
            vv = sch.emit("vector", lambda v: v.tensor_scalar(
                tmpArow[0:1, :], tmpArow[0:1, :], EPS, None, ALU.add), inc="vS")
            sch.wait("scalar", "vS", vv)
            sch.emit("scalar", lambda s: s.activation(
                rows["rstd2"][0:1, :], tmpArow[0:1, :], ACT.Ln))
            vs = sch.emit("scalar", lambda s: s.activation(
                tmpArow[0:1, :], rows["rstd2"][0:1, :], ACT.Exp, scale=-0.5),
                inc="sS")
            sch.wait("vector", "sS", vs)
            sch.emit("vector", lambda v: v.tensor_copy(
                rows["rstd2"][0:1, :], tmpArow[0:1, :]))
            sch.emit("vector", lambda v: v.scalar_tensor_tensor(
                rows["c2"][0:1, :], rows["mu2"][0:1, :], -1.0, rows["rstd2"][0:1, :],
                ALU.mult, ALU.mult))

            if "tap_x2" in taps:
                sch.wait("sync", "dO")
                sch.emit("sync", lambda s: s.dma_start(
                    out=taps["tap_x2"][:, :], in_=x2_d[:, :]), inc="dO", n=16)

            sch.barrier()

        spanB.close()

        with ExitStack() as p2:
            e2 = p2.enter_context
            W116 = e2(nc.sbuf_tensor("W116", [128, HC * (DFF // 2)], BF16))
            W216 = e2(nc.sbuf_tensor("W216", [128, (DFF // 256) * D], BF16))
            x2b = [e2(nc.sbuf_tensor(f"x2b{j}", [128, HC * 256], F32))
                   for j in range(3)]
            tmp2 = e2(nc.sbuf_tensor("tmp2", [128, HC * 256], F32))
            h16 = [e2(nc.sbuf_tensor(f"h16{j}", [128, HC * 256], BF16))
                   for j in range(2)]
            hid16 = [e2(nc.sbuf_tensor(f"hid16{j}", [128, (DFF // 256) * 256], BF16))
                     for j in range(2)]
            s_t = e2(nc.sbuf_tensor("s_t", [128, HC * 256], F32))
            sprev = [e2(nc.sbuf_tensor(f"sprev{j}", [128, HC * 256], F32))
                     for j in range(3)]
            br2 = [e2(nc.sbuf_tensor(f"br2{j}", [128, 256], F32)) for j in range(2)]
            bc2 = [e2(nc.sbuf_tensor(f"bc2{j}", [128, 256], F32)) for j in range(2)]

            DH2 = DFF // 2
            NDC = DH2 // 128
            NTt = T // 256
            for half in range(2):
                sch.wait("gpsimd", "tS")
                wld(W116.ap().rearrange("p (c j) -> p c j", c=HC),
                    W1[:, half * DH2:(half + 1) * DH2].rearrange(
                        "(c p) j -> p c j", p=128))
                w_p2 = wld(W216.ap().rearrange("p (c j) -> p c j", c=NDC),
                           W2[half * DH2:(half + 1) * DH2, :].rearrange(
                               "(c p) j -> p c j", p=128))

                end_V = {}
                end_H = {}
                end_W1 = {}
                end_G = {}
                end_S = {}
                end_ST = {}

                def LD(i):
                    j = i % 2
                    j3 = i % 3
                    tksl = slice(i * 256, (i + 1) * 256)
                    if i - 3 in end_S:
                        sch.wait("sync", "vS", end_S[i - 3])
                    if i - 2 in end_H:
                        sch.wait("sync", "sS", end_H[i - 2])
                    ld_cols(x2b[j3], x2_d, tksl)
                    vX = sch.cnt["dX"]
                    if half == 1:
                        vX = ld_cols(sprev[j3], s_d, tksl)
                    bcast2(rows["rstd2"], rows["c2"], br2[j], bc2[j], tksl)
                    return vX

                def V(i, vX):
                    j = i % 2
                    j3 = i % 3
                    sch.wait("vector", "dX", vX)
                    if i - 1 in end_H:
                        sch.wait("vector", "sS", end_H[i - 1])
                    sch.emit("vector", lambda v, j=j, j3=j3: v.tensor_tensor(
                        tmp2.ap().rearrange("p (c t) -> p c t", c=HC),
                        x2b[j3].ap().rearrange("p (c t) -> p c t", c=HC),
                        br2[j].ap().unsqueeze(1).broadcast_to([128, HC, 256]),
                        ALU.mult))
                    vv = sch.emit("vector", lambda v, j=j: v.tensor_tensor(
                        tmp2.ap().rearrange("p (c t) -> p c t", c=HC),
                        tmp2.ap().rearrange("p (c t) -> p c t", c=HC),
                        bc2[j].ap().unsqueeze(1).broadcast_to([128, HC, 256]),
                        ALU.add), inc="vS")
                    end_V[i] = vv

                def H(i):
                    j = i % 2
                    sch.wait("scalar", "vS", end_V[i])
                    if i - 2 in end_W1:
                        sch.wait("scalar", "tS", end_W1[i - 2])
                    for c in range(HC):
                        sch.emit("scalar", lambda s, c=c, j=j: s.activation(
                            h16[j].ap()[:, c * 256:(c + 1) * 256],
                            tmp2.ap()[:, c * 256:(c + 1) * 256], ACT.Identity,
                            bias=vb2[:, c:c + 1], scale=vg2[:, c:c + 1]), inc="sS")
                    end_H[i] = sch.cnt["sS"]

                def W1G(i):
                    j = i % 2
                    sch.wait("tensor", "sS", end_H[i])
                    if i == 0:
                        sch.wait("tensor", "dW", w_p2)
                    for d in range(NDC):
                        bank = d % 2
                        sch.bank_acquire("tensor", f"pb{bank}")
                        for h in range(HC):
                            vt = sch.emit("tensor", lambda t, d=d, h=h, b=bank, j=j:
                                          t.matmul(
                                              pb[b][:, 0:256],
                                              W116.ap()[:, h * DH2 + d * 128:
                                                        h * DH2 + d * 128 + 128],
                                              h16[j].ap()[:, h * 256:(h + 1) * 256],
                                              start=(h == 0), stop=(h == HC - 1)),
                                          inc="tS" if h == HC - 1 else None)
                        sch.wait("scalar", "tS", vt)
                        vs = sch.emit("scalar", lambda s, d=d, b=bank, half=half,
                                      j=j: s.activation(
                                          hid16[j].ap()[:, d * 256:(d + 1) * 256],
                                          pb[b][:, 0:256], ACT.Gelu,
                                          bias=vb1m[:, half * NDC + d:
                                                    half * NDC + d + 1]), inc="sS")
                        sch.bank_release(f"pb{bank}", "sS", vs)
                    end_W1[i] = sch.cnt["tS"]
                    end_G[i] = sch.cnt["sS"]

                def W2S(i):
                    j = i % 2
                    j3 = i % 3
                    sch.wait("tensor", "sS", end_G[i])
                    if i - 1 in end_ST:
                        sch.wait("vector", "dO", end_ST[i - 1])
                    for d in range(HC):
                        bank = 2 + (d % 2)
                        sch.bank_acquire("tensor", f"pb{bank}")
                        for h in range(NDC):
                            vt = sch.emit("tensor", lambda t, d=d, h=h, b=bank, j=j:
                                          t.matmul(
                                              pb[b][:, 0:256],
                                              W216.ap()[:, h * D + d * 128:
                                                        h * D + d * 128 + 128],
                                              hid16[j].ap()[:, h * 256:
                                                            (h + 1) * 256],
                                              start=(h == 0), stop=(h == NDC - 1)),
                                          inc="tS" if h == NDC - 1 else None)
                        sch.wait("vector", "tS", vt)
                        if half == 0:
                            vv = sch.emit("vector", lambda v, d=d, b=bank, j3=j3:
                                          v.tensor_tensor(
                                              s_t.ap()[:, d * 256:(d + 1) * 256],
                                              pb[b][:, 0:256],
                                              x2b[j3].ap()[:, d * 256:
                                                           (d + 1) * 256],
                                              ALU.add), inc="vS")
                        else:
                            vv = sch.emit("vector", lambda v, d=d, b=bank, j3=j3:
                                          v.scalar_tensor_tensor(
                                              s_t.ap()[:, d * 256:(d + 1) * 256],
                                              pb[b][:, 0:256], vb2m[:, d:d + 1],
                                              sprev[j3].ap()[:, d * 256:
                                                             (d + 1) * 256],
                                              ALU.add, ALU.add), inc="vS")
                        sch.bank_release(f"pb{bank}", "vS", vv)
                    end_S[i] = sch.cnt["vS"]

                def ST(i):
                    tksl = slice(i * 256, (i + 1) * 256)
                    sch.wait("sync", "vS", end_S[i])
                    st_cols(s_t, s_d, tksl)
                    end_ST[i] = sch.cnt["dO"]

                vX0 = LD(0)
                V(0, vX0)
                H(0)
                for i in range(NTt):
                    W1G(i)
                    if i + 1 < NTt:
                        vX = LD(i + 1)
                        V(i + 1, vX)
                        H(i + 1)
                    if i >= 1:
                        W2S(i - 1)
                        ST(i - 1)
                W2S(NTt - 1)
                ST(NTt - 1)

            if "tap_s" in taps:
                sch.wait("sync", "dO")
                sch.emit("sync", lambda s: s.dma_start(
                    out=taps["tap_s"][:, :], in_=s_d[:, :]), inc="dO", n=16)

            sch.barrier()

        with ExitStack() as p3:
            e3 = p3.enter_context
            TH = T // 2
            gateT16 = e3(nc.sbuf_tensor("gateT16", [128, HC * NE], BF16))
            cw_fm = e3(nc.sbuf_tensor("cw_fm", [NE, T], F32))

            w_gate = wld(*chunked(gateT16, gateT, HC))

            with ExitStack() as pr:
                er = pr.enter_context
                NTL = T // 128
                sc_fm = er(nc.sbuf_tensor("sc_fm", [NE, T], F32))
                sfc_fm = er(nc.sbuf_tensor("sfc_fm", [NE, T], F32))
                sc_tm = er(nc.sbuf_tensor("sc_tm", [128, NTL * NE], F32))
                sfc_tm = er(nc.sbuf_tensor("sfc_tm", [128, NTL * NE], F32))
                gsc = er(nc.sbuf_tensor("gsc", [128, NTL * 4], F32))
                gth = er(nc.sbuf_tensor("gth", [128, NTL], F32))
                emask = er(nc.sbuf_tensor("emask", [128, NTL * NE],
                                          mybir.dt.uint8))
                neg30 = er(nc.sbuf_tensor("neg30", [128, NTL * NE], F32))
                maskd = er(nc.sbuf_tensor("maskd", [128, NTL * NE], F32))
                sel = er(nc.sbuf_tensor("sel", [128, NTL * NE], F32))
                eqm = er(nc.sbuf_tensor("eqm", [128, NTL * NE], F32))
                mrow = er(nc.sbuf_tensor("mrow", [128, NTL], F32))
                sumw = er(nc.sbuf_tensor("sumw", [128, 2 * NTL], F32))
                cw_tm = er(nc.sbuf_tensor("cw_tm", [128, NTL * NE], F32))

                xng = er(nc.sbuf_tensor("xng", [128, HC * 512], BF16))
                sch.wait("tensor", "dW", w_gate)
                for gh in range(NT):
                    sch.wait("sync", "tS")
                    v_xn = ld_cols(xng, xn16_d, slice(gh * 512, (gh + 1) * 512))
                    sch.wait("tensor", "dX", v_xn)
                    for nt in range(1):
                        bank = gh % 2
                        sch.bank_acquire("tensor", f"pb{bank}")
                        for h in range(HC):
                            vt = sch.emit("tensor",
                                          lambda t, h=h, b=bank: t.matmul(
                                              pb[b][0:NE, 0:512],
                                              gateT16.ap()[:, h * NE:(h + 1) * NE],
                                              xng.ap()[:, h * 512:(h + 1) * 512],
                                              start=(h == 0), stop=(h == HC - 1)),
                                          inc="tS" if h == HC - 1 else None)
                        sch.wait("scalar", "tS", vt)
                        vs = sch.emit("scalar", lambda s, b=bank, gh=gh:
                                      s.activation(
                                          sc_fm.ap()[:, gh * 512:(gh + 1) * 512],
                                          pb[b][0:NE, 0:512], ACT.Sigmoid),
                                      inc="sS")
                        sch.bank_release(f"pb{bank}", "sS", vs)
                v_sc = sch.cnt["sS"]
                sch.wait("vector", "sS", v_sc)
                vv = sch.emit("vector", lambda v: v.tensor_scalar(
                    sfc_fm[:], sc_fm[:], vgb[:, 0:1], None, ALU.add), inc="vS")

                sch.wait("tensor", "vS", vv)
                for it in range(NTL):
                    bank = it % 2
                    sch.bank_acquire("tensor", f"pb{bank}")
                    sch.emit("tensor", lambda t, it=it, b=bank: t.matmul(
                        pb[b][:, 0:NE], sc_fm.ap()[:, it * 128:(it + 1) * 128],
                        id32[0:NE, 0:NE], start=True, stop=True))
                    vt = sch.emit("tensor", lambda t, it=it, b=bank: t.matmul(
                        pb[b][:, NE:2 * NE], sfc_fm.ap()[:, it * 128:(it + 1) * 128],
                        id32[0:NE, 0:NE], start=True, stop=True), inc="tS")
                    sch.wait("vector", "tS", vt)
                    sch.emit("vector", lambda v, it=it, b=bank: v.tensor_copy(
                        sc_tm.ap()[:, it * NE:(it + 1) * NE], pb[b][:, 0:NE]))
                    vv = sch.emit("vector", lambda v, it=it, b=bank: v.tensor_copy(
                        sfc_tm.ap()[:, it * NE:(it + 1) * NE], pb[b][:, NE:2 * NE]),
                        inc="vS")
                    sch.bank_release(f"pb{bank}", "vS", vv)

                Sf = sfc_tm.ap()
                G64 = gsc.ap()
                qtmp = eqm.ap()[:, 128:192]
                tmin = eqm.ap()[:, 0:128]
                gm64 = neg30.ap()[:, 0:64]
                M3 = maskd.ap().rearrange("p (tl e) -> p tl e", tl=NTL)

                firstpair = True
                for aa in range(4):
                    for bb in range(aa + 1, 4):
                        sch.emit("vector", lambda v, aa=aa, bb=bb: v.tensor_tensor(
                            qtmp, Sf[:, aa::4], Sf[:, bb::4], ALU.add))
                        if firstpair:
                            sch.emit("vector", lambda v: v.tensor_copy(G64, qtmp))
                            firstpair = False
                        else:
                            sch.emit("vector", lambda v: v.tensor_tensor(
                                G64, G64, qtmp, ALU.max))
                sch.vsync()
                sch.emit("vector", lambda v: v.memset(tmin, NEG))
                sch.vsync()
                kk = 0
                for aa in range(4):
                    for bb in range(aa + 1, 4):
                        sch.emit("vector", lambda v, aa=aa, bb=bb, kk=kk:
                                 v.tensor_tensor(tmin[:, kk::8], G64[:, aa::4],
                                                 G64[:, bb::4], ALU.min))
                        kk += 1
                sch.vsync()
                sch.emit("vector", lambda v: v.tensor_reduce(
                    gth.ap().unsqueeze(2)[:, :, 0:1],
                    tmin.rearrange("p (tl k) -> p tl k", tl=NTL), AX.X, ALU.max))
                sch.vsync()
                for g in range(4):
                    sch.emit("vector", lambda v, g=g: v.tensor_tensor(
                        gm64[:, g::4], G64[:, g::4], gth.ap()[:, :], ALU.is_ge))
                sch.vsync()
                for ee in range(NE):
                    sch.emit("vector", lambda v, ee=ee: v.tensor_copy(
                        emask.ap()[:, ee::NE], gm64[:, (ee // 4)::4]))
                sch.vsync()
                sch.emit("vector", lambda v: v.memset(neg30[:], NEG))
                sch.vsync()
                sch.emit("vector", lambda v: v.select(
                    maskd[:], emask[:], sfc_tm[:], neg30[:]))
                sch.emit("vector", lambda v: v.memset(sel[:], 0.0))
                for k in range(4):
                    sch.emit("vector", lambda v: v.tensor_reduce(
                        mrow.ap().unsqueeze(2)[:, :, 0:1], M3, AX.X, ALU.max))
                    sch.vsync()
                    for ee in range(NE):
                        sch.emit("vector", lambda v, ee=ee: v.tensor_tensor(
                            eqm.ap()[:, ee::NE], maskd.ap()[:, ee::NE],
                            mrow.ap()[:, :], ALU.is_equal))
                    sch.vsync()
                    sch.emit("vector", lambda v: v.tensor_tensor(
                        sel[:], sel[:], eqm[:], ALU.add))
                    if k < 3:
                        sch.emit("vector", lambda v: v.scalar_tensor_tensor(
                            maskd[:], eqm[:], -2.0e30, maskd[:], ALU.mult, ALU.add))
                sch.emit("vector", lambda v: v.tensor_tensor(
                    cw_tm[:], sel[:], sc_tm[:], ALU.mult))
                CW3 = cw_tm.ap().rearrange("p (tl e) -> p tl e", tl=NTL)
                sch.emit("vector", lambda v: v.tensor_reduce(
                    sumw.ap()[:, 0:NTL].unsqueeze(2)[:, :, 0:1], CW3, AX.X, ALU.add))
                sch.vsync()
                sch.emit("vector", lambda v: v.tensor_scalar(
                    sumw.ap()[:, 0:NTL], sumw.ap()[:, 0:NTL], 1e-20, None, ALU.add))
                sch.vsync()
                sch.emit("vector", lambda v: v.reciprocal(
                    sumw.ap()[:, NTL:2 * NTL], sumw.ap()[:, 0:NTL]))
                sch.vsync()
                sch.emit("vector", lambda v: v.tensor_scalar(
                    sumw.ap()[:, NTL:2 * NTL], sumw.ap()[:, NTL:2 * NTL], 2.5, None,
                    ALU.mult))
                sch.vsync()
                vv = None
                for ee in range(NE):
                    vv = sch.emit("vector", lambda v, ee=ee: v.tensor_tensor(
                        cw_tm.ap()[:, ee::NE], cw_tm.ap()[:, ee::NE],
                        sumw.ap()[:, NTL:2 * NTL], ALU.mult),
                        inc="vS" if ee == NE - 1 else None)
                sch.wait("vector", "vS", vv)

                sch.wait("tensor", "vS", vv)
                for it in range(NTL):
                    bank = it % 2
                    sch.bank_acquire("tensor", f"pb{bank}")
                    vt = sch.emit("tensor", lambda t, it=it, b=bank: t.matmul(
                        pb[b][0:NE, 0:128], cw_tm.ap()[:, it * NE:(it + 1) * NE],
                        id32[:, 0:128], start=True, stop=True), inc="tS")
                    sch.wait("vector", "tS", vt)
                    vv = sch.emit("vector", lambda v, it=it, b=bank: v.tensor_copy(
                        cw_fm.ap()[:, it * 128:(it + 1) * 128], pb[b][0:NE, 0:128]),
                        inc="vS")
                    sch.bank_release(f"pb{bank}", "vS", vv)

                if "tap_sel" in taps:
                    sch.wait("sync", "vS")
                    for nm, buf_ in (("tap_scfm", sc_fm), ("tap_sctm", sc_tm),
                                     ("tap_sfctm", sfc_tm), ("tap_gsc", gsc),
                                     ("tap_sel", sel), ("tap_cwtm", cw_tm)):
                        if nm in taps:
                            sch.emit("sync", lambda s, nm=nm, b=buf_: s.dma_start(
                                out=taps[nm][:, :], in_=b[:]), inc="dO", n=16)
                sch.wait("sync", "vS")
                sch.emit("sync", lambda s: s.dma_start(
                    out=cw_d[:, :], in_=cw_fm[:]), inc="dO", n=16)
                if "tap_cw" in taps:
                    sch.wait("sync", "dO")
                    sch.emit("sync", lambda s: s.dma_start(
                        out=taps["tap_cw"][:, :], in_=cw_d[:, :]), inc="dO", n=16)

            xn16s = e3(nc.sbuf_tensor("xn16s", [128, HC * TH], BF16))
            ybuf = e3(nc.sbuf_tensor("ybuf", [128, HC * TH], F32))
            Wg16 = [e3(nc.sbuf_tensor(f"Wg16_{i}", [128, HC * IE], BF16))
                    for i in range(2)]
            Wu16 = [e3(nc.sbuf_tensor(f"Wu16_{i}", [128, HC * IE], BF16))
                    for i in range(2)]
            Wd16 = [e3(nc.sbuf_tensor(f"Wd16_{i}", [128, (IE // 128) * D], BF16))
                    for i in range(2)]
            silu16 = e3(nc.sbuf_tensor("silu16", [128, (IE // 128) * 256], BF16))
            hu16 = e3(nc.sbuf_tensor("hu16", [128, (IE // 128) * 256], BF16))
            act16 = [e3(nc.sbuf_tensor(f"act16_{i}", [128, (IE // 128) * 256],
                                       BF16)) for i in range(2)]
            cwb16 = e3(nc.sbuf_tensor("cwb16", [128, 256], BF16))
            cwrow = [e3(nc.sbuf_tensor(f"cwrow{i}", [1, TH], F32)) for i in range(2)]
            Xf = e3(nc.sbuf_tensor("Xf", [128, HC * 256], F32))
            sf = e3(nc.sbuf_tensor("sf", [128, HC * 256], F32))
            gf = e3(nc.sbuf_tensor("gf", [128, HC * 256], F32))
            yf = e3(nc.sbuf_tensor("yf", [128, HC * 256], F32))
            brf = e3(nc.sbuf_tensor("brf", [128, TH], F32))
            bcf = e3(nc.sbuf_tensor("bcf", [128, TH], F32))
            EXL = NE + 2
            GROUP = 1
            NTC = TH // 256

            def eld(buf_i, eidx):
                if eidx < NE:
                    wld(*chunked(Wg16[buf_i], WgT[eidx], HC))
                    wld(*chunked(Wu16[buf_i], WuT[eidx], HC))
                    return wld(*chunked(Wd16[buf_i], WdT[eidx], IE // 128))
                hh = eidx - NE
                wld(*chunked(Wg16[buf_i], WsgT[:, hh * IE:(hh + 1) * IE], HC))
                wld(*chunked(Wu16[buf_i], WsuT[:, hh * IE:(hh + 1) * IE], HC))
                return wld(*chunked(Wd16[buf_i], WsdT[hh * IE:(hh + 1) * IE, :],
                                    IE // 128))

            def expert_front(half0, eidx, tc, slot, buf, v_cw, prev_act):
                tok0 = tc * 256
                if eidx < NE:
                    sch.bank_acquire("tensor", "pb6")
                    if tc == 0:
                        sch.wait("tensor", "dX", v_cw)
                    vt = sch.emit("tensor", lambda t, buf=buf, tok0=tok0:
                                  t.matmul(pb[6][:, 0:256], ones_r[0:1, :],
                                           cwrow[buf].ap()[0:1, tok0:tok0 + 256],
                                           start=True, stop=True), inc="tS")
                    sch.wait("vector", "tS", vt)
                    vv = sch.emit("vector", lambda v: v.tensor_copy(
                        cwb16[:], pb[6][:, 0:256]), inc="vS")
                    sch.bank_release("pb6", "vS", vv)
                if prev_act is not None:
                    sch.wait("scalar", "vS", prev_act)
                for ic in range(IE // 128):
                    bank = 4 + (ic % 2)
                    sch.bank_acquire("tensor", f"pb{bank}")
                    for h in range(HC):
                        vt = sch.emit("tensor", lambda t, ic=ic, h=h, b=bank,
                                      buf=buf, tok0=tok0: t.matmul(
                                          pb[b][:, 0:256],
                                          Wg16[buf].ap()[:, h * IE + ic * 128:
                                                         h * IE + ic * 128 + 128],
                                          xn16s.ap()[:, h * TH + tok0:
                                                     h * TH + tok0 + 256],
                                          start=(h == 0), stop=(h == HC - 1)),
                                      inc="tS" if h == HC - 1 else None)
                    sch.wait("scalar", "tS", vt)
                    vs = sch.emit("scalar", lambda s, ic=ic, b=bank: s.activation(
                        silu16.ap()[:, ic * 256:(ic + 1) * 256], pb[b][:, 0:256],
                        ACT.Silu), inc="sS")
                    sch.bank_release(f"pb{bank}", "sS", vs)
                for ic in range(IE // 128):
                    bank = 6 + (ic % 2)
                    sch.bank_acquire("tensor", f"pb{bank}")
                    for h in range(HC):
                        vt = sch.emit("tensor", lambda t, ic=ic, h=h, b=bank,
                                      buf=buf, tok0=tok0: t.matmul(
                                          pb[b][:, 0:256],
                                          Wu16[buf].ap()[:, h * IE + ic * 128:
                                                         h * IE + ic * 128 + 128],
                                          xn16s.ap()[:, h * TH + tok0:
                                                     h * TH + tok0 + 256],
                                          start=(h == 0), stop=(h == HC - 1)),
                                      inc="tS" if h == HC - 1 else None)
                    sch.wait("vector", "tS", vt)
                    if eidx < NE:
                        vv = sch.emit("vector", lambda v, ic=ic, b=bank:
                                      v.scalar_tensor_tensor(
                                          hu16.ap()[:, ic * 256:(ic + 1) * 256],
                                          pb[b][:, 0:256], 1.0, cwb16[:],
                                          ALU.mult, ALU.mult), inc="vS")
                    else:
                        vv = sch.emit("vector", lambda v, ic=ic, b=bank:
                                      v.tensor_copy(
                                          hu16.ap()[:, ic * 256:(ic + 1) * 256],
                                          pb[b][:, 0:256]), inc="vS")
                    sch.bank_release(f"pb{bank}", "vS", vv)
                sch.wait("vector", "sS")
                return sch.emit("vector", lambda v, slot=slot: v.tensor_tensor(
                    act16[slot][:], silu16[:], hu16[:], ALU.mult), inc="vS")

            def expert_back(eidx, tc, slot, buf, vv_act):
                tok0 = tc * 256
                sch.wait("tensor", "vS", vv_act)
                for d in range(HC):
                    bank = d // 2
                    hfb = d % 2
                    if hfb == 0:
                        sch.bank_acquire("tensor", f"pb{bank}")
                    for icc in range(IE // 128):
                        last = (icc == IE // 128 - 1 and d == HC - 1)
                        vt = sch.emit("tensor", lambda t, d=d, icc=icc, b=bank,
                                      hfb=hfb, buf=buf, slot=slot: t.matmul(
                                          pb[b][:, hfb * 256:hfb * 256 + 256],
                                          Wd16[buf].ap()[:, icc * D + d * 128:
                                                         icc * D + d * 128 + 128],
                                          act16[slot].ap()[:, icc * 256:
                                                           (icc + 1) * 256],
                                          start=(icc == 0),
                                          stop=(icc == IE // 128 - 1),
                                          skip_group_check=True),
                                      inc="tS" if last else None)
                sch.wait("vector", "tS", vt)
                for d in range(HC):
                    bank = d // 2
                    hfb = d % 2
                    if eidx == 0:
                        vv = sch.emit("vector", lambda v, d=d, b=bank, hfb=hfb,
                                      tok0=tok0: v.tensor_copy(
                                          ybuf.ap()[:, d * TH + tok0:
                                                    d * TH + tok0 + 256],
                                          pb[b][:, hfb * 256:hfb * 256 + 256]),
                                      inc="vS")
                    else:
                        vv = sch.emit("vector", lambda v, d=d, b=bank, hfb=hfb,
                                      tok0=tok0: v.tensor_tensor(
                                          ybuf.ap()[:, d * TH + tok0:
                                                    d * TH + tok0 + 256],
                                          ybuf.ap()[:, d * TH + tok0:
                                                    d * TH + tok0 + 256],
                                          pb[b][:, hfb * 256:hfb * 256 + 256],
                                          ALU.add), inc="vS")
                    if hfb == 1:
                        sch.bank_release(f"pb{bank}", "vS", vv)

            def expert_loop(half0, v_xn, w_e, v_cw0, jobs=()):
                jobs = list(jobs)
                sch.wait("tensor", "dX", v_xn)
                pipe = []
                v_cw = {0: v_cw0}
                gstep = 0
                prev_act = None
                for eidx in range(EXL):
                    buf = eidx % 2
                    if eidx + 1 < NE:
                        sch.wait("sync", "tS")
                        v_cw[eidx + 1] = ld(cwrow[(eidx + 1) % 2][0:1, :],
                                            cw_d[eidx + 1:eidx + 2,
                                                 half0:half0 + TH])
                    if jobs and eidx >= 2:
                        jobs.pop(0)()
                    sch.wait("tensor", "dW", w_e[eidx])
                    for tc in range(NTC):
                        slot = gstep % 2
                        vv_act = expert_front(half0, eidx, tc, slot, buf,
                                              v_cw.get(eidx), prev_act)
                        prev_act = vv_act
                        pipe.append((eidx, tc, slot, buf, vv_act))
                        if len(pipe) > 1:
                            expert_back(*pipe.pop(0))
                        gstep += 1
                        if tc == 1 and 1 <= eidx < EXL - 1:
                            sch.wait("gpsimd", "tS", sch.cnt["tS"])
                            w_e[eidx + 1] = eld((eidx + 1) % 2, eidx + 1)
                expert_back(*pipe.pop(0))
                for j in jobs:
                    j()

            def finale_bcast(half0):
                for tc in range(NTC):
                    bcast2(rows["rstd0"], rows["c0"],
                           brf.ap()[:, tc * 256:(tc + 1) * 256],
                           bcf.ap()[:, tc * 256:(tc + 1) * 256],
                           slice(half0 + tc * 256, half0 + (tc + 1) * 256))

            fin_vg = {}

            def finale_a(half0, tc, v_yst):
                tok0 = tc * 256
                tksl = slice(half0 + tok0, half0 + tok0 + 256)
                sch.wait("sync", "gS")
                sch.wait("sync", "vS")
                sch.wait("sync", "dO")
                ld_cols(Xf, X, tksl)
                vX = ld_cols(sf, s_d, tksl)
                if v_yst is not None:
                    sch.wait("sync", "dO", v_yst)
                    vX = ld_cols(yf, y_d, tksl)
                sch.wait("vector", "dX", vX)
                sch.emit("vector", lambda v, tok0=tok0: v.tensor_tensor(
                    gf.ap().rearrange("p (c t) -> p c t", c=HC),
                    Xf.ap().rearrange("p (c t) -> p c t", c=HC),
                    brf.ap()[:, tok0:tok0 + 256].unsqueeze(1).broadcast_to(
                        [128, HC, 256]), ALU.mult))
                sch.emit("vector", lambda v, tok0=tok0: v.tensor_tensor(
                    gf.ap().rearrange("p (c t) -> p c t", c=HC),
                    gf.ap().rearrange("p (c t) -> p c t", c=HC),
                    bcf.ap()[:, tok0:tok0 + 256].unsqueeze(1).broadcast_to(
                        [128, HC, 256]), ALU.add))
                if v_yst is not None:
                    vv = sch.emit("vector", lambda v: v.tensor_tensor(
                        gf[:], gf[:], yf[:], ALU.add), inc="vS")
                else:
                    vv = sch.emit("vector", lambda v, tok0=tok0: v.tensor_tensor(
                        gf.ap().rearrange("p (c t) -> p c t", c=HC),
                        gf.ap().rearrange("p (c t) -> p c t", c=HC),
                        ybuf.ap().rearrange("p (c t2) -> p c t2", c=HC)[
                            :, :, tok0:tok0 + 256], ALU.add), inc="vS")
                sch.wait("scalar", "vS", vv)
                vs = sch.emit("scalar", lambda s: s.activation(
                    gf[:], gf[:], ACT.Silu), inc="sS")
                sch.wait("vector", "sS", vs)
                sch.emit("vector", lambda v: v.tensor_tensor(
                    sf[:], sf[:], gf[:], ALU.mult))

            def finale_b(half0, tc):
                tok0 = tc * 256
                tksl = slice(half0 + tok0, half0 + tok0 + 256)
                vv = sch.emit("vector", lambda v: v.tensor_tensor(
                    Xf[:], Xf[:], sf[:], ALU.add), inc="vS")
                sch.wait("sync", "vS", vv)
                st_cols(Xf, O, tksl)

            sch.wait("sync", "tS")
            v_xn_cur = ld_cols(xn16s, xn16_d, slice(0, TH))
            sch.wait("gpsimd", "tS")
            w_e_cur = [None] * EXL
            w_e_cur[0] = eld(0, 0)
            w_e_cur[1] = eld(1, 1)
            sch.wait("sync", "tS")
            v_cw0_cur = ld(cwrow[0][0:1, :], cw_d[0:1, 0:TH])

            expert_loop(0, v_xn_cur, w_e_cur, v_cw0_cur)
            finale_bcast(0)
            sch.wait("sync", "vS")
            for c in range(HC):
                sch.emit("sync", lambda s, c=c: s.dma_start(
                    out=y_d[c * 128:(c + 1) * 128, 0:TH],
                    in_=ybuf.ap()[:, c * TH:(c + 1) * TH]), inc="dO", n=16)
            v_yst = sch.cnt["dO"]
            if "tap_y" in taps:
                sch.wait("sync", "dO", v_yst)
                sch.emit("sync", lambda s: s.dma_start(
                    out=taps["tap_y"][:, 0:TH], in_=y_d[:, 0:TH]),
                    inc="dO", n=16)
            sch.wait("sync", "tS")
            v_xn_cur = ld_cols(xn16s, xn16_d, slice(TH, T))
            sch.wait("gpsimd", "tS")
            w_e_cur = [None] * EXL
            w_e_cur[0] = eld(0, 0)
            w_e_cur[1] = eld(1, 1)
            sch.wait("sync", "tS")
            v_cw0_cur = ld(cwrow[0][0:1, :], cw_d[0:1, TH:T])

            jobs = []
            for tc in range(NTC):
                jobs.append(lambda tc=tc: finale_a(0, tc, v_yst))
                jobs.append(lambda tc=tc: finale_b(0, tc))
            expert_loop(TH, v_xn_cur, w_e_cur, v_cw0_cur, jobs)
            finale_bcast(TH)
            if "tap_y" in taps:
                sch.wait("sync", "vS")
                for c in range(HC):
                    sch.emit("sync", lambda s, c=c: s.dma_start(
                        out=taps["tap_y"][c * 128:(c + 1) * 128, TH:T],
                        in_=ybuf.ap()[:, c * TH:(c + 1) * TH]), inc="dO", n=16)
            for tc in range(NTC):
                finale_a(TH, tc, None)
                finale_b(TH, tc)

            sch.barrier()

        block = ctx.enter_context(nc.Block())

        @block.sync
        def _(eng):
            for fn in sch.q["sync"]:
                fn(eng)

        @block.gpsimd
        def _(eng):
            for fn in sch.q["gpsimd"]:
                fn(eng)

        @block.vector
        def _(eng):
            for fn in sch.q["vector"]:
                fn(eng)

        @block.scalar
        def _(eng):
            for fn in sch.q["scalar"]:
                fn(eng)

        @block.tensor
        def _(eng):
            for fn in sch.q["tensor"]:
                fn(eng)

    return nc


def host_inputs(inputs):
    x = np.asarray(inputs["L0_attn_x"], dtype=np.float32)
    m_int, m_q1, m_q0 = make_masks()

    def t(a):
        return np.ascontiguousarray(np.asarray(a, dtype=np.float32).T)

    def cc(vec, n):
        return np.ascontiguousarray(
            np.asarray(vec, dtype=np.float32).reshape(n, 128).T)

    common = dict(
        Wqkv=np.ascontiguousarray(np.asarray(inputs["L0_attn_W_qkv"], np.float32)),
        WO=np.ascontiguousarray(np.asarray(inputs["L0_attn_W_O"], np.float32)),
        Wug=np.ascontiguousarray(np.asarray(inputs["L0_attn_W_ug"], np.float32)),
        Bw=np.ascontiguousarray(np.asarray(inputs["L0_attn_B_w"], np.float32)),
        Cw=np.ascontiguousarray(np.asarray(inputs["L0_attn_C_w"], np.float32)),
        W1=np.ascontiguousarray(np.asarray(inputs["L0_attn_mlp_W1"], np.float32)),
        W2=np.ascontiguousarray(np.asarray(inputs["L0_attn_mlp_W2"], np.float32)),
        gateT=t(inputs["L0_ffn_gate_w"]),
        WgT=np.ascontiguousarray(
            np.asarray(inputs["L0_ffn_Wg"], np.float32).transpose(0, 2, 1)),
        WuT=np.ascontiguousarray(
            np.asarray(inputs["L0_ffn_Wu"], np.float32).transpose(0, 2, 1)),
        WdT=np.ascontiguousarray(
            np.asarray(inputs["L0_ffn_Wd"], np.float32).transpose(0, 2, 1)),
        WsgT=t(inputs["L0_ffn_Ws_g"]),
        WsuT=t(inputs["L0_ffn_Ws_u"]),
        WsdT=t(inputs["L0_ffn_Ws_d"]),
        g1c=cc(inputs["L0_attn_ln1_g"], HC), b1c=cc(inputs["L0_attn_ln1_b"], HC),
        g2c=cc(inputs["L0_attn_ln2_g"], HC), b2c=cc(inputs["L0_attn_ln2_b"], HC),
        bOc=cc(inputs["L0_attn_b_O"], HC), b2mc=cc(inputs["L0_attn_mlp_b2"], HC),
        bugc=cc(inputs["L0_attn_b_ug"], HC + 1),
        b1mc=cc(inputs["L0_attn_mlp_b1"], DFF // 128),
        gbias=np.ascontiguousarray(
            np.asarray(inputs["L0_ffn_gate_bias"], np.float32).reshape(NE, 1)),
        Avec=np.ascontiguousarray(
            np.asarray(inputs["L0_attn_A"], np.float32).reshape(SD, 1)),
        id16=np.eye(128, dtype=np.float32).astype(ml_dtypes.bfloat16),
        id32=np.eye(128, dtype=np.float32),
        mint=m_int, mq1=m_q1, mq0=m_q0,
    )
    in_maps = []
    for b in range(8):
        m = dict(common)
        m["X"] = np.ascontiguousarray(x[b].T)
        in_maps.append(m)
    return in_maps


def kernel(**inputs):
    in_maps = host_inputs(inputs)
    nc = build_nc()
    res = run_bass_kernel_spmd(nc, in_maps, core_ids=list(range(8)))
    out = np.stack([np.asarray(r["O"]).T for r in res.results])
    return out.astype(np.float32)


if __name__ == "__main__":
    nc = build_nc()
    print("built", nc.n_instructions(), "instructions")


# version 14
# speedup vs baseline: 1.1555x; 1.0442x over previous
import numpy as np
import ml_dtypes
from contextlib import ExitStack

import concourse.bass as bass
import concourse.mybir as mybir
from concourse.bass_utils import run_bass_kernel_spmd

F32 = mybir.dt.float32
BF16 = mybir.dt.bfloat16
AX = mybir.AxisListType
ALU = mybir.AluOpType
ACT = mybir.ActivationFunctionType

D = 1024
T = 2048
HC = D // 128
NH = 16
DH = 64
SD = 128
DFF = 4096
NE = 16
IE = 512
EPS = 1e-5
NEG = -1.0e30
NT = 4

ENGINES = ("sync", "gpsimd", "vector", "scalar", "tensor")

TAPS = set()


class Sched:
    def __init__(self, nc, ctx):
        self.nc = nc
        self.q = {e: [] for e in ENGINES}
        self.sem = {}
        self.cnt = {}
        for name in ("dW", "dX", "dO", "tS", "vS", "sS", "gS", "B"):
            self.sem[name] = ctx.enter_context(nc.semaphore(name))
            self.cnt[name] = 0
        self.bank_last = {}

    def emit(self, eng, fn, inc=None, n=1):
        if inc is not None:
            sem = self.sem[inc]
            self.cnt[inc] += n
            self.q[eng].append(lambda e, fn=fn, sem=sem, n=n: fn(e).then_inc(sem, n))
            return self.cnt[inc]
        self.q[eng].append(fn)
        return None

    def wait(self, eng, name, val=None):
        v = self.cnt[name] if val is None else val
        s = self.sem[name]
        self.q[eng].append(lambda e, s=s, v=v: e.wait_ge(s, v))

    def bank_acquire(self, eng, bank):
        last = self.bank_last.get(bank)
        if last is not None:
            sem_name, v = last
            self.wait(eng, sem_name, v)
            self.bank_last[bank] = None

    def bank_release(self, bank, sem_name, v):
        self.bank_last[bank] = (sem_name, v)

    def vsync(self):
        v = self.emit("vector", lambda e: e.tensor_copy(self.dumV[:], self.dumV[:]),
                      inc="vS")
        self.wait("vector", "vS", v)

    def barrier(self):
        semB = self.sem["B"]
        self.wait("sync", "dX")
        self.wait("sync", "dO")
        self.wait("gpsimd", "dW")
        self.q["vector"].append(
            lambda e: e.tensor_copy(self.dumV[:], self.dumV[:]).then_inc(semB, 1))
        self.q["scalar"].append(
            lambda e: e.copy(self.dumS[:], self.dumS[:]).then_inc(semB, 1))
        self.q["gpsimd"].append(
            lambda e: e.tensor_copy(self.dumG[:], self.dumG[:]).then_inc(semB, 1))
        self.q["tensor"].append(lambda e: e.nop().then_inc(semB, 1))
        self.q["sync"].append(lambda e: e.nop().then_inc(semB, 1))
        self.cnt["B"] += 5
        tgt = self.cnt["B"]
        for eng in ENGINES:
            self.wait(eng, "B", tgt)
        self.bank_last = {}


def make_masks():
    i = np.arange(128)[:, None]
    j3 = np.arange(384)[None, :]
    m_int = np.where((j3 >= i + 1) & (j3 <= i + 256), 0.0, NEG)
    j2 = np.arange(256)[None, :]
    m_q1 = np.where(j2 <= i + 128, 0.0, NEG)
    j1 = np.arange(128)[None, :]
    m_q0 = np.where(j1 <= i, 0.0, NEG)
    bf = lambda a: np.ascontiguousarray(a.astype(np.float32)).astype(ml_dtypes.bfloat16)
    return bf(m_int), bf(m_q1), bf(m_q0)


def build_nc():
    nc = bass.Bass("TRN2", target_bir_lowering=False)
    ctx = ExitStack()

    def dram_in(name, shape, dt=F32):
        return nc.dram_tensor(name, list(shape), dt, kind="ExternalInput").ap()

    X = dram_in("X", [D, T])
    Wqkv = dram_in("Wqkv", [D, 3 * D])
    WO = dram_in("WO", [D, D])
    Wug = dram_in("Wug", [D, SD + D])
    Bw = dram_in("Bw", [D, SD])
    Cw = dram_in("Cw", [SD, D])
    W1 = dram_in("W1", [D, DFF])
    W2 = dram_in("W2", [DFF, D])
    gateT = dram_in("gateT", [D, NE])
    WgT = dram_in("WgT", [NE, D, IE])
    WuT = dram_in("WuT", [NE, D, IE])
    WdT = dram_in("WdT", [NE, IE, D])
    WsgT = dram_in("WsgT", [D, 2 * IE])
    WsuT = dram_in("WsuT", [D, 2 * IE])
    WsdT = dram_in("WsdT", [2 * IE, D])
    g1c = dram_in("g1c", [128, HC])
    b1c = dram_in("b1c", [128, HC])
    g2c = dram_in("g2c", [128, HC])
    b2c = dram_in("b2c", [128, HC])
    bOc = dram_in("bOc", [128, HC])
    b2mc = dram_in("b2mc", [128, HC])
    bugc = dram_in("bugc", [128, HC + 1])
    b1mc = dram_in("b1mc", [128, DFF // 128])
    gbias = dram_in("gbias", [NE, 1])
    Avec_d = dram_in("Avec", [SD, 1])
    id16_d = dram_in("id16", [128, 128], BF16)
    id32_d = dram_in("id32", [128, 128])
    mint_d = dram_in("mint", [128, 384], BF16)
    mq1_d = dram_in("mq1", [128, 256], BF16)
    mq0_d = dram_in("mq0", [128, 128], BF16)

    O = nc.dram_tensor("O", [D, T], F32, kind="ExternalOutput").ap()

    xn16_d = nc.dram_tensor("xn16_d", [D, T], BF16, kind="Internal").ap()
    xn1_d = nc.dram_tensor("xn1_d", [D, T], BF16, kind="Internal").ap()
    gt_d = nc.dram_tensor("gt_d", [D, T], F32, kind="Internal").ap()
    x2_d = nc.dram_tensor("x2_d", [D, T], F32, kind="Internal").ap()
    s_d = nc.dram_tensor("s_d", [D, T], F32, kind="Internal").ap()
    cw_d = nc.dram_tensor("cw_d", [NE, T], F32, kind="Internal").ap()
    y_d = nc.dram_tensor("y_d", [D, T], F32, kind="Internal").ap()
    w1b_d = nc.dram_tensor("w1b_d", [D, DFF], BF16, kind="Internal").ap()
    w2b_d = nc.dram_tensor("w2b_d", [DFF, D], BF16, kind="Internal").ap()

    taps = {}
    for tname, shape, dt in (
        ("tap_xn16", [D, T], BF16), ("tap_xn1", [D, T], BF16),
        ("tap_gt", [D, T], F32), ("tap_drive", [SD, T], F32),
        ("tap_states", [SD, T], F32), ("tap_x2", [D, T], F32),
        ("tap_s", [D, T], F32), ("tap_cw", [NE, T], F32),
        ("tap_y", [D, T], F32), ("tap_attn", [D, T], F32),
        ("tap_yssm", [D, T], F32), ("tap_scfm", [NE, T], F32),
        ("tap_sctm", [128, (T // 128) * NE], F32),
        ("tap_sfctm", [128, (T // 128) * NE], F32),
        ("tap_gsc", [128, (T // 128) * 4], F32),
        ("tap_sel", [128, (T // 128) * NE], F32),
        ("tap_cwtm", [128, (T // 128) * NE], F32),
    ):
        if tname in TAPS:
            taps[tname] = nc.dram_tensor(tname, shape, dt, kind="ExternalOutput").ap()

    with ctx:
        e = ctx.enter_context
        sch = Sched(nc, ctx)

        id16 = e(nc.sbuf_tensor("id16s", [128, 128], BF16))
        id32 = e(nc.sbuf_tensor("id32s", [128, 128], F32))
        mint = e(nc.sbuf_tensor("mints", [128, 384], BF16))
        mq1 = e(nc.sbuf_tensor("mq1s", [128, 256], BF16))
        mq0 = e(nc.sbuf_tensor("mq0s", [128, 128], BF16))
        ones_c = e(nc.sbuf_tensor("ones_c", [128, 2], F32))
        ones_r = e(nc.sbuf_tensor("ones_r", [1, 128], F32))
        zrow16 = e(nc.sbuf_tensor("zrow16", [1, 512], BF16))
        vg1 = e(nc.sbuf_tensor("vg1", [128, HC], F32))
        vb1 = e(nc.sbuf_tensor("vb1", [128, HC], F32))
        vg2 = e(nc.sbuf_tensor("vg2", [128, HC], F32))
        vb2 = e(nc.sbuf_tensor("vb2", [128, HC], F32))
        vbO = e(nc.sbuf_tensor("vbO", [128, HC], F32))
        vb2m = e(nc.sbuf_tensor("vb2m", [128, HC], F32))
        vbug = e(nc.sbuf_tensor("vbug", [128, HC + 1], F32))
        vb1m = e(nc.sbuf_tensor("vb1m", [128, DFF // 128], F32))
        vgb = e(nc.sbuf_tensor("vgb", [NE, 1], F32))
        vA = e(nc.sbuf_tensor("vA", [SD, 1], F32))

        rows = {}
        for rn in ("rstd0", "c0", "mu2", "rstd2", "c2"):
            rows[rn] = e(nc.sbuf_tensor("row_" + rn, [1, T], F32))

        pb = [e(nc.psum_tensor(f"pb{i}", [128, 512], F32)) for i in range(8)]
        sch.dumV = e(nc.sbuf_tensor("dumV", [128, 64], F32))
        sch.dumS = e(nc.sbuf_tensor("dumS", [128, 64], F32))
        sch.dumG = e(nc.sbuf_tensor("dumG", [128, 64], F32))

        def ld(dst, src):
            return sch.emit("sync",
                            lambda s, d=dst, a=src: s.dma_start(out=d, in_=a),
                            inc="dX", n=16)

        def wld(dst_ap, src_ap):
            return sch.emit("gpsimd",
                            lambda g, d=dst_ap, s=src_ap: g.dma_start(out=d, in_=s),
                            inc="dW", n=16)

        def chunked(dst_sb, src_dram, cdim):
            return (dst_sb.ap().rearrange("p (c j) -> p c j", c=cdim),
                    src_dram.rearrange("(c p) j -> p c j", p=128))

        def ld_cols(dst_sb, src_dram, tksl, cdim=HC):
            return ld(dst_sb.ap().rearrange("p (c t) -> p c t", c=cdim),
                      src_dram[:, tksl].rearrange("(c p) t -> p c t", p=128))

        def st_cols(src_sb, dst_dram, tksl, cdim=HC):
            return sch.emit("sync", lambda s, a=src_sb, d=dst_dram, tk=tksl, c=cdim:
                            s.dma_start(
                                out=d[:, tk].rearrange("(c p) t -> p c t", p=128),
                                in_=a.ap().rearrange("p (c t) -> p c t", c=c)),
                            inc="dO", n=16)

        ld(id16[:], id16_d[:, :])
        ld(id32[:], id32_d[:, :])
        ld(mint[:], mint_d[:, :])
        ld(mq1[:], mq1_d[:, :])
        ld(mq0[:], mq0_d[:, :])
        for dst, src in ((vg1, g1c), (vb1, b1c), (vg2, g2c), (vb2, b2c),
                         (vbO, bOc), (vb2m, b2mc), (vbug, bugc), (vb1m, b1mc),
                         (vgb, gbias), (vA, Avec_d)):
            ld(dst[:], src[:, :])
        cst_loaded = sch.cnt["dX"]

        sch.wait("vector", "dX", cst_loaded)
        sch.emit("vector", lambda v: v.memset(ones_c[:], 1.0))
        sch.emit("vector", lambda v: v.memset(ones_r[:], 1.0))
        v_consts = sch.emit("vector", lambda v: v.memset(zrow16[:], 0.0), inc="vS")
        sch.wait("tensor", "vS", v_consts)

        def bcast2(row_a, row_b, dst_a, dst_b, tksl, banks=(6, 7), n=256):
            ba, bb = banks
            sch.bank_acquire("tensor", f"pb{ba}")
            sch.emit("tensor", lambda t, r=row_a, tk=tksl, n=n: t.matmul(
                pb[ba][:, 0:n], ones_r[0:1, :], r[0:1, tk], start=True, stop=True))
            sch.bank_acquire("tensor", f"pb{bb}")
            vt = sch.emit("tensor", lambda t, r=row_b, tk=tksl, n=n: t.matmul(
                pb[bb][:, 0:n], ones_r[0:1, :], r[0:1, tk], start=True, stop=True),
                inc="tS")
            sch.wait("vector", "tS", vt)
            sch.emit("vector", lambda v, d=dst_a, n=n: v.tensor_copy(
                d[:, 0:n], pb[ba][:, 0:n]))
            vv = sch.emit("vector", lambda v, d=dst_b, n=n: v.tensor_copy(
                d[:, 0:n], pb[bb][:, 0:n]), inc="vS")
            sch.bank_release(f"pb{ba}", "vS", vv)
            sch.bank_release(f"pb{bb}", "vS", vv)
            return vv

        spanA = ExitStack()
        Wug16 = spanA.enter_context(
            nc.sbuf_tensor("Wug16", [128, HC * (SD + D)], BF16, side="right"))
        Bw16 = spanA.enter_context(
            nc.sbuf_tensor("Bw16", [128, HC * SD], BF16, side="right"))
        wld(*chunked(Wug16, Wug, HC))
        w_p1a = wld(*chunked(Bw16, Bw, HC))

        wstg = spanA.enter_context(
            nc.sbuf_tensor("wstg", [128, 2 * DFF], BF16, side="right"))
        pc_rounds = ([(W1[k * 128:(k + 1) * 128, :],
                       w1b_d[k * 128:(k + 1) * 128, :], DFF)
                      for k in range(D // 128)] +
                     [(W2[k * 128:(k + 1) * 128, :],
                       w2b_d[k * 128:(k + 1) * 128, :], D)
                      for k in range(DFF // 128)])
        pc_st = {}
        for k, (src, dst, w) in enumerate(pc_rounds):
            sl = wstg.ap()[:, (k % 2) * DFF:(k % 2) * DFF + w]
            if k >= 2:
                sch.wait("gpsimd", "dW", pc_st[k - 2])
            vg = sch.emit("gpsimd", lambda g, sl=sl, src=src: g.dma_start(
                out=sl, in_=src), inc="dW", n=16)
            sch.wait("gpsimd", "dW", vg)
            pc_st[k] = sch.emit("gpsimd", lambda g, sl=sl, dst=dst: g.dma_start(
                out=dst, in_=sl), inc="dW", n=16)
        v_preconv = sch.cnt["dW"]

        spanB = ExitStack()
        eB = spanB.enter_context
        drive = eB(nc.sbuf_tensor("drive", [SD, T], F32))
        states = eB(nc.sbuf_tensor("states", [SD, T], F32))
        st16 = eB(nc.sbuf_tensor("st16", [SD, T], BF16))
        pw64 = eB(nc.sbuf_tensor("pw64", [SD, 64], F32))
        apows = eB(nc.sbuf_tensor("apows", [SD, 16], F32))

        with ExitStack() as p0:
            e0 = p0.enter_context
            xbuf = [e0(nc.sbuf_tensor(f"xbuf{i}", [128, T], F32)) for i in range(2)]
            xsq2 = [e0(nc.sbuf_tensor(f"xsq{i}", [128, T], F32)) for i in range(2)]
            bA = e0(nc.sbuf_tensor("bA", [128, T], F32))
            bB = e0(nc.sbuf_tensor("bB", [128, T], F32))
            stg = [e0(nc.sbuf_tensor(f"stg{i}", [128, T], BF16)) for i in range(2)]
            prow = {}
            for rn in ("mu0", "mu1", "rstd1", "tmpA", "tmpB"):
                prow[rn] = e0(nc.sbuf_tensor("prow_" + rn, [1, T], F32))

            def rsqrt_row(dst_row, src_row):
                sch.emit("scalar", lambda s: s.activation(
                    prow["tmpB"][0:1, :], src_row, ACT.Ln))
                return sch.emit("scalar", lambda s: s.activation(
                    dst_row, prow["tmpB"][0:1, :], ACT.Exp, scale=-0.5), inc="sS")

            def stats_mms(src_fn, c):
                if c == 0:
                    for b_ in range(2 * NT):
                        sch.bank_acquire("tensor", f"pb{b_}")
                for nt in range(NT):
                    sch.emit("tensor", lambda t, nt=nt, c=c, src=src_fn: t.matmul(
                        pb[nt][0:1, 0:512], ones_c[:, 0:1], src(nt),
                        start=(c == 0), stop=(c == HC - 1)))

            def stats_mms_sq(c):
                vt = None
                xq = xsq2[c % 2]
                for nt in range(NT):
                    vt = sch.emit("tensor", lambda t, nt=nt, c=c, xq=xq: t.matmul(
                        pb[4 + nt][0:1, 0:512], ones_c[:, 0:1],
                        xq[:, nt * 512:(nt + 1) * 512],
                        start=(c == 0), stop=(c == HC - 1)),
                        inc="tS" if nt == NT - 1 else None)
                return vt

            def row_stats(mu_row, rstd_row):
                for nt in range(NT):
                    sch.emit("vector", lambda v, nt=nt, mu=mu_row: v.tensor_scalar(
                        mu[0:1, nt * 512:(nt + 1) * 512], pb[nt][0:1, 0:512],
                        1.0 / D, None, ALU.mult))
                    sch.emit("vector", lambda v, nt=nt: v.tensor_scalar(
                        prow["tmpA"][0:1, nt * 512:(nt + 1) * 512],
                        pb[4 + nt][0:1, 0:512], 1.0 / D, None, ALU.mult))
                vv = sch.emit("vector", lambda v, mu=mu_row: v.tensor_tensor(
                    prow["tmpB"][0:1, :], mu[0:1, :], mu[0:1, :], ALU.mult),
                    inc="vS")
                for b_ in range(2 * NT):
                    sch.bank_release(f"pb{b_}", "vS", vv)
                sch.emit("vector", lambda v: v.tensor_tensor(
                    prow["tmpA"][0:1, :], prow["tmpA"][0:1, :], prow["tmpB"][0:1, :],
                    ALU.subtract))
                vv = sch.emit("vector", lambda v: v.tensor_scalar(
                    prow["tmpA"][0:1, :], prow["tmpA"][0:1, :], EPS, None, ALU.add),
                    inc="vS")
                sch.wait("scalar", "vS", vv)
                return rsqrt_row(rstd_row, prow["tmpA"][0:1, :])

            ld_c = {0: ld(xbuf[0][:], X[0:128, :]),
                    1: ld(xbuf[1][:], X[128:256, :])}
            vts = {}
            for c in range(HC):
                x_c = xbuf[c % 2]
                sch.wait("scalar", "dX", ld_c[c])
                if c >= 2:
                    sch.wait("scalar", "tS", vts[c - 2])
                vs = sch.emit("scalar", lambda s, xc=x_c, c=c: s.activation(
                    xsq2[c % 2][:], xc[:], ACT.Square), inc="sS")
                sch.wait("tensor", "dX", ld_c[c])
                stats_mms(lambda nt, xc=x_c: xc[:, nt * 512:(nt + 1) * 512], c)
                sch.wait("tensor", "sS", vs)
                vts[c] = stats_mms_sq(c)
                if c + 2 < HC:
                    sch.wait("sync", "tS", vts[c])
                    sch.wait("sync", "sS", vs)
                    ld_c[c + 2] = ld(xbuf[c % 2][:],
                                     X[(c + 2) * 128:(c + 3) * 128, :])
            sch.wait("vector", "tS")
            vs_r0 = row_stats(prow["mu0"][0:1, :], rows["rstd0"][0:1, :])
            sch.wait("vector", "sS", vs_r0)
            vv_c0 = sch.emit("vector", lambda v: v.scalar_tensor_tensor(
                rows["c0"][0:1, :], prow["mu0"][0:1, :], -1.0, rows["rstd0"][0:1, :],
                ALU.mult, ALU.mult), inc="vS")
            sch.wait("tensor", "vS", vv_c0)
            for nt in range(NT):
                bcast2(rows["rstd0"], rows["c0"],
                       bA.ap()[:, nt * 512:nt * 512 + 512],
                       bB.ap()[:, nt * 512:nt * 512 + 512],
                       slice(nt * 512, (nt + 1) * 512), banks=(nt % 2, 2 + nt % 2),
                       n=512)

            sch.wait("sync", "tS")
            sch.wait("sync", "sS")
            ld_c = {0: ld(xbuf[0][:], X[0:128, :]),
                    1: ld(xbuf[1][:], X[128:256, :])}
            vts = {}
            for c in range(HC):
                x_c = xbuf[c % 2]
                sch.wait("vector", "dX", ld_c[c])
                sch.emit("vector", lambda v, xc=x_c: v.tensor_tensor(
                    xc[:], xc[:], bA[:], ALU.mult))
                vv = sch.emit("vector", lambda v, xc=x_c: v.tensor_tensor(
                    xc[:], xc[:], bB[:], ALU.add), inc="vS")
                st = stg[c % 2]
                sch.wait("vector", "dO")
                vv2 = sch.emit("vector", lambda v, xc=x_c, st=st: v.tensor_copy(
                    st[:], xc[:]), inc="vS")
                sch.wait("sync", "vS", vv2)
                sch.emit("sync", lambda s, st=st, c=c: s.dma_start(
                    out=xn16_d[c * 128:(c + 1) * 128, :], in_=st[:]), inc="dO", n=16)
                sch.wait("scalar", "vS", vv)
                if c >= 2:
                    sch.wait("scalar", "tS", vts[c - 2])
                vs = sch.emit("scalar", lambda s, xc=x_c, c=c: s.activation(
                    xsq2[c % 2][:], xc[:], ACT.Square), inc="sS")
                sch.wait("tensor", "vS", vv)
                stats_mms(lambda nt, xc=x_c: xc[:, nt * 512:(nt + 1) * 512], c)
                sch.wait("tensor", "sS", vs)
                vts[c] = stats_mms_sq(c)
                if c + 2 < HC:
                    sch.wait("sync", "tS", vts[c])
                    sch.wait("sync", "vS", vv2)
                    ld_c[c + 2] = ld(xbuf[c % 2][:],
                                     X[(c + 2) * 128:(c + 3) * 128, :])
            sch.wait("vector", "tS")
            vs_r1 = row_stats(prow["mu1"][0:1, :], prow["rstd1"][0:1, :])
            sch.wait("vector", "sS", vs_r1)
            sch.emit("vector", lambda v: v.tensor_tensor(
                prow["tmpA"][0:1, :], rows["rstd0"][0:1, :], prow["rstd1"][0:1, :],
                ALU.mult))
            sch.emit("vector", lambda v: v.tensor_tensor(
                prow["tmpB"][0:1, :], rows["c0"][0:1, :], prow["mu1"][0:1, :],
                ALU.subtract))
            vv_cc = sch.emit("vector", lambda v: v.tensor_tensor(
                prow["tmpB"][0:1, :], prow["tmpB"][0:1, :], prow["rstd1"][0:1, :],
                ALU.mult), inc="vS")
            sch.wait("tensor", "vS", vv_cc)
            for nt in range(NT):
                bcast2(prow["tmpA"], prow["tmpB"],
                       bA.ap()[:, nt * 512:nt * 512 + 512],
                       bB.ap()[:, nt * 512:nt * 512 + 512],
                       slice(nt * 512, (nt + 1) * 512), banks=(nt % 2, 2 + nt % 2),
                       n=512)

            sch.wait("sync", "tS")
            sch.wait("sync", "sS")
            sch.wait("sync", "vS")
            ld_c = {0: ld(xbuf[0][:], X[0:128, :]),
                    1: ld(xbuf[1][:], X[128:256, :])}
            st_c = {}
            for c in range(HC):
                x_c = xbuf[c % 2]
                sch.wait("vector", "dX", ld_c[c])
                sch.emit("vector", lambda v, xc=x_c: v.tensor_tensor(
                    xc[:], xc[:], bA[:], ALU.mult))
                vv = sch.emit("vector", lambda v, xc=x_c: v.tensor_tensor(
                    xc[:], xc[:], bB[:], ALU.add), inc="vS")
                st = stg[c % 2]
                sch.wait("scalar", "vS", vv)
                if c >= 2:
                    sch.wait("scalar", "dO", st_c[c - 2])
                vs = sch.emit("scalar", lambda s, xc=x_c, st=st, c=c: s.activation(
                    st[:], xc[:], ACT.Identity,
                    bias=vb1[:, c:c + 1], scale=vg1[:, c:c + 1]), inc="sS")
                sch.wait("sync", "sS", vs)
                st_c[c] = sch.emit("sync", lambda s, st=st, c=c: s.dma_start(
                    out=xn1_d[c * 128:(c + 1) * 128, :], in_=st[:]),
                    inc="dO", n=16)
                if c + 2 < HC:
                    sch.wait("sync", "vS", vv)
                    sch.wait("sync", "sS", vs)
                    ld_c[c + 2] = ld(xbuf[c % 2][:],
                                     X[(c + 2) * 128:(c + 3) * 128, :])

            if "tap_xn1" in taps:
                sch.wait("sync", "dO")
                sch.emit("sync", lambda s: s.dma_start(
                    out=taps["tap_xn1"][:, :], in_=xn1_d[:, :]), inc="dO", n=16)
            if "tap_xn16" in taps:
                sch.wait("sync", "dO")
                sch.emit("sync", lambda s: s.dma_start(
                    out=taps["tap_xn16"][:, :], in_=xn16_d[:, :]), inc="dO", n=16)

            sch.barrier()

        with ExitStack() as p1a:
            e1 = p1a.enter_context
            gtst = [e1(nc.sbuf_tensor(f"gtst{i}", [128, 512], F32)) for i in range(4)]
            xn1t = [e1(nc.sbuf_tensor(f"xn1tA{i}", [128, HC * 512], BF16))
                    for i in range(4)]

            sch.wait("tensor", "dW", w_p1a)
            ld_nt = {}
            for nt in range(NT):
                ld_nt[nt] = ld_cols(xn1t[nt], xn1_d,
                                    slice(nt * 512, (nt + 1) * 512))
            for nt in range(NT):
                t0 = nt * 512
                xb = xn1t[nt]
                sch.wait("tensor", "dX", ld_nt[nt])
                bank0 = nt % 2
                sch.bank_acquire("tensor", f"pb{bank0}")
                for h in range(HC):
                    sch.emit("tensor", lambda t, h=h, xb=xb, b=bank0: t.matmul(
                        pb[b][:, 0:512], Bw16.ap()[:, h * SD:h * SD + 128],
                        xb.ap()[:, h * 512:(h + 1) * 512],
                        start=(h == 0), stop=False))
                for h in range(HC):
                    vt = sch.emit("tensor", lambda t, h=h, xb=xb, b=bank0: t.matmul(
                        pb[b][:, 0:512],
                        Wug16.ap()[:, h * (SD + D):h * (SD + D) + 128],
                        xb.ap()[:, h * 512:(h + 1) * 512],
                        start=False, stop=(h == HC - 1)),
                        inc="tS" if h == HC - 1 else None)
                sch.wait("scalar", "tS", vt)
                vs = sch.emit("scalar", lambda s, t0=t0, b=bank0: s.activation(
                    drive.ap()[:, t0:t0 + 512], pb[b][:, 0:512], ACT.Identity,
                    bias=vbug[:, 0:1]), inc="sS")
                sch.bank_release(f"pb{bank0}", "sS", vs)

            def p1a_gates():
                st_done = {}
                gi = 0
                for nt in range(NT):
                    t0 = nt * 512
                    xb = xn1t[nt]
                    for dc in range(1, HC + 1):
                        bank = 2 + (gi % 2)
                        sch.bank_acquire("tensor", f"pb{bank}")
                        for h in range(HC):
                            vt = sch.emit("tensor",
                                          lambda t, h=h, dc=dc, b=bank, xb=xb:
                                          t.matmul(
                                              pb[b][:, 0:512],
                                              Wug16.ap()[:, h * (SD + D) + dc * 128:
                                                         h * (SD + D) + dc * 128
                                                         + 128],
                                              xb.ap()[:, h * 512:(h + 1) * 512],
                                              start=(h == 0), stop=(h == HC - 1)),
                                          inc="tS" if h == HC - 1 else None)
                        stb = gtst[gi % 4]
                        sch.wait("scalar", "tS", vt)
                        if gi >= 4:
                            sch.wait("scalar", "dO", st_done[gi - 4])
                        vs = sch.emit("scalar", lambda s, b=bank, stb=stb, dc=dc:
                                      s.activation(stb[:], pb[b][:, 0:512],
                                                   ACT.Sigmoid,
                                                   bias=vbug[:, dc:dc + 1]),
                                      inc="sS")
                        sch.bank_release(f"pb{bank}", "sS", vs)
                        sch.wait("sync", "sS", vs)
                        st_done[gi] = sch.emit("sync",
                                               lambda s, dc=dc, t0=t0, stb=stb:
                                               s.dma_start(
                                                   out=gt_d[(dc - 1) * 128:dc * 128,
                                                            t0:t0 + 512],
                                                   in_=stb[:]),
                                               inc="dO", n=16)
                        gi += 1

            sch.emit("vector", lambda v: v.memset(pw64[:], 1.0))
            sch.vsync()
            sch.emit("vector", lambda v: v.tensor_scalar(
                pw64[:], pw64[:], vA[:, 0:1], None, ALU.mult))
            sch.vsync()
            for l in range(11):
                sch.emit("vector", lambda v, l=l: v.tensor_copy(
                    apows[:, l:l + 1], pw64[:, 0:1]))
                sch.vsync()
                sch.emit("vector", lambda v: v.tensor_tensor(
                    pw64[:], pw64[:], pw64[:], ALU.mult))
                sch.vsync()
            sch.wait("vector", "sS")
            bufs = [drive, states]
            for l in range(11):
                dd = 1 << l
                srcb = bufs[l % 2]
                dstb = bufs[1 - l % 2]
                sch.emit("vector", lambda v, dd=dd, s=srcb, d=dstb: v.tensor_copy(
                    d.ap()[:, 0:dd], s.ap()[:, 0:dd]))
                sch.emit("vector", lambda v, dd=dd, s=srcb, d=dstb, l=l:
                         v.scalar_tensor_tensor(
                             d.ap()[:, dd:T], s.ap()[:, 0:T - dd],
                             apows[:, l:l + 1], s.ap()[:, dd:T],
                             ALU.mult, ALU.add))
                sch.vsync()
            v_scan = sch.emit("vector", lambda v: v.tensor_copy(
                st16[:], states[:]), inc="vS")

            p1a_gates()

            if "tap_drive" in taps:
                sch.wait("sync", "vS")
                sch.emit("sync", lambda s: s.dma_start(
                    out=taps["tap_drive"][:, :], in_=drive[:]), inc="dO", n=16)
            if "tap_states" in taps:
                sch.wait("sync", "vS")
                sch.emit("sync", lambda s: s.dma_start(
                    out=taps["tap_states"][:, :], in_=states[:]), inc="dO", n=16)
            if "tap_gt" in taps:
                sch.wait("sync", "dO")
                sch.emit("sync", lambda s: s.dma_start(
                    out=taps["tap_gt"][:, :], in_=gt_d[:, :]), inc="dO", n=16)

            p1a_end = {k: sch.cnt[k] for k in ("tS", "sS", "dO")}

        spanA.close()

        with ExitStack() as p1b:
            e1 = p1b.enter_context
            xn1t = [e1(nc.sbuf_tensor(f"xn1tB{i}", [128, HC * 256], BF16))
                    for i in range(2)]
            xt = e1(nc.sbuf_tensor("xt", [128, HC * 256], F32))
            gtt = e1(nc.sbuf_tensor("gtt", [128, HC * 256], F32))
            q16 = e1(nc.sbuf_tensor("q16", [128, HC * 256], BF16))
            kring = [e1(nc.sbuf_tensor(f"kring{i}", [128, HC * 256], BF16))
                     for i in range(2)]
            vring = [[e1(nc.sbuf_tensor(f"vring{i}{j}", [128, D], BF16))
                      for j in range(2)] for i in range(2)]
            Wqkv16 = e1(nc.sbuf_tensor("Wqkv16", [128, HC * 3 * D], BF16))
            WO16 = e1(nc.sbuf_tensor("WO16", [128, HC * D], BF16))
            Cw16 = e1(nc.sbuf_tensor("Cw16", [SD, D], BF16))
            o16 = e1(nc.sbuf_tensor("o16", [128, HC * 256], BF16))
            E16 = e1(nc.sbuf_tensor("E16", [128, 5 * 384], BF16))
            P16 = e1(nc.sbuf_tensor("P16", [128, 5 * 384], BF16))
            PT16 = e1(nc.sbuf_tensor("PT16", [128, 5 * 384], BF16))
            nmax = e1(nc.sbuf_tensor("nmax", [128, 5], F32))
            denom = e1(nc.sbuf_tensor("denom", [128, 10], F32))
            a_c = e1(nc.sbuf_tensor("a_c", [128, 256], F32))
            y_c = e1(nc.sbuf_tensor("y_c", [128, 256], F32))
            x2t = e1(nc.sbuf_tensor("x2t", [128, HC * 256], F32))
            xsqt = e1(nc.sbuf_tensor("xsqt", [128, 256], F32))
            br0t = e1(nc.sbuf_tensor("br0t", [128, 256], F32))
            bc0t = e1(nc.sbuf_tensor("bc0t", [128, 256], F32))
            tmpArow = xt

            for k_, v_ in p1a_end.items():
                sch.wait("sync", k_, v_)
            wld(*chunked(Wqkv16, Wqkv, HC))
            wld(*chunked(WO16, WO, HC))
            w_p1b = wld(Cw16.ap(), Cw[:, :])
            sch.wait("tensor", "dW", w_p1b)

            n_tiles = T // 256
            stats_pend = [None]
            stats_done = [None]

            def ln2_stats(si, vv_x2s):
                tksl_s = slice(si * 256, (si + 1) * 256)
                sch.wait("sync", "vS", vv_x2s)
                st_cols(x2t, x2_d, tksl_s)
                v_st = sch.cnt["dO"]
                sch.bank_acquire("tensor", "pb6")
                sch.bank_acquire("tensor", "pb7")
                vt = None
                vs = None
                for c in range(HC):
                    sch.wait("scalar", "vS", vv_x2s)
                    if vt is not None:
                        sch.wait("scalar", "tS", vt)
                    vs = sch.emit("scalar", lambda s, c=c: s.activation(
                        xsqt[:], x2t.ap()[:, c * 256:(c + 1) * 256], ACT.Square),
                        inc="sS")
                    sch.wait("tensor", "vS", vv_x2s)
                    sch.emit("tensor", lambda t, c=c: t.matmul(
                        pb[6][0:1, 0:256], ones_c[:, 0:1],
                        x2t.ap()[:, c * 256:(c + 1) * 256],
                        start=(c == 0), stop=(c == HC - 1)))
                    sch.wait("tensor", "sS", vs)
                    vt = sch.emit("tensor", lambda t, c=c: t.matmul(
                        pb[7][0:1, 0:256], ones_c[:, 0:1], xsqt[:],
                        start=(c == 0), stop=(c == HC - 1)), inc="tS")
                sch.wait("vector", "tS", vt)
                sch.emit("vector", lambda v, si=si: v.tensor_scalar(
                    rows["mu2"][0:1, si * 256:(si + 1) * 256], pb[6][0:1, 0:256],
                    1.0 / D, None, ALU.mult))
                vv = sch.emit("vector", lambda v, si=si: v.tensor_scalar(
                    rows["c2"][0:1, si * 256:(si + 1) * 256], pb[7][0:1, 0:256],
                    1.0 / D, None, ALU.mult), inc="vS")
                sch.bank_release("pb6", "vS", vv)
                sch.bank_release("pb7", "vS", vv)
                stats_done[0] = (vt, vs, v_st)

            ld_t = {0: ld_cols(xn1t[0], xn1_d, slice(0, 256))}
            for i in range(n_tiles):
                tok0 = i * 256
                tksl = slice(tok0, tok0 + 256)
                xb = xn1t[i % 2]
                sch.wait("sync", "vS")
                sch.wait("sync", "sS")
                sch.wait("sync", "gS")
                if i + 1 < n_tiles:
                    ld_t[i + 1] = ld_cols(xn1t[(i + 1) % 2], xn1_d,
                                          slice(tok0 + 256, tok0 + 512))
                ld_cols(xt, X, tksl)
                vXg = ld_cols(gtt, gt_d, tksl)

                bcast2(rows["rstd0"], rows["c0"], br0t, bc0t, tksl)

                sch.wait("vector", "dX", vXg)
                sch.wait("vector", "dO")
                sch.emit("vector", lambda v: v.tensor_tensor(
                    xt.ap().rearrange("p (c t) -> p c t", c=HC),
                    xt.ap().rearrange("p (c t) -> p c t", c=HC),
                    br0t.ap().unsqueeze(1).broadcast_to([128, HC, 256]), ALU.mult))
                vv_xn = sch.emit("vector", lambda v: v.tensor_tensor(
                    xt.ap().rearrange("p (c t) -> p c t", c=HC),
                    xt.ap().rearrange("p (c t) -> p c t", c=HC),
                    bc0t.ap().unsqueeze(1).broadcast_to([128, HC, 256]), ALU.add),
                    inc="vS")

                sch.wait("tensor", "dX", ld_t[i])
                ksl = kring[i % 2]
                for d in range(16):
                    bank = d % 2
                    sch.bank_acquire("tensor", f"pb{bank}")
                    for h in range(HC):
                        vt = sch.emit("tensor", lambda t, d=d, h=h, b=bank, xb=xb:
                                      t.matmul(
                                          pb[b][:, 0:256],
                                          Wqkv16.ap()[:, h * 3 * D + d * 128:
                                                      h * 3 * D + d * 128 + 128],
                                          xb.ap()[:, h * 256:(h + 1) * 256],
                                          start=(h == 0), stop=(h == HC - 1)),
                                      inc="tS" if h == HC - 1 else None)
                    dst = q16 if d < HC else ksl
                    dc = d % HC
                    sch.wait("scalar", "tS", vt)
                    if d < HC:
                        vv = sch.emit("scalar", lambda s, dc=dc, b=bank:
                                      s.activation(
                                          q16.ap()[:, dc * 256:(dc + 1) * 256],
                                          pb[b][:, 0:256], ACT.Identity,
                                          scale=0.125), inc="sS")
                    else:
                        vv = sch.emit("scalar", lambda s, dst=dst, dc=dc, b=bank:
                                      s.copy(
                                          dst.ap()[:, dc * 256:(dc + 1) * 256],
                                          pb[b][:, 0:256]), inc="sS")
                    sch.bank_release(f"pb{bank}", "sS", vv)
                for ts_ in range(2):
                    for half in range(2):
                        bank = 2 + half
                        sch.bank_acquire("tensor", f"pb{bank}")
                        for h in range(HC):
                            vt = sch.emit("tensor",
                                          lambda t, ts_=ts_, half=half, h=h, b=bank,
                                          xb=xb: t.matmul(
                                              pb[b][:, 0:512],
                                              xb.ap()[:, h * 256 + ts_ * 128:
                                                      h * 256 + ts_ * 128 + 128],
                                              Wqkv16.ap()[:, h * 3 * D + 2 * D +
                                                          half * 512:
                                                          h * 3 * D + 2 * D +
                                                          half * 512 + 512],
                                              start=(h == 0), stop=(h == HC - 1)),
                                          inc="tS" if h == HC - 1 else None)
                        vtm = vring[i % 2][ts_]
                        sch.wait("scalar", "tS", vt)
                        vv = sch.emit("scalar", lambda s, vtm=vtm, half=half, b=bank:
                                      s.copy(
                                          vtm.ap()[:, half * 512:(half + 1) * 512],
                                          pb[b][:, 0:512]), inc="sS")
                        sch.bank_release(f"pb{bank}", "sS", vv)

                if stats_pend[0] is not None:
                    ln2_stats(*stats_pend[0])
                    stats_pend[0] = None

                NK_IT = NH * 2
                DEPTH = 4
                NSL = 5

                rec_done = {}
                max_done = {}
                exp_done = {}

                def it_meta(k):
                    h = k // 2
                    qc = k % 2
                    p0_ = 64 * (h % 2)
                    hd = h // 2
                    kstart = tok0 + qc * 128 - 256
                    kchunks = [kstart + j * 128 for j in range(3)
                               if kstart + j * 128 >= 0]
                    w = 128 * len(kchunks)
                    mask = {384: mint, 256: mq1, 128: mq0}[w]
                    return h, qc, p0_, hd, kchunks, w, mask

                def attn_front(k):
                    h, qc, p0_, hd, kchunks, w, mask = it_meta(k)
                    s = k % NSL
                    sbk = (0, 1, 6)[k % 3]
                    sch.bank_acquire("tensor", f"pb{sbk}")
                    for ji, abs0 in enumerate(kchunks):
                        slot = (abs0 // 256) % 2
                        off = abs0 % 256
                        sch.emit("tensor", lambda t, ji=ji, slot=slot, off=off,
                                 sbk=sbk, hd=hd, p0_=p0_, qc=qc: t.matmul(
                                     pb[sbk][:, ji * 128:(ji + 1) * 128],
                                     q16.ap()[p0_:p0_ + 64,
                                              hd * 256 + qc * 128:
                                              hd * 256 + qc * 128 + 128],
                                     kring[slot].ap()[p0_:p0_ + 64,
                                                      hd * 256 + off:
                                                      hd * 256 + off + 128],
                                     start=(ji == 0), stop=False))
                    vt = sch.emit("tensor", lambda t, sbk=sbk, w=w, mask=mask:
                                  t.matmul(pb[sbk][:, 0:w], id16[:],
                                           mask.ap()[:, 0:w],
                                           start=False, stop=True), inc="tS")
                    sch.wait("vector", "tS", vt)
                    max_done[k] = sch.emit("vector", lambda v, sbk=sbk, w=w, s=s:
                                           v.reduce_max(nmax[:, s:s + 1],
                                                        pb[sbk][:, 0:w],
                                                        axis=AX.X, negate=True),
                                           inc="vS")

                def attn_exp(k):
                    h, qc, p0_, hd, kchunks, w, mask = it_meta(k)
                    s = k % NSL
                    sbk = (0, 1, 6)[k % 3]
                    sch.wait("scalar", "vS", max_done[k])
                    vs = sch.emit("scalar", lambda sc, sbk=sbk, w=w, s=s:
                                  sc.activation(E16.ap()[:, s * 384:s * 384 + w],
                                                pb[sbk][:, 0:w], ACT.Exp,
                                                bias=nmax[:, s:s + 1],
                                                accum_out=denom[:, 2 * s:2 * s + 1]),
                                  inc="sS")
                    sch.bank_release(f"pb{sbk}", "sS", vs)
                    exp_done[k] = vs

                def attn_rec(k):
                    s = k % NSL
                    sch.wait("vector", "sS", exp_done[k])
                    rec_done[k] = sch.emit("vector", lambda v, s=s: v.reciprocal(
                        denom[:, 2 * s + 1:2 * s + 2], denom[:, 2 * s:2 * s + 1]),
                        inc="vS")

                def attn_back(k):
                    h, qc, p0_, hd, kchunks, w, mask = it_meta(k)
                    s = k % NSL
                    nk = len(kchunks)
                    sch.wait("scalar", "vS", rec_done[k])
                    vv = sch.emit("scalar", lambda sc, w=w, s=s: sc.mul(
                        P16.ap()[:, s * 384:s * 384 + w],
                        E16.ap()[:, s * 384:s * 384 + w],
                        denom[:, 2 * s + 1:2 * s + 2]), inc="sS")
                    ptb = 2 + (k % 2)
                    sch.bank_acquire("tensor", f"pb{ptb}")
                    sch.wait("tensor", "sS", vv)
                    for j in range(nk):
                        vt = sch.emit("tensor", lambda t, j=j, ptb=ptb, s=s:
                                      t.matmul(pb[ptb][:, j * 128:(j + 1) * 128],
                                               P16.ap()[:, s * 384 + j * 128:
                                                        s * 384 + (j + 1) * 128],
                                               id16[:], start=True, stop=True),
                                      inc="tS" if j == nk - 1 else None)
                    sch.wait("scalar", "tS", vt)
                    vv = sch.emit("scalar", lambda sc, ptb=ptb, w=w, s=s:
                                  sc.copy(PT16.ap()[:, s * 384:s * 384 + w],
                                          pb[ptb][:, 0:w]), inc="sS")
                    sch.bank_release(f"pb{ptb}", "sS", vv)
                    pvb = 4 + (hd % 2)
                    if k % 4 == 0:
                        sch.bank_acquire("tensor", f"pb{pvb}")
                    sch.wait("tensor", "sS", vv)
                    for j, abs0 in enumerate(kchunks):
                        slot = (abs0 // 256) % 2
                        ts_ = (abs0 % 256) // 128
                        last_of_pair = (j == nk - 1 and k % 4 == 3)
                        vt = sch.emit("tensor", lambda t, j=j, slot=slot, ts_=ts_,
                                      h=h, qc=qc, pvb=pvb, p0_=p0_, nk=nk, s=s:
                                      t.matmul(
                                          pb[pvb][p0_:p0_ + 64,
                                                  qc * 128:qc * 128 + 128],
                                          vring[slot][ts_].ap()[:, 64 * h:
                                                                64 * h + 64],
                                          PT16.ap()[:, s * 384 + j * 128:
                                                    s * 384 + (j + 1) * 128],
                                          start=(j == 0), stop=(j == nk - 1),
                                          tile_position=(0, p0_)),
                                      inc="tS" if last_of_pair else None)
                    if k % 4 == 3:
                        sch.wait("vector", "tS", vt)
                        vv = sch.emit("vector", lambda v, hd=hd, pvb=pvb:
                                      v.tensor_copy(
                                          o16.ap()[:, hd * 256:(hd + 1) * 256],
                                          pb[pvb][:, 0:256]), inc="vS")
                        sch.bank_release(f"pb{pvb}", "vS", vv)

                for step in range(NK_IT + DEPTH):
                    if step >= DEPTH:
                        attn_back(step - DEPTH)
                    if 1 <= step < NK_IT + 1:
                        attn_exp(step - 1)
                    if 2 <= step < NK_IT + 2:
                        attn_rec(step - 2)
                    if step < NK_IT:
                        attn_front(step)
                vv_o16 = sch.cnt["vS"]

                if i == 0:
                    sch.wait("tensor", "vS", v_scan)
                sch.wait("tensor", "vS", vv_o16)
                sch.wait("gpsimd", "dX", vXg)
                sch.wait("gpsimd", "vS", vv_xn)
                if stats_done[0] is not None:
                    svt, svs, sst = stats_done[0]
                    sch.wait("vector", "tS", svt)
                    sch.wait("vector", "sS", svs)
                    sch.wait("vector", "dO", sst)
                    stats_done[0] = None
                vv_prev_comb = None
                for d in range(HC):
                    bko = 6 + (d % 2)
                    sch.bank_acquire("tensor", f"pb{bko}")
                    for h in range(HC):
                        sch.emit("tensor", lambda t, d=d, h=h, b=bko: t.matmul(
                            pb[b][:, 0:256],
                            WO16.ap()[:, h * D + d * 128:h * D + d * 128 + 128],
                            o16.ap()[:, h * 256:(h + 1) * 256],
                            start=(h == 0), stop=(h == HC - 1)))
                    bky = 4 + (d % 2)
                    sch.bank_acquire("tensor", f"pb{bky}")
                    vt = sch.emit("tensor", lambda t, d=d, b=bky, tok0=tok0:
                                  t.matmul(pb[b][:, 0:256],
                                           Cw16.ap()[:, d * 128:(d + 1) * 128],
                                           st16.ap()[:, tok0:tok0 + 256],
                                           start=True, stop=True), inc="tS")
                    sch.wait("scalar", "tS", vt)
                    sch.wait("scalar", "gS")
                    if vv_prev_comb is not None:
                        sch.wait("scalar", "vS", vv_prev_comb)
                    vs = sch.emit("scalar", lambda s, d=d, b=bko: s.activation(
                        a_c[:], pb[b][:, 0:256], ACT.Identity, bias=vbO[:, d:d + 1]),
                        inc="sS")
                    sch.bank_release(f"pb{bko}", "sS", vs)
                    vs = sch.emit("scalar", lambda s, b=bky: s.copy(
                        y_c[:], pb[b][:, 0:256]), inc="sS")
                    sch.bank_release(f"pb{bky}", "sS", vs)
                    if "tap_attn" in taps:
                        sch.wait("sync", "sS", vs)
                        sch.emit("sync", lambda s, d=d, tksl=tksl: s.dma_start(
                            out=taps["tap_attn"][d * 128:(d + 1) * 128, tksl],
                            in_=a_c[:]), inc="dO", n=16)
                        sch.emit("sync", lambda s, d=d, tksl=tksl: s.dma_start(
                            out=taps["tap_yssm"][d * 128:(d + 1) * 128, tksl],
                            in_=y_c[:]), inc="dO", n=16)
                        sch.wait("gpsimd", "dO")
                    sch.wait("gpsimd", "sS", vs)
                    sch.emit("gpsimd", lambda g: g.tensor_tensor(
                        a_c[:], a_c[:], y_c[:], ALU.subtract))
                    vg = sch.emit("gpsimd", lambda g, d=d: g.tensor_tensor(
                        y_c[:], xt.ap()[:, d * 256:(d + 1) * 256], y_c[:], ALU.add),
                        inc="gS")
                    sch.wait("vector", "gS", vg)
                    sch.emit("vector", lambda v, d=d: v.tensor_tensor(
                        a_c[:], gtt.ap()[:, d * 256:(d + 1) * 256], a_c[:],
                        ALU.mult))
                    vv = sch.emit("vector", lambda v, d=d: v.tensor_tensor(
                        x2t.ap()[:, d * 256:(d + 1) * 256], y_c[:], a_c[:],
                        ALU.add), inc="vS")
                    vv_prev_comb = vv
                vv_x2 = sch.cnt["vS"]
                stats_pend[0] = (i, vv_x2)

            ln2_stats(*stats_pend[0])
            stats_pend[0] = None

            sch.emit("vector", lambda v: v.tensor_tensor(
                tmpArow[0:1, :], rows["mu2"][0:1, :], rows["mu2"][0:1, :], ALU.mult))
            sch.emit("vector", lambda v: v.tensor_tensor(
                tmpArow[0:1, :], rows["c2"][0:1, :], tmpArow[0:1, :], ALU.subtract))
            vv = sch.emit("vector", lambda v: v.tensor_scalar(
                tmpArow[0:1, :], tmpArow[0:1, :], EPS, None, ALU.add), inc="vS")
            sch.wait("scalar", "vS", vv)
            sch.emit("scalar", lambda s: s.activation(
                rows["rstd2"][0:1, :], tmpArow[0:1, :], ACT.Ln))
            vs = sch.emit("scalar", lambda s: s.activation(
                tmpArow[0:1, :], rows["rstd2"][0:1, :], ACT.Exp, scale=-0.5),
                inc="sS")
            sch.wait("vector", "sS", vs)
            sch.emit("vector", lambda v: v.tensor_copy(
                rows["rstd2"][0:1, :], tmpArow[0:1, :]))
            sch.emit("vector", lambda v: v.scalar_tensor_tensor(
                rows["c2"][0:1, :], rows["mu2"][0:1, :], -1.0, rows["rstd2"][0:1, :],
                ALU.mult, ALU.mult))

            if "tap_x2" in taps:
                sch.wait("sync", "dO")
                sch.emit("sync", lambda s: s.dma_start(
                    out=taps["tap_x2"][:, :], in_=x2_d[:, :]), inc="dO", n=16)

            sch.barrier()

        spanB.close()

        with ExitStack() as p2:
            e2 = p2.enter_context
            W116 = e2(nc.sbuf_tensor("W116", [128, HC * (DFF // 2)], BF16))
            W216 = e2(nc.sbuf_tensor("W216", [128, (DFF // 256) * D], BF16))
            x2b = [e2(nc.sbuf_tensor(f"x2b{j}", [128, HC * 256], F32))
                   for j in range(3)]
            tmp2 = e2(nc.sbuf_tensor("tmp2", [128, HC * 256], F32))
            h16 = [e2(nc.sbuf_tensor(f"h16{j}", [128, HC * 256], BF16))
                   for j in range(2)]
            hid16 = [e2(nc.sbuf_tensor(f"hid16{j}", [128, (DFF // 256) * 256], BF16))
                     for j in range(2)]
            s_t = e2(nc.sbuf_tensor("s_t", [128, HC * 256], F32))
            sprev = [e2(nc.sbuf_tensor(f"sprev{j}", [128, HC * 256], F32))
                     for j in range(3)]
            br2 = [e2(nc.sbuf_tensor(f"br2{j}", [128, 256], F32)) for j in range(2)]
            bc2 = [e2(nc.sbuf_tensor(f"bc2{j}", [128, 256], F32)) for j in range(2)]

            DH2 = DFF // 2
            NDC = DH2 // 128
            NTt = T // 256
            for half in range(2):
                if half == 0:
                    sch.wait("gpsimd", "tS")
                    sch.wait("gpsimd", "dW", v_preconv)
                    wld(W116.ap().rearrange("p (c j) -> p c j", c=HC),
                        w1b_d[:, half * DH2:(half + 1) * DH2].rearrange(
                            "(c p) j -> p c j", p=128))
                    w_p2 = wld(W216.ap().rearrange("p (c j) -> p c j", c=NDC),
                               w2b_d[half * DH2:(half + 1) * DH2, :].rearrange(
                                   "(c p) j -> p c j", p=128))

                end_V = {}
                end_H = {}
                end_W1 = {}
                end_G = {}
                end_S = {}
                end_ST = {}

                def LD(i):
                    j = i % 2
                    j3 = i % 3
                    tksl = slice(i * 256, (i + 1) * 256)
                    if i - 3 in end_S:
                        sch.wait("sync", "vS", end_S[i - 3])
                    if i - 2 in end_H:
                        sch.wait("sync", "sS", end_H[i - 2])
                    ld_cols(x2b[j3], x2_d, tksl)
                    vX = sch.cnt["dX"]
                    if half == 1:
                        vX = ld_cols(sprev[j3], s_d, tksl)
                    bcast2(rows["rstd2"], rows["c2"], br2[j], bc2[j], tksl)
                    return vX

                def V(i, vX):
                    j = i % 2
                    j3 = i % 3
                    sch.wait("vector", "dX", vX)
                    if i - 1 in end_H:
                        sch.wait("vector", "sS", end_H[i - 1])
                    sch.emit("vector", lambda v, j=j, j3=j3: v.tensor_tensor(
                        tmp2.ap().rearrange("p (c t) -> p c t", c=HC),
                        x2b[j3].ap().rearrange("p (c t) -> p c t", c=HC),
                        br2[j].ap().unsqueeze(1).broadcast_to([128, HC, 256]),
                        ALU.mult))
                    vv = sch.emit("vector", lambda v, j=j: v.tensor_tensor(
                        tmp2.ap().rearrange("p (c t) -> p c t", c=HC),
                        tmp2.ap().rearrange("p (c t) -> p c t", c=HC),
                        bc2[j].ap().unsqueeze(1).broadcast_to([128, HC, 256]),
                        ALU.add), inc="vS")
                    end_V[i] = vv

                def H(i):
                    j = i % 2
                    sch.wait("scalar", "vS", end_V[i])
                    if i - 2 in end_W1:
                        sch.wait("scalar", "tS", end_W1[i - 2])
                    for c in range(HC):
                        sch.emit("scalar", lambda s, c=c, j=j: s.activation(
                            h16[j].ap()[:, c * 256:(c + 1) * 256],
                            tmp2.ap()[:, c * 256:(c + 1) * 256], ACT.Identity,
                            bias=vb2[:, c:c + 1], scale=vg2[:, c:c + 1]), inc="sS")
                    end_H[i] = sch.cnt["sS"]

                def W1G(i):
                    j = i % 2
                    sch.wait("tensor", "sS", end_H[i])
                    if i == 0:
                        sch.wait("tensor", "dW", w_p2)
                    for d in range(NDC):
                        bank = d % 2
                        sch.bank_acquire("tensor", f"pb{bank}")
                        for h in range(HC):
                            vt = sch.emit("tensor", lambda t, d=d, h=h, b=bank, j=j:
                                          t.matmul(
                                              pb[b][:, 0:256],
                                              W116.ap()[:, h * DH2 + d * 128:
                                                        h * DH2 + d * 128 + 128],
                                              h16[j].ap()[:, h * 256:(h + 1) * 256],
                                              start=(h == 0), stop=(h == HC - 1)),
                                          inc="tS" if h == HC - 1 else None)
                        sch.wait("scalar", "tS", vt)
                        vs = sch.emit("scalar", lambda s, d=d, b=bank, half=half,
                                      j=j: s.activation(
                                          hid16[j].ap()[:, d * 256:(d + 1) * 256],
                                          pb[b][:, 0:256], ACT.Gelu,
                                          bias=vb1m[:, half * NDC + d:
                                                    half * NDC + d + 1]), inc="sS")
                        sch.bank_release(f"pb{bank}", "sS", vs)
                    end_W1[i] = sch.cnt["tS"]
                    end_G[i] = sch.cnt["sS"]

                def W2S(i):
                    j = i % 2
                    j3 = i % 3
                    sch.wait("tensor", "sS", end_G[i])
                    if i - 1 in end_ST:
                        sch.wait("vector", "dO", end_ST[i - 1])
                    for d in range(HC):
                        bank = 2 + (d % 2)
                        sch.bank_acquire("tensor", f"pb{bank}")
                        for h in range(NDC):
                            vt = sch.emit("tensor", lambda t, d=d, h=h, b=bank, j=j:
                                          t.matmul(
                                              pb[b][:, 0:256],
                                              W216.ap()[:, h * D + d * 128:
                                                        h * D + d * 128 + 128],
                                              hid16[j].ap()[:, h * 256:
                                                            (h + 1) * 256],
                                              start=(h == 0), stop=(h == NDC - 1)),
                                          inc="tS" if h == NDC - 1 else None)
                        sch.wait("vector", "tS", vt)
                        if half == 0:
                            vv = sch.emit("vector", lambda v, d=d, b=bank, j3=j3:
                                          v.tensor_tensor(
                                              s_t.ap()[:, d * 256:(d + 1) * 256],
                                              pb[b][:, 0:256],
                                              x2b[j3].ap()[:, d * 256:
                                                           (d + 1) * 256],
                                              ALU.add), inc="vS")
                        else:
                            vv = sch.emit("vector", lambda v, d=d, b=bank, j3=j3:
                                          v.scalar_tensor_tensor(
                                              s_t.ap()[:, d * 256:(d + 1) * 256],
                                              pb[b][:, 0:256], vb2m[:, d:d + 1],
                                              sprev[j3].ap()[:, d * 256:
                                                             (d + 1) * 256],
                                              ALU.add, ALU.add), inc="vS")
                        sch.bank_release(f"pb{bank}", "vS", vv)
                    end_S[i] = sch.cnt["vS"]

                def ST(i):
                    tksl = slice(i * 256, (i + 1) * 256)
                    sch.wait("sync", "vS", end_S[i])
                    st_cols(s_t, s_d, tksl)
                    end_ST[i] = sch.cnt["dO"]

                vX0 = LD(0)
                V(0, vX0)
                H(0)
                for i in range(NTt):
                    W1G(i)
                    if i + 1 < NTt:
                        vX = LD(i + 1)
                        V(i + 1, vX)
                        H(i + 1)
                    if i >= 1:
                        W2S(i - 1)
                        ST(i - 1)
                W2S(NTt - 1)
                ST(NTt - 1)
                if half == 0:
                    sch.wait("gpsimd", "tS", end_W1[NTt - 1])
                    wld(W116.ap().rearrange("p (c j) -> p c j", c=HC),
                        w1b_d[:, DH2:2 * DH2].rearrange("(c p) j -> p c j", p=128))
                    sch.wait("gpsimd", "tS", sch.cnt["tS"])
                    w_p2 = wld(W216.ap().rearrange("p (c j) -> p c j", c=NDC),
                               w2b_d[DH2:2 * DH2, :].rearrange(
                                   "(c p) j -> p c j", p=128))

            if "tap_s" in taps:
                sch.wait("sync", "dO")
                sch.emit("sync", lambda s: s.dma_start(
                    out=taps["tap_s"][:, :], in_=s_d[:, :]), inc="dO", n=16)

            sch.barrier()

        with ExitStack() as p3:
            e3 = p3.enter_context
            TH = T // 2
            gateT16 = e3(nc.sbuf_tensor("gateT16", [128, HC * NE], BF16))
            cw_fm = e3(nc.sbuf_tensor("cw_fm", [NE, T], F32))

            w_gate = wld(*chunked(gateT16, gateT, HC))

            with ExitStack() as pr:
                er = pr.enter_context
                NTL = T // 128
                sc_fm = er(nc.sbuf_tensor("sc_fm", [NE, T], F32))
                sfc_fm = er(nc.sbuf_tensor("sfc_fm", [NE, T], F32))
                sc_tm = er(nc.sbuf_tensor("sc_tm", [128, NTL * NE], F32))
                sfc_tm = er(nc.sbuf_tensor("sfc_tm", [128, NTL * NE], F32))
                gsc = er(nc.sbuf_tensor("gsc", [128, NTL * 4], F32))
                gth = er(nc.sbuf_tensor("gth", [128, NTL], F32))
                emask = er(nc.sbuf_tensor("emask", [128, NTL * NE],
                                          mybir.dt.uint8))
                neg30 = er(nc.sbuf_tensor("neg30", [128, NTL * NE], F32))
                maskd = er(nc.sbuf_tensor("maskd", [128, NTL * NE], F32))
                sel = er(nc.sbuf_tensor("sel", [128, NTL * NE], F32))
                eqm = er(nc.sbuf_tensor("eqm", [128, NTL * NE], F32))
                mrow = er(nc.sbuf_tensor("mrow", [128, NTL], F32))
                sumw = er(nc.sbuf_tensor("sumw", [128, 2 * NTL], F32))
                cw_tm = er(nc.sbuf_tensor("cw_tm", [128, NTL * NE], F32))

                xng = er(nc.sbuf_tensor("xng", [128, HC * 512], BF16))
                sch.wait("tensor", "dW", w_gate)
                for gh in range(NT):
                    sch.wait("sync", "tS")
                    v_xn = ld_cols(xng, xn16_d, slice(gh * 512, (gh + 1) * 512))
                    sch.wait("tensor", "dX", v_xn)
                    for nt in range(1):
                        bank = gh % 2
                        sch.bank_acquire("tensor", f"pb{bank}")
                        for h in range(HC):
                            vt = sch.emit("tensor",
                                          lambda t, h=h, b=bank: t.matmul(
                                              pb[b][0:NE, 0:512],
                                              gateT16.ap()[:, h * NE:(h + 1) * NE],
                                              xng.ap()[:, h * 512:(h + 1) * 512],
                                              start=(h == 0), stop=(h == HC - 1)),
                                          inc="tS" if h == HC - 1 else None)
                        sch.wait("scalar", "tS", vt)
                        vs = sch.emit("scalar", lambda s, b=bank, gh=gh:
                                      s.activation(
                                          sc_fm.ap()[:, gh * 512:(gh + 1) * 512],
                                          pb[b][0:NE, 0:512], ACT.Sigmoid),
                                      inc="sS")
                        sch.bank_release(f"pb{bank}", "sS", vs)
                v_sc = sch.cnt["sS"]
                sch.wait("vector", "sS", v_sc)
                vv = sch.emit("vector", lambda v: v.tensor_scalar(
                    sfc_fm[:], sc_fm[:], vgb[:, 0:1], None, ALU.add), inc="vS")

                sch.wait("tensor", "vS", vv)
                for it in range(NTL):
                    bank = it % 2
                    sch.bank_acquire("tensor", f"pb{bank}")
                    sch.emit("tensor", lambda t, it=it, b=bank: t.matmul(
                        pb[b][:, 0:NE], sc_fm.ap()[:, it * 128:(it + 1) * 128],
                        id32[0:NE, 0:NE], start=True, stop=True))
                    vt = sch.emit("tensor", lambda t, it=it, b=bank: t.matmul(
                        pb[b][:, NE:2 * NE], sfc_fm.ap()[:, it * 128:(it + 1) * 128],
                        id32[0:NE, 0:NE], start=True, stop=True), inc="tS")
                    sch.wait("vector", "tS", vt)
                    sch.emit("vector", lambda v, it=it, b=bank: v.tensor_copy(
                        sc_tm.ap()[:, it * NE:(it + 1) * NE], pb[b][:, 0:NE]))
                    vv = sch.emit("vector", lambda v, it=it, b=bank: v.tensor_copy(
                        sfc_tm.ap()[:, it * NE:(it + 1) * NE], pb[b][:, NE:2 * NE]),
                        inc="vS")
                    sch.bank_release(f"pb{bank}", "vS", vv)

                Sf = sfc_tm.ap()
                G64 = gsc.ap()
                qtmp = eqm.ap()[:, 128:192]
                tmin = eqm.ap()[:, 0:128]
                gm64 = neg30.ap()[:, 0:64]
                M3 = maskd.ap().rearrange("p (tl e) -> p tl e", tl=NTL)

                firstpair = True
                for aa in range(4):
                    for bb in range(aa + 1, 4):
                        sch.emit("vector", lambda v, aa=aa, bb=bb: v.tensor_tensor(
                            qtmp, Sf[:, aa::4], Sf[:, bb::4], ALU.add))
                        if firstpair:
                            sch.emit("vector", lambda v: v.tensor_copy(G64, qtmp))
                            firstpair = False
                        else:
                            sch.emit("vector", lambda v: v.tensor_tensor(
                                G64, G64, qtmp, ALU.max))
                sch.vsync()
                sch.emit("vector", lambda v: v.memset(tmin, NEG))
                sch.vsync()
                kk = 0
                for aa in range(4):
                    for bb in range(aa + 1, 4):
                        sch.emit("vector", lambda v, aa=aa, bb=bb, kk=kk:
                                 v.tensor_tensor(tmin[:, kk::8], G64[:, aa::4],
                                                 G64[:, bb::4], ALU.min))
                        kk += 1
                sch.vsync()
                sch.emit("vector", lambda v: v.tensor_reduce(
                    gth.ap().unsqueeze(2)[:, :, 0:1],
                    tmin.rearrange("p (tl k) -> p tl k", tl=NTL), AX.X, ALU.max))
                sch.vsync()
                for g in range(4):
                    sch.emit("vector", lambda v, g=g: v.tensor_tensor(
                        gm64[:, g::4], G64[:, g::4], gth.ap()[:, :], ALU.is_ge))
                sch.vsync()
                for ee in range(NE):
                    sch.emit("vector", lambda v, ee=ee: v.tensor_copy(
                        emask.ap()[:, ee::NE], gm64[:, (ee // 4)::4]))
                sch.vsync()
                sch.emit("vector", lambda v: v.memset(neg30[:], NEG))
                sch.vsync()
                sch.emit("vector", lambda v: v.select(
                    maskd[:], emask[:], sfc_tm[:], neg30[:]))
                sch.emit("vector", lambda v: v.memset(sel[:], 0.0))
                for k in range(4):
                    sch.emit("vector", lambda v: v.tensor_reduce(
                        mrow.ap().unsqueeze(2)[:, :, 0:1], M3, AX.X, ALU.max))
                    sch.vsync()
                    for ee in range(NE):
                        sch.emit("vector", lambda v, ee=ee: v.tensor_tensor(
                            eqm.ap()[:, ee::NE], maskd.ap()[:, ee::NE],
                            mrow.ap()[:, :], ALU.is_equal))
                    sch.vsync()
                    sch.emit("vector", lambda v: v.tensor_tensor(
                        sel[:], sel[:], eqm[:], ALU.add))
                    if k < 3:
                        sch.emit("vector", lambda v: v.scalar_tensor_tensor(
                            maskd[:], eqm[:], -2.0e30, maskd[:], ALU.mult, ALU.add))
                sch.emit("vector", lambda v: v.tensor_tensor(
                    cw_tm[:], sel[:], sc_tm[:], ALU.mult))
                CW3 = cw_tm.ap().rearrange("p (tl e) -> p tl e", tl=NTL)
                sch.emit("vector", lambda v: v.tensor_reduce(
                    sumw.ap()[:, 0:NTL].unsqueeze(2)[:, :, 0:1], CW3, AX.X, ALU.add))
                sch.vsync()
                sch.emit("vector", lambda v: v.tensor_scalar(
                    sumw.ap()[:, 0:NTL], sumw.ap()[:, 0:NTL], 1e-20, None, ALU.add))
                sch.vsync()
                sch.emit("vector", lambda v: v.reciprocal(
                    sumw.ap()[:, NTL:2 * NTL], sumw.ap()[:, 0:NTL]))
                sch.vsync()
                sch.emit("vector", lambda v: v.tensor_scalar(
                    sumw.ap()[:, NTL:2 * NTL], sumw.ap()[:, NTL:2 * NTL], 2.5, None,
                    ALU.mult))
                sch.vsync()
                vv = None
                for ee in range(NE):
                    vv = sch.emit("vector", lambda v, ee=ee: v.tensor_tensor(
                        cw_tm.ap()[:, ee::NE], cw_tm.ap()[:, ee::NE],
                        sumw.ap()[:, NTL:2 * NTL], ALU.mult),
                        inc="vS" if ee == NE - 1 else None)
                sch.wait("vector", "vS", vv)

                sch.wait("tensor", "vS", vv)
                for it in range(NTL):
                    bank = it % 2
                    sch.bank_acquire("tensor", f"pb{bank}")
                    vt = sch.emit("tensor", lambda t, it=it, b=bank: t.matmul(
                        pb[b][0:NE, 0:128], cw_tm.ap()[:, it * NE:(it + 1) * NE],
                        id32[:, 0:128], start=True, stop=True), inc="tS")
                    sch.wait("vector", "tS", vt)
                    vv = sch.emit("vector", lambda v, it=it, b=bank: v.tensor_copy(
                        cw_fm.ap()[:, it * 128:(it + 1) * 128], pb[b][0:NE, 0:128]),
                        inc="vS")
                    sch.bank_release(f"pb{bank}", "vS", vv)

                if "tap_sel" in taps:
                    sch.wait("sync", "vS")
                    for nm, buf_ in (("tap_scfm", sc_fm), ("tap_sctm", sc_tm),
                                     ("tap_sfctm", sfc_tm), ("tap_gsc", gsc),
                                     ("tap_sel", sel), ("tap_cwtm", cw_tm)):
                        if nm in taps:
                            sch.emit("sync", lambda s, nm=nm, b=buf_: s.dma_start(
                                out=taps[nm][:, :], in_=b[:]), inc="dO", n=16)
                sch.wait("sync", "vS")
                sch.emit("sync", lambda s: s.dma_start(
                    out=cw_d[:, :], in_=cw_fm[:]), inc="dO", n=16)
                if "tap_cw" in taps:
                    sch.wait("sync", "dO")
                    sch.emit("sync", lambda s: s.dma_start(
                        out=taps["tap_cw"][:, :], in_=cw_d[:, :]), inc="dO", n=16)

            xn16s = e3(nc.sbuf_tensor("xn16s", [128, HC * TH], BF16))
            ybuf = e3(nc.sbuf_tensor("ybuf", [128, HC * TH], F32))
            Wg16 = [e3(nc.sbuf_tensor(f"Wg16_{i}", [128, HC * IE], BF16))
                    for i in range(2)]
            Wu16 = [e3(nc.sbuf_tensor(f"Wu16_{i}", [128, HC * IE], BF16))
                    for i in range(2)]
            Wd16 = [e3(nc.sbuf_tensor(f"Wd16_{i}", [128, (IE // 128) * D], BF16))
                    for i in range(2)]
            silu16 = e3(nc.sbuf_tensor("silu16", [128, (IE // 128) * 256], BF16))
            hu16 = e3(nc.sbuf_tensor("hu16", [128, (IE // 128) * 256], BF16))
            act16 = [e3(nc.sbuf_tensor(f"act16_{i}", [128, (IE // 128) * 256],
                                       BF16)) for i in range(2)]
            cwb16 = e3(nc.sbuf_tensor("cwb16", [128, 256], BF16))
            cwrow = [e3(nc.sbuf_tensor(f"cwrow{i}", [1, TH], F32)) for i in range(2)]
            Xf = e3(nc.sbuf_tensor("Xf", [128, HC * 256], F32))
            sf = e3(nc.sbuf_tensor("sf", [128, HC * 256], F32))
            gf = e3(nc.sbuf_tensor("gf", [128, HC * 256], F32))
            yf = e3(nc.sbuf_tensor("yf", [128, HC * 256], F32))
            brf = e3(nc.sbuf_tensor("brf", [128, TH], F32))
            bcf = e3(nc.sbuf_tensor("bcf", [128, TH], F32))
            EXL = NE + 2
            GROUP = 1
            NTC = TH // 256

            def eld(buf_i, eidx):
                if eidx < NE:
                    wld(*chunked(Wg16[buf_i], WgT[eidx], HC))
                    wld(*chunked(Wu16[buf_i], WuT[eidx], HC))
                    return wld(*chunked(Wd16[buf_i], WdT[eidx], IE // 128))
                hh = eidx - NE
                wld(*chunked(Wg16[buf_i], WsgT[:, hh * IE:(hh + 1) * IE], HC))
                wld(*chunked(Wu16[buf_i], WsuT[:, hh * IE:(hh + 1) * IE], HC))
                return wld(*chunked(Wd16[buf_i], WsdT[hh * IE:(hh + 1) * IE, :],
                                    IE // 128))

            def expert_front(half0, eidx, tc, slot, buf, v_cw, prev_act):
                tok0 = tc * 256
                if eidx < NE:
                    sch.bank_acquire("tensor", "pb6")
                    if tc == 0:
                        sch.wait("tensor", "dX", v_cw)
                    vt = sch.emit("tensor", lambda t, buf=buf, tok0=tok0:
                                  t.matmul(pb[6][:, 0:256], ones_r[0:1, :],
                                           cwrow[buf].ap()[0:1, tok0:tok0 + 256],
                                           start=True, stop=True), inc="tS")
                    sch.wait("vector", "tS", vt)
                    vv = sch.emit("vector", lambda v: v.tensor_copy(
                        cwb16[:], pb[6][:, 0:256]), inc="vS")
                    sch.bank_release("pb6", "vS", vv)
                if prev_act is not None:
                    sch.wait("scalar", "vS", prev_act)
                for ic in range(IE // 128):
                    bank = 4 + (ic % 2)
                    sch.bank_acquire("tensor", f"pb{bank}")
                    for h in range(HC):
                        vt = sch.emit("tensor", lambda t, ic=ic, h=h, b=bank,
                                      buf=buf, tok0=tok0: t.matmul(
                                          pb[b][:, 0:256],
                                          Wg16[buf].ap()[:, h * IE + ic * 128:
                                                         h * IE + ic * 128 + 128],
                                          xn16s.ap()[:, h * TH + tok0:
                                                     h * TH + tok0 + 256],
                                          start=(h == 0), stop=(h == HC - 1)),
                                      inc="tS" if h == HC - 1 else None)
                    sch.wait("scalar", "tS", vt)
                    vs = sch.emit("scalar", lambda s, ic=ic, b=bank: s.activation(
                        silu16.ap()[:, ic * 256:(ic + 1) * 256], pb[b][:, 0:256],
                        ACT.Silu), inc="sS")
                    sch.bank_release(f"pb{bank}", "sS", vs)
                for ic in range(IE // 128):
                    bank = 6 + (ic % 2)
                    sch.bank_acquire("tensor", f"pb{bank}")
                    for h in range(HC):
                        vt = sch.emit("tensor", lambda t, ic=ic, h=h, b=bank,
                                      buf=buf, tok0=tok0: t.matmul(
                                          pb[b][:, 0:256],
                                          Wu16[buf].ap()[:, h * IE + ic * 128:
                                                         h * IE + ic * 128 + 128],
                                          xn16s.ap()[:, h * TH + tok0:
                                                     h * TH + tok0 + 256],
                                          start=(h == 0), stop=(h == HC - 1)),
                                      inc="tS" if h == HC - 1 else None)
                    sch.wait("vector", "tS", vt)
                    if eidx < NE:
                        vv = sch.emit("vector", lambda v, ic=ic, b=bank:
                                      v.scalar_tensor_tensor(
                                          hu16.ap()[:, ic * 256:(ic + 1) * 256],
                                          pb[b][:, 0:256], 1.0, cwb16[:],
                                          ALU.mult, ALU.mult), inc="vS")
                    else:
                        vv = sch.emit("vector", lambda v, ic=ic, b=bank:
                                      v.tensor_copy(
                                          hu16.ap()[:, ic * 256:(ic + 1) * 256],
                                          pb[b][:, 0:256]), inc="vS")
                    sch.bank_release(f"pb{bank}", "vS", vv)
                sch.wait("vector", "sS")
                return sch.emit("vector", lambda v, slot=slot: v.tensor_tensor(
                    act16[slot][:], silu16[:], hu16[:], ALU.mult), inc="vS")

            def expert_back(eidx, tc, slot, buf, vv_act):
                tok0 = tc * 256
                sch.wait("tensor", "vS", vv_act)
                for d in range(HC):
                    bank = d // 2
                    hfb = d % 2
                    if hfb == 0:
                        sch.bank_acquire("tensor", f"pb{bank}")
                    for icc in range(IE // 128):
                        last = (icc == IE // 128 - 1 and d == HC - 1)
                        vt = sch.emit("tensor", lambda t, d=d, icc=icc, b=bank,
                                      hfb=hfb, buf=buf, slot=slot: t.matmul(
                                          pb[b][:, hfb * 256:hfb * 256 + 256],
                                          Wd16[buf].ap()[:, icc * D + d * 128:
                                                         icc * D + d * 128 + 128],
                                          act16[slot].ap()[:, icc * 256:
                                                           (icc + 1) * 256],
                                          start=(icc == 0),
                                          stop=(icc == IE // 128 - 1),
                                          skip_group_check=True),
                                      inc="tS" if last else None)
                sch.wait("vector", "tS", vt)
                for d in range(HC):
                    bank = d // 2
                    hfb = d % 2
                    if eidx == 0:
                        vv = sch.emit("vector", lambda v, d=d, b=bank, hfb=hfb,
                                      tok0=tok0: v.tensor_copy(
                                          ybuf.ap()[:, d * TH + tok0:
                                                    d * TH + tok0 + 256],
                                          pb[b][:, hfb * 256:hfb * 256 + 256]),
                                      inc="vS")
                    else:
                        vv = sch.emit("vector", lambda v, d=d, b=bank, hfb=hfb,
                                      tok0=tok0: v.tensor_tensor(
                                          ybuf.ap()[:, d * TH + tok0:
                                                    d * TH + tok0 + 256],
                                          ybuf.ap()[:, d * TH + tok0:
                                                    d * TH + tok0 + 256],
                                          pb[b][:, hfb * 256:hfb * 256 + 256],
                                          ALU.add), inc="vS")
                    if hfb == 1:
                        sch.bank_release(f"pb{bank}", "vS", vv)

            def expert_loop(half0, v_xn, w_e, v_cw0, jobs=()):
                jobs = list(jobs)
                sch.wait("tensor", "dX", v_xn)
                pipe = []
                v_cw = {0: v_cw0}
                gstep = 0
                prev_act = None
                for eidx in range(EXL):
                    buf = eidx % 2
                    if eidx + 1 < NE:
                        sch.wait("sync", "tS")
                        v_cw[eidx + 1] = ld(cwrow[(eidx + 1) % 2][0:1, :],
                                            cw_d[eidx + 1:eidx + 2,
                                                 half0:half0 + TH])
                    if jobs and eidx >= 2:
                        jobs.pop(0)()
                    sch.wait("tensor", "dW", w_e[eidx])
                    for tc in range(NTC):
                        slot = gstep % 2
                        vv_act = expert_front(half0, eidx, tc, slot, buf,
                                              v_cw.get(eidx), prev_act)
                        prev_act = vv_act
                        pipe.append((eidx, tc, slot, buf, vv_act))
                        if len(pipe) > 1:
                            expert_back(*pipe.pop(0))
                        gstep += 1
                        if tc == 1 and 1 <= eidx < EXL - 1:
                            sch.wait("gpsimd", "tS", sch.cnt["tS"])
                            w_e[eidx + 1] = eld((eidx + 1) % 2, eidx + 1)
                expert_back(*pipe.pop(0))
                for j in jobs:
                    j()

            def finale_bcast(half0):
                for tc in range(NTC):
                    bcast2(rows["rstd0"], rows["c0"],
                           brf.ap()[:, tc * 256:(tc + 1) * 256],
                           bcf.ap()[:, tc * 256:(tc + 1) * 256],
                           slice(half0 + tc * 256, half0 + (tc + 1) * 256))

            fin_ld = {}

            def finale_ld(half0, tc, v_yst):
                tksl = slice(half0 + tc * 256, half0 + tc * 256 + 256)
                sch.wait("sync", "vS")
                sch.wait("sync", "dO")
                ld_cols(Xf, X, tksl)
                vX = ld_cols(sf, s_d, tksl)
                if v_yst is not None:
                    sch.wait("sync", "dO", v_yst)
                    vX = ld_cols(yf, y_d, tksl)
                fin_ld[(half0, tc)] = (vX, v_yst)

            def finale_a(half0, tc):
                tok0 = tc * 256
                vX, v_yst = fin_ld[(half0, tc)]
                sch.wait("vector", "dX", vX)
                sch.emit("vector", lambda v, tok0=tok0: v.tensor_tensor(
                    gf.ap().rearrange("p (c t) -> p c t", c=HC),
                    Xf.ap().rearrange("p (c t) -> p c t", c=HC),
                    brf.ap()[:, tok0:tok0 + 256].unsqueeze(1).broadcast_to(
                        [128, HC, 256]), ALU.mult))
                sch.emit("vector", lambda v, tok0=tok0: v.tensor_tensor(
                    gf.ap().rearrange("p (c t) -> p c t", c=HC),
                    gf.ap().rearrange("p (c t) -> p c t", c=HC),
                    bcf.ap()[:, tok0:tok0 + 256].unsqueeze(1).broadcast_to(
                        [128, HC, 256]), ALU.add))
                if v_yst is not None:
                    vv = sch.emit("vector", lambda v: v.tensor_tensor(
                        gf[:], gf[:], yf[:], ALU.add), inc="vS")
                else:
                    vv = sch.emit("vector", lambda v, tok0=tok0: v.tensor_tensor(
                        gf.ap().rearrange("p (c t) -> p c t", c=HC),
                        gf.ap().rearrange("p (c t) -> p c t", c=HC),
                        ybuf.ap().rearrange("p (c t2) -> p c t2", c=HC)[
                            :, :, tok0:tok0 + 256], ALU.add), inc="vS")
                sch.wait("scalar", "vS", vv)
                vs = sch.emit("scalar", lambda s: s.activation(
                    gf[:], gf[:], ACT.Silu), inc="sS")
                sch.wait("vector", "sS", vs)
                sch.emit("vector", lambda v: v.tensor_tensor(
                    sf[:], sf[:], gf[:], ALU.mult))

            def finale_b(half0, tc):
                tok0 = tc * 256
                tksl = slice(half0 + tok0, half0 + tok0 + 256)
                vv = sch.emit("vector", lambda v: v.tensor_tensor(
                    Xf[:], Xf[:], sf[:], ALU.add), inc="vS")
                sch.wait("sync", "vS", vv)
                st_cols(Xf, O, tksl)

            sch.wait("sync", "tS")
            v_xn_cur = ld_cols(xn16s, xn16_d, slice(0, TH))
            sch.wait("gpsimd", "tS")
            w_e_cur = [None] * EXL
            w_e_cur[0] = eld(0, 0)
            w_e_cur[1] = eld(1, 1)
            sch.wait("sync", "tS")
            v_cw0_cur = ld(cwrow[0][0:1, :], cw_d[0:1, 0:TH])

            expert_loop(0, v_xn_cur, w_e_cur, v_cw0_cur)
            finale_bcast(0)
            sch.wait("sync", "vS")
            for c in range(HC):
                sch.emit("sync", lambda s, c=c: s.dma_start(
                    out=y_d[c * 128:(c + 1) * 128, 0:TH],
                    in_=ybuf.ap()[:, c * TH:(c + 1) * TH]), inc="dO", n=16)
            v_yst = sch.cnt["dO"]
            if "tap_y" in taps:
                sch.wait("sync", "dO", v_yst)
                sch.emit("sync", lambda s: s.dma_start(
                    out=taps["tap_y"][:, 0:TH], in_=y_d[:, 0:TH]),
                    inc="dO", n=16)
            sch.wait("sync", "tS")
            v_xn_cur = ld_cols(xn16s, xn16_d, slice(TH, T))
            sch.wait("gpsimd", "tS")
            w_e_cur = [None] * EXL
            w_e_cur[0] = eld(0, 0)
            w_e_cur[1] = eld(1, 1)
            sch.wait("sync", "tS")
            v_cw0_cur = ld(cwrow[0][0:1, :], cw_d[0:1, TH:T])

            jobs = [lambda: finale_ld(0, 0, v_yst),
                    lambda: finale_a(0, 0)]
            for tc in range(NTC):
                def jb(tc=tc):
                    finale_b(0, tc)
                    if tc + 1 < NTC:
                        finale_ld(0, tc + 1, v_yst)
                jobs.append(jb)
                if tc + 1 < NTC:
                    jobs.append(lambda tc=tc: finale_a(0, tc + 1))
            expert_loop(TH, v_xn_cur, w_e_cur, v_cw0_cur, jobs)
            finale_bcast(TH)
            if "tap_y" in taps:
                sch.wait("sync", "vS")
                for c in range(HC):
                    sch.emit("sync", lambda s, c=c: s.dma_start(
                        out=taps["tap_y"][c * 128:(c + 1) * 128, TH:T],
                        in_=ybuf.ap()[:, c * TH:(c + 1) * TH]), inc="dO", n=16)
            for tc in range(NTC):
                if tc == 0:
                    finale_ld(TH, 0, None)
                finale_a(TH, tc)
                finale_b(TH, tc)
                if tc + 1 < NTC:
                    finale_ld(TH, tc + 1, None)

            sch.barrier()

        block = ctx.enter_context(nc.Block())

        @block.sync
        def _(eng):
            for fn in sch.q["sync"]:
                fn(eng)

        @block.gpsimd
        def _(eng):
            for fn in sch.q["gpsimd"]:
                fn(eng)

        @block.vector
        def _(eng):
            for fn in sch.q["vector"]:
                fn(eng)

        @block.scalar
        def _(eng):
            for fn in sch.q["scalar"]:
                fn(eng)

        @block.tensor
        def _(eng):
            for fn in sch.q["tensor"]:
                fn(eng)

    return nc


def host_inputs(inputs):
    x = np.asarray(inputs["L0_attn_x"], dtype=np.float32)
    m_int, m_q1, m_q0 = make_masks()

    def t(a):
        return np.ascontiguousarray(np.asarray(a, dtype=np.float32).T)

    def cc(vec, n):
        return np.ascontiguousarray(
            np.asarray(vec, dtype=np.float32).reshape(n, 128).T)

    common = dict(
        Wqkv=np.ascontiguousarray(np.asarray(inputs["L0_attn_W_qkv"], np.float32)),
        WO=np.ascontiguousarray(np.asarray(inputs["L0_attn_W_O"], np.float32)),
        Wug=np.ascontiguousarray(np.asarray(inputs["L0_attn_W_ug"], np.float32)),
        Bw=np.ascontiguousarray(np.asarray(inputs["L0_attn_B_w"], np.float32)),
        Cw=np.ascontiguousarray(np.asarray(inputs["L0_attn_C_w"], np.float32)),
        W1=np.ascontiguousarray(np.asarray(inputs["L0_attn_mlp_W1"], np.float32)),
        W2=np.ascontiguousarray(np.asarray(inputs["L0_attn_mlp_W2"], np.float32)),
        gateT=t(inputs["L0_ffn_gate_w"]),
        WgT=np.ascontiguousarray(
            np.asarray(inputs["L0_ffn_Wg"], np.float32).transpose(0, 2, 1)),
        WuT=np.ascontiguousarray(
            np.asarray(inputs["L0_ffn_Wu"], np.float32).transpose(0, 2, 1)),
        WdT=np.ascontiguousarray(
            np.asarray(inputs["L0_ffn_Wd"], np.float32).transpose(0, 2, 1)),
        WsgT=t(inputs["L0_ffn_Ws_g"]),
        WsuT=t(inputs["L0_ffn_Ws_u"]),
        WsdT=t(inputs["L0_ffn_Ws_d"]),
        g1c=cc(inputs["L0_attn_ln1_g"], HC), b1c=cc(inputs["L0_attn_ln1_b"], HC),
        g2c=cc(inputs["L0_attn_ln2_g"], HC), b2c=cc(inputs["L0_attn_ln2_b"], HC),
        bOc=cc(inputs["L0_attn_b_O"], HC), b2mc=cc(inputs["L0_attn_mlp_b2"], HC),
        bugc=cc(inputs["L0_attn_b_ug"], HC + 1),
        b1mc=cc(inputs["L0_attn_mlp_b1"], DFF // 128),
        gbias=np.ascontiguousarray(
            np.asarray(inputs["L0_ffn_gate_bias"], np.float32).reshape(NE, 1)),
        Avec=np.ascontiguousarray(
            np.asarray(inputs["L0_attn_A"], np.float32).reshape(SD, 1)),
        id16=np.eye(128, dtype=np.float32).astype(ml_dtypes.bfloat16),
        id32=np.eye(128, dtype=np.float32),
        mint=m_int, mq1=m_q1, mq0=m_q0,
    )
    in_maps = []
    for b in range(8):
        m = dict(common)
        m["X"] = np.ascontiguousarray(x[b].T)
        in_maps.append(m)
    return in_maps


def kernel(**inputs):
    in_maps = host_inputs(inputs)
    nc = build_nc()
    res = run_bass_kernel_spmd(nc, in_maps, core_ids=list(range(8)))
    out = np.stack([np.asarray(r["O"]).T for r in res.results])
    return out.astype(np.float32)


if __name__ == "__main__":
    nc = build_nc()
    print("built", nc.n_instructions(), "instructions")
